# version 10
# speedup vs baseline: 1.2759x; 1.0933x over previous
import contextlib
import numpy as np
import ml_dtypes
import concourse.bass as bass
import concourse.mybir as mybir
from concourse.bass_utils import run_bass_kernel_spmd

F32 = mybir.dt.float32
BF16 = mybir.dt.bfloat16
U8 = mybir.dt.uint8
AF = mybir.ActivationFunctionType
ALU = mybir.AluOpType
AX = mybir.AxisListType.X
BF = ml_dtypes.bfloat16

NCORES = 8
T = 8192
D = 4096
C = T // NCORES
ISQ = 0.08838834764831843


def zz_rows(c):
    return np.concatenate([np.arange(128) + 128 * (8 * b + c)
                           for b in range(8)])


def host_prep(inp):
    x = np.ascontiguousarray(np.asarray(inp["L0_attn_x"]).reshape(T, D))
    sh = {}
    Wq = np.asarray(inp["L0_attn_Wq"])
    sh["wq_r"] = np.ascontiguousarray(
        Wq.reshape(D, 128, 8, 4).transpose(0, 2, 3, 1).reshape(D, D))
    Wk = np.asarray(inp["L0_attn_Wk"])
    sh["wk_r"] = np.ascontiguousarray(Wk.transpose(0, 2, 1).reshape(D, 1024))
    sh["wv_r"] = np.ascontiguousarray(
        np.asarray(inp["L0_attn_Wv"]).transpose(0, 2, 1).reshape(D, 1024))
    sh["bq"] = np.ascontiguousarray(np.asarray(inp["L0_attn_bq"]))
    sh["bk"] = np.ascontiguousarray(np.asarray(inp["L0_attn_bk"]))
    sh["bv"] = np.ascontiguousarray(np.asarray(inp["L0_attn_bv"]))
    sh["ident"] = np.eye(128, dtype=np.float32).astype(BF)
    sh["epsin"] = np.full((128, 1), 1e-5, np.float32)
    percore = [{"x_rows": np.ascontiguousarray(x[zz_rows(c)])}
               for c in range(NCORES)]
    return sh, percore



def host_prep_ph1(inp):
    qwgu = np.asarray(inp["L0_ffn_qweight_gu"])
    qzgu = np.asarray(inp["L0_ffn_qzeros_gu"])
    sgu = np.asarray(inp["L0_ffn_scales_gu"])
    qwdn = np.asarray(inp["L0_ffn_qweight_dn"])
    qzdn = np.asarray(inp["L0_ffn_qzeros_dn"])
    sdn = np.asarray(inp["L0_ffn_scales_dn"])
    qwgu_B = np.ascontiguousarray(qwgu).view(np.uint8).reshape(D, 11008)
    qzgu_B = np.ascontiguousarray(qzgu).view(np.uint8).reshape(32, 11008)
    qwdn_B = np.ascontiguousarray(qwdn).view(np.uint8).reshape(11008, 2048)
    qzdn_B = np.ascontiguousarray(qzdn).view(np.uint8).reshape(86, 2048)
    HSL = 1376
    BSL = HSL // 2
    pc = []
    for c in range(NCORES):
        m = {}
        m["qwgu_b"] = np.ascontiguousarray(
            np.concatenate([qwgu_B[:, BSL * c:BSL * (c + 1)],
                            qwgu_B[:, 5504 + BSL * c:5504 + BSL * (c + 1)]], axis=1))
        m["qzgu_b"] = np.ascontiguousarray(
            np.concatenate([qzgu_B[:, BSL * c:BSL * (c + 1)],
                            qzgu_B[:, 5504 + BSL * c:5504 + BSL * (c + 1)]], axis=1))
        m["sgu_c"] = np.ascontiguousarray(
            np.concatenate([sgu[:, HSL * c:HSL * (c + 1)],
                            sgu[:, 11008 + HSL * c:11008 + HSL * (c + 1)]], axis=1))
        m["qwdn_b"] = np.ascontiguousarray(qwdn_B[HSL * c:HSL * (c + 1)])
        g_lo = (HSL * c) // 128
        g_hi = (HSL * (c + 1) - 1) // 128
        ng = g_hi - g_lo + 1
        qz = np.zeros((12, 2048), np.uint8)
        qz[:ng] = qzdn_B[g_lo:g_hi + 1]
        m["qzdn_b"] = qz
        sd = np.zeros((12, 4096), np.float32)
        sd[:ng] = sdn[g_lo:g_hi + 1]
        m["sdn_c"] = sd
        Ei = np.zeros((11 * 128, 16), np.float32)
        Eid = np.zeros((12, 11 * 128), np.float32)
        for r in range(HSL):
            gl = (HSL * c + r) // 128 - g_lo
            Ei[r, gl] = 1.0
            Eid[gl, r] = 1.0
        m["Ei"] = Ei.astype(BF)
        m["Eid"] = np.ascontiguousarray(
            Eid.reshape(12, 11, 128).transpose(1, 0, 2)).astype(BF)
        pc.append(m)
    shared = {}
    Gc = np.zeros((D, 32), np.float32)
    for k in range(D):
        Gc[k, k // 128] = 1.0
    shared["Gc"] = Gc.astype(BF)
    sel = np.zeros((32, 32 * 128), np.float32)
    for kk in range(32):
        sel[kk, 128 * kk:128 * (kk + 1)] = 1.0
    shared["sel"] = sel.astype(BF)
    return shared, pc


class Ctx:
    def __init__(self, nc, es):
        self.nc = nc
        self.es = es
        self.sem = {}
        self.cnt = {}
        arena_cols = 51200
        self.arena = es.enter_context(nc.sbuf_tensor("arena", [128, arena_cols], F32))
        self.aoff = 0
        self.amax = arena_cols * 4
        self.marks = []
        self.psb = [es.enter_context(nc.psum_tensor(f"psb{i}", [128, 512], F32))
                    for i in range(8)]

    def sems(self, *names):
        for n in names:
            self.sem[n] = self.es.enter_context(self.nc.semaphore(n))
            self.cnt[n] = 0

    def alloc(self, cols, dtype):
        sz = {F32: 4, BF16: 2, U8: 1}[dtype]
        nbytes = cols * sz
        self.aoff = (self.aoff + 63) & ~63
        assert self.aoff + nbytes <= self.amax, f"SBUF arena overflow at {self.aoff + nbytes}"
        st = self.aoff // 4
        n32 = (nbytes + 3) // 4
        ap = self.arena[:, st:st + n32]
        self.aoff += n32 * 4
        if dtype != F32:
            ap = ap.bitcast(dtype)
        return ap

    def push(self):
        self.marks.append(self.aoff)

    def pop(self):
        self.aoff = self.marks.pop()



def emit_ph0(env, dram):
    ctx = env.ctx
    cnt = ctx.cnt
    psb = ctx.psb
    CN, XA, XB, W0, W1 = "CN", "XA", "XB", "W0", "W1"
    TS0, TS1, QA, QB = "TS0", "TS1", "QA", "QB"
    R0, R1, SO0, SO1, GD, TE, VE, SE = "R0", "R1", "SO0", "SO1", "GD", "TE", "VE", "SE"
    XS = [XA, XB]
    WS = [W0, W1]
    TSs = [TS0, TS1]
    QS = [QA, QB]
    RS = [R0, R1]
    SOS = [SO0, SO1]

    ctx.push()
    identt = ctx.alloc(128, BF16)
    stats = ctx.alloc(64, F32)
    bqs = ctx.alloc(32, F32)
    bqt = ctx.alloc(8, F32)
    bks = ctx.alloc(8, F32)
    bvs = ctx.alloc(8, F32)
    varsum = ctx.alloc(8, F32)
    qT = ctx.alloc(8 * 1024, BF16)
    kT = ctx.alloc(8 * 1024, BF16)
    vT = ctx.alloc(8 * 1024, BF16)
    epst = dram["epsc"]

    ctx.push()
    xlnT = ctx.alloc(32 * 1024, BF16)
    ctx.push()
    xr = [ctx.alloc(4096, F32), ctx.alloc(4096, F32)]
    lnscr = ctx.alloc(4096, BF16)

    V0 = cnt["VE"]; S0 = cnt["SE"]; E0 = cnt["TE"]
    CN0 = cnt["CN"]; XA0 = cnt["XA"]; XB0 = cnt["XB"]
    XBASE = [XA0, XB0]
    def ve_s1(tt, k):
        return V0 + 1 + 9 * tt + k

    @env.sync
    def _(sync):
        sync.dma_start(out=identt, in_=dram["ident"]).then_inc(CN, 16)
        sync.dma_start(out=bqs, in_=dram["bq"]).then_inc(CN, 16)
        sync.dma_start(out=bks, in_=dram["bk"]).then_inc(CN, 16)
        sync.dma_start(out=bvs, in_=dram["bv"]).then_inc(CN, 16)
        for tt in range(8):
            if tt >= 2:
                sync.wait(VE, val=ve_s1(tt - 2, 5))
                sync.wait(SE, val=S0 + 2 * (tt - 2) + 1)
            sync.dma_start(out=xr[tt % 2],
                           in_=dram["x_rows"][128 * tt:128 * (tt + 1)]
                           ).then_inc(XS[tt % 2], 16)
    assert cnt["CN"] == (CN0 + 16 * 4), ("cnt CN", cnt["CN"], CN0 + 16 * 4)
    assert cnt["XA"] == (XA0 + 16 * 4), ("cnt XA", cnt["XA"], XA0 + 16 * 4)
    assert cnt["XB"] == (XB0 + 16 * 4), ("cnt XB", cnt["XB"], XB0 + 16 * 4)

    @env.vector
    def _(vector):
        vector.wait(CN, val=CN0 + 64)
        vector.memzero(varsum)
        vector.tensor_reduce(out=bqt.rearrange("p (a b) -> p a b", b=1),
                             in_=bqs.rearrange("p (a b) -> p a b", b=4),
                             axis=AX, op=ALU.add).then_inc(VE, 1)
        for tt in range(8):
            mus = stats[:, 8 * tt + 0:8 * tt + 1]
            nmu = stats[:, 8 * tt + 1:8 * tt + 2]
            rstd = stats[:, 8 * tt + 2:8 * tt + 3]
            sqv = stats[:, 8 * tt + 3:8 * tt + 4]
            nmur = stats[:, 8 * tt + 4:8 * tt + 5]
            vector.wait(XS[tt % 2], val=XBASE[tt % 2] + 16 * (tt // 2 + 1))
            vector.reduce_sum(out=mus, in_=xr[tt % 2], axis=AX).then_inc(VE, 1)
            vector.wait(VE, val=ve_s1(tt, 1))
            vector.tensor_scalar_mul(nmu, mus, -1.0 / 4096.0).then_inc(VE, 1)
            vector.wait(SE, val=S0 + 2 * tt + 2)
            vector.reciprocal(rstd, sqv).then_inc(VE, 1)
            vector.wait(VE, val=ve_s1(tt, 3))
            vector.tensor_mul(nmur, nmu, rstd).then_inc(VE, 1)
            vector.wait(VE, val=ve_s1(tt, 4))
            vector.tensor_scalar(out=lnscr, in0=xr[tt % 2], scalar1=rstd,
                                 scalar2=nmur, op0=ALU.mult, op1=ALU.add
                                 ).then_inc(VE, 1)
            for b in range(4):
                vector.wait(TE, val=E0 + 4 * tt + b + 1)
                vector.tensor_copy(
                    out=xlnT.rearrange("p (kk t) -> p kk t", t=1024)
                        [:, 8 * b:8 * b + 8, 128 * tt:128 * (tt + 1)],
                    in_=psb[b][:, :].bitcast(BF16).rearrange("p (kk t) -> p kk t", t=128),
                ).then_inc(VE, 1)
    assert cnt["VE"] == (V0 + 1 + 72), ("cnt VE", cnt["VE"], V0 + 1 + 72)

    @env.scalar
    def _(scalar):
        for tt in range(8):
            nmu = stats[:, 8 * tt + 1:8 * tt + 2]
            sqv = stats[:, 8 * tt + 3:8 * tt + 4]
            scalar.wait(VE, val=ve_s1(tt, 2))
            scalar.activation(out=lnscr, in_=xr[tt % 2], func=AF.Square, bias=nmu,
                              scale=1.0, accum_out=varsum[:, tt:tt + 1]
                              ).then_inc(SE, 1)
            scalar.wait(SE, val=S0 + 2 * tt + 1)
            scalar.activation(out=sqv, in_=varsum[:, tt:tt + 1], func=AF.Sqrt,
                              scale=1.0 / 4096.0, bias=epst).then_inc(SE, 1)
    assert cnt["SE"] == (S0 + 16), ("cnt SE", cnt["SE"], S0 + 16)

    @env.tensor
    def _(tensor):
        for tt in range(8):
            tensor.wait(VE, val=ve_s1(tt, 5))
            for b in range(4):
                if 4 * tt + b >= 4:
                    tpp, bp = divmod(4 * tt + b - 4, 4)
                    tensor.wait(VE, val=ve_s1(tpp, 6 + bp))
                pbf = psb[b][:, :].bitcast(BF16)
                for u in range(8):
                    kk = 8 * b + u
                    ins = tensor.transpose(pbf[:, 128 * u:128 * (u + 1)],
                                           lnscr[:, 128 * kk:128 * (kk + 1)], identt)
                ins.then_inc(TE, 1)
    assert cnt["TE"] == (E0 + 32), ("cnt TE", cnt["TE"], E0 + 32)

    ctx.pop()

    ctx.push()
    wbuf = [ctx.alloc(32 * 512, BF16), ctx.alloc(32 * 512, BF16)]
    E1 = cnt["TE"]; V1 = cnt["VE"]
    WB = [cnt["W0"], cnt["W1"]]

    slabs = ([("q", h) for h in range(8)] + [("k", h) for h in range(8)] +
             [("v", h) for h in range(8)])

    @env.gpsimd
    def _(gpsimd):
        gpsimd.wait(XA, val=cnt["XA"])
        gpsimd.wait(XB, val=cnt["XB"])
        gpsimd.wait(TE, val=E1)
        gpsimd.wait(VE, val=V1)
        gpsimd.wait(SE, val=cnt["SE"])
        for i, (kind, h) in enumerate(slabs):
            if i >= 2:
                gpsimd.wait(TE, val=E1 + 2 * (i - 1))
            w = {"q": dram["wq_r"], "k": dram["wk_r"], "v": dram["wv_r"]}[kind]
            ncols = 512 if kind == "q" else 128
            dst = wbuf[i % 2].rearrange("p (kk n) -> p kk n", n=512)[:, :, 0:ncols]
            gpsimd.dma_start(
                out=dst,
                in_=w[:, ncols * h:ncols * (h + 1)]
                    .rearrange("(kk p) n -> p kk n", p=128)).then_inc(WS[i % 2], 16)
    assert cnt["W0"] == (WB[0] + 16 * 12), ("cnt W0", cnt["W0"], WB[0] + 16 * 12)
    assert cnt["W1"] == (WB[1] + 16 * 12), ("cnt W1", cnt["W1"], WB[1] + 16 * 12)

    @env.tensor
    def _(tensor):
        tensor.wait(VE, val=V1)
        for i, (kind, h) in enumerate(slabs):
            tensor.wait(WS[i % 2], val=WB[i % 2] + 16 * (i // 2 + 1))
            for tb in range(2):
                u = 2 * i + tb
                if u >= 2:
                    tensor.wait(VE, val=V1 + u - 1)
                njs = 4 if kind == "q" else 1
                n = 0
                for j in range(njs):
                    for kk in range(32):
                        n += 1
                        ins = tensor.matmul(
                            psb[u % 2][:, :],
                            wbuf[i % 2].rearrange("p (kk n) -> p kk n", n=512)
                                [:, kk, 128 * j:128 * (j + 1)],
                            xlnT[:, 1024 * kk + 512 * tb:1024 * kk + 512 * (tb + 1)],
                            start=(n == 1), stop=(n == njs * 32))
                ins.then_inc(TE, 1)
    assert cnt["TE"] == (E1 + 48), ("cnt TE", cnt["TE"], E1 + 48)

    @env.vector
    def _(vector):
        for i, (kind, h) in enumerate(slabs):
            dstf = {"q": qT, "k": kT, "v": vT}[kind]
            biasf = {"q": bqt, "k": bks, "v": bvs}[kind]
            for tb in range(2):
                u = 2 * i + tb
                vector.wait(TE, val=E1 + u + 1)
                vector.tensor_scalar(
                    out=dstf[:, 1024 * h + 512 * tb:1024 * h + 512 * (tb + 1)],
                    in0=psb[u % 2][:, :], scalar1=biasf[:, h:h + 1],
                    scalar2=None, op0=ALU.add).then_inc(VE, 1)
    assert cnt["VE"] == (V1 + 48), ("cnt VE", cnt["VE"], V1 + 48)

    ctx.pop()
    ctx.pop()

    ctx.push()
    tokstage = [ctx.alloc(1024, BF16), ctx.alloc(1024, BF16)]
    E2 = cnt["TE"]; V2 = cnt["VE"]
    TB = [cnt["TS0"], cnt["TS1"]]

    @env.tensor
    def _(tensor):
        tensor.wait(VE, val=V2)
        for idx in range(16):
            src = qT if idx < 8 else kT
            tt = idx % 8
            bank = 4 + idx % 2
            if idx >= 2:
                tensor.wait(VE, val=V2 + idx - 1)
            pbf = psb[bank][:, :].bitcast(BF16)
            for pt in range(8):
                ins = tensor.transpose(
                    pbf[:, 128 * pt:128 * (pt + 1)],
                    src[:, 1024 * pt + 128 * tt:1024 * pt + 128 * (tt + 1)], identt)
            ins.then_inc(TE, 1)
    assert cnt["TE"] == (E2 + 16), ("cnt TE", cnt["TE"], E2 + 16)

    @env.vector
    def _(vector):
        for idx in range(16):
            vector.wait(TE, val=E2 + idx + 1)
            if idx >= 2:
                vector.wait(TSs[idx % 2], val=TB[idx % 2] + 16 * (idx // 2))
            vector.tensor_copy(out=tokstage[idx % 2],
                               in_=psb[4 + idx % 2][:, :].bitcast(BF16)).then_inc(VE, 1)
    assert cnt["VE"] == (V2 + 16), ("cnt VE", cnt["VE"], V2 + 16)

    @env.gpsimd
    def _(gpsimd):
        for idx in range(16):
            dstd = dram["qtok_d"] if idx < 8 else dram["ktok_d"]
            tt = idx % 8
            gpsimd.wait(VE, val=V2 + idx + 1)
            gpsimd.dma_start(out=dstd[128 * tt:128 * (tt + 1)],
                             in_=tokstage[idx % 2]).then_inc(TSs[idx % 2], 16)
    assert cnt["TS0"] == (TB[0] + 16 * 8), ("cnt TS0", cnt["TS0"], TB[0] + 16 * 8)
    assert cnt["TS1"] == (TB[1] + 16 * 8), ("cnt TS1", cnt["TS1"], TB[1] + 16 * 8)
    ctx.pop()

    ctx.push()
    bd3 = ctx.alloc(32 * 1024, BF16)
    attn = ctx.alloc(16 * 1024, BF16)
    attnT = ctx.alloc(16 * 1024, BF16)
    sums = ctx.alloc(128, F32)
    recip = ctx.alloc(128, F32)
    nmx = ctx.alloc(64, F32)

    E3 = cnt["TE"]; V3 = cnt["VE"]; S3c = cnt["SE"]; G3 = cnt["GD"]
    QB0 = [cnt["QA"], cnt["QB"]]
    TS_done = [cnt["TS0"], cnt["TS1"]]
    TE_CH = 32 + 16 + 32
    VE_CH = 146
    SE_CH = 32

    def te4(ch, k):
        return E3 + TE_CH * ch + k

    def ve4(ch, k):
        return V3 + VE_CH * ch + k

    @env.sync
    def _(sync):
        sync.wait(TS0, val=TS_done[0])
        sync.wait(TS1, val=TS_done[1])
        for ch in range(8):
            if ch >= 1:
                sync.wait(TE, val=te4(ch - 1, 32))
            for dstt, srcd in ((bd3[0:8, 0:16384], dram["qtok_d"]),
                               (bd3[0:8, 16384:32768], dram["ktok_d"])):
                sync.dma_start(
                    out=dstt.rearrange("p (t d) -> p t d", d=128),
                    in_=srcd[128 * ch:128 * (ch + 1)]
                        .rearrange("t (h d) -> h t d", d=128)
                    ).then_inc(QS[ch % 2], 16)
    assert cnt["QA"] == (QB0[0] + 16 * 8), ("cnt QA", cnt["QA"], QB0[0] + 16 * 8)
    assert cnt["QB"] == (QB0[1] + 16 * 8), ("cnt QB", cnt["QB"], QB0[1] + 16 * 8)

    @env.tensor
    def _(tensor):
        for ch in range(8):
            tensor.wait(QS[ch % 2], val=QB0[ch % 2] + 32 * (ch // 2 + 1))
            for g in range(32):
                nglob = 32 * ch + g
                if nglob >= 2:
                    tensor.wait(SE, val=S3c + nglob - 1)
                for tau in range(4):
                    t = 4 * g + tau
                    ins = tensor.matmul(
                        psb[g % 2][:, 128 * tau:128 * (tau + 1)],
                        bd3[0:8, 128 * t:128 * (t + 1)],
                        bd3[0:8, 16384 + 128 * t:16384 + 128 * (t + 1)],
                        start=True, stop=True)
                ins.then_inc(TE, 1)
            tensor.wait(VE, val=ve4(ch, 98))
            for bt in range(16):
                nglob = 16 * ch + bt
                if nglob >= 2:
                    chp, btp = divmod(nglob - 2, 16)
                    tensor.wait(VE, val=ve4(chp, 98 + btp + 1))
                pbf = psb[2 + bt % 2][:, :].bitcast(BF16)
                for u in range(8):
                    t = 8 * bt + u
                    ins = tensor.transpose(pbf[:, 128 * u:128 * (u + 1)],
                                           attn[:, 128 * t:128 * (t + 1)], identt)
                ins.then_inc(TE, 1)
            for g in range(32):
                nglob = 32 * ch + g
                if nglob >= 2:
                    chp, gp = divmod(nglob - 2, 32)
                    tensor.wait(VE, val=ve4(chp, 114 + gp + 1))
                if g == 0 and ch > 0:
                    tensor.wait(GD, val=G3 + 16 * ch)
                bt_need = (4 * g + 3) // 8 + 1
                tensor.wait(VE, val=ve4(ch, 98 + bt_need))
                for tau in range(4):
                    t = 4 * g + tau
                    glob_t = 128 * ch + t
                    v_t = vT.rearrange("p (h t) -> p h t", t=1024)[:, :, glob_t]
                    ins = tensor.matmul(
                        psb[4 + g % 2][0:8, 128 * tau:128 * (tau + 1)],
                        v_t, attnT[:, 128 * t:128 * (t + 1)],
                        start=True, stop=True)
                ins.then_inc(TE, 1)
    assert cnt["TE"] == (E3 + 8 * TE_CH), ("cnt TE", cnt["TE"], E3 + 8 * TE_CH)

    @env.vector
    def _(vector):
        for ch in range(8):
            for g in range(32):
                vector.wait(TE, val=te4(ch, g + 1))
                vector.tensor_reduce(out=nmx[:, 2 * g:2 * g + 1], in_=psb[g % 2][:, :],
                                     axis=AX, op=ALU.max, negate=True).then_inc(VE, 1)
                vector.wait(VE, val=ve4(ch, 3 * g + 1))
                vector.tensor_scalar_mul(nmx[:, 2 * g + 1:2 * g + 2],
                                         nmx[:, 2 * g:2 * g + 1], ISQ).then_inc(VE, 1)
                vector.wait(SE, val=S3c + 32 * ch + g + 1)
                vector.tensor_reduce(
                    out=sums.rearrange("p (g t o) -> p g t o", t=4, o=1)[:, g],
                    in_=attn.rearrange("p (g t e) -> p g t e", t=4, e=128)[:, g],
                    axis=AX, op=ALU.add).then_inc(VE, 1)
            vector.wait(VE, val=ve4(ch, 96))
            vector.reciprocal(recip, sums).then_inc(VE, 1)
            vector.wait(VE, val=ve4(ch, 97))
            vector.tensor_tensor(
                out=attn.rearrange("p (t e) -> p t e", e=128),
                in0=attn.rearrange("p (t e) -> p t e", e=128),
                in1=recip.rearrange("p (t o) -> p t o", o=1)
                    .broadcast_to([128, 128, 128]),
                op=ALU.mult).then_inc(VE, 1)
            for bt in range(16):
                vector.wait(TE, val=te4(ch, 32 + bt + 1))
                vector.tensor_copy(out=attnT[:, 1024 * bt:1024 * (bt + 1)],
                                   in_=psb[2 + bt % 2][:, :].bitcast(BF16)).then_inc(VE, 1)
            for g in range(32):
                if g == 0 and ch > 0:
                    vector.wait(GD, val=G3 + 16 * ch)
                vector.wait(TE, val=te4(ch, 48 + g + 1))
                vector.tensor_copy(out=bd3[64:72, 512 * g:512 * (g + 1)],
                                   in_=psb[4 + g % 2][0:8, :]).then_inc(VE, 1)
    assert cnt["VE"] == (V3 + 8 * VE_CH), ("cnt VE", cnt["VE"], V3 + 8 * VE_CH)

    @env.scalar
    def _(scalar):
        for ch in range(8):
            for g in range(32):
                scalar.wait(VE, val=ve4(ch, 3 * g + 2))
                scalar.activation(out=attn[:, 512 * g:512 * (g + 1)],
                                  in_=psb[g % 2][:, :], func=AF.Exp,
                                  scale=ISQ, bias=nmx[:, 2 * g + 1:2 * g + 2]
                                  ).then_inc(SE, 1)
    assert cnt["SE"] == (S3c + 8 * SE_CH), ("cnt SE", cnt["SE"], S3c + 8 * SE_CH)

    @env.gpsimd
    def _(gpsimd):
        for ch in range(8):
            gpsimd.wait(VE, val=ve4(ch, VE_CH))
            gpsimd.dma_start(
                out=dram["otok_d"][128 * ch:128 * (ch + 1)]
                    .rearrange("t (h d) -> h t d", d=128),
                in_=bd3[64:72, 0:16384].rearrange("p (t d) -> p t d", d=128)).then_inc(GD, 16)
    assert cnt["GD"] == (G3 + 16 * 8), ("cnt GD", cnt["GD"], G3 + 16 * 8)
    ctx.pop()

    ctx.push()
    xr2 = [ctx.alloc(4096, F32), ctx.alloc(4096, F32)]
    ot2 = [ctx.alloc(1024, BF16), ctx.alloc(1024, BF16)]
    xout = [ctx.alloc(4096, F32), ctx.alloc(4096, F32)]
    V5 = cnt["VE"]; G5 = cnt["GD"]
    RB = [cnt["R0"], cnt["R1"]]
    SOB = [cnt["SO0"], cnt["SO1"]]

    @env.sync
    def _(sync):
        sync.wait(GD, val=G5)
        for tt in range(8):
            if tt >= 2:
                sync.wait(VE, val=V5 + tt - 1)
            sync.dma_start(out=xr2[tt % 2],
                           in_=dram["x_rows"][128 * tt:128 * (tt + 1)]
                           ).then_inc(RS[tt % 2], 16)
            sync.dma_start(out=ot2[tt % 2],
                           in_=dram["otok_d"][128 * tt:128 * (tt + 1)]
                           ).then_inc(RS[tt % 2], 16)
    assert cnt["R0"] == (RB[0] + 16 * 8), ("cnt R0", cnt["R0"], RB[0] + 16 * 8)
    assert cnt["R1"] == (RB[1] + 16 * 8), ("cnt R1", cnt["R1"], RB[1] + 16 * 8)

    @env.vector
    def _(vector):
        for tt in range(8):
            vector.wait(RS[tt % 2], val=RB[tt % 2] + 32 * (tt // 2 + 1))
            if tt >= 2:
                vector.wait(SOS[tt % 2], val=SOB[tt % 2] + 16 * (tt // 2))
            vector.tensor_tensor(
                out=xout[tt % 2].rearrange("p (d h j) -> p d h j", h=8, j=4),
                in0=xr2[tt % 2].rearrange("p (d h j) -> p d h j", h=8, j=4),
                in1=ot2[tt % 2].rearrange("p (h d) -> p h d", d=128)
                    .transpose([0, 2, 1]).unsqueeze(3).broadcast_to([128, 128, 8, 4]),
                op=ALU.add).then_inc(VE, 1)
    assert cnt["VE"] == (V5 + 8), ("cnt VE", cnt["VE"], V5 + 8)

    @env.gpsimd
    def _(gpsimd):
        for tt in range(8):
            gpsimd.wait(VE, val=V5 + tt + 1)
            gpsimd.dma_start(out=dram["xres1"][128 * tt:128 * (tt + 1)],
                             in_=xout[tt % 2]).then_inc(SOS[tt % 2], 16)
    assert cnt["SO0"] == (SOB[0] + 16 * 4), ("cnt SO0", cnt["SO0"], SOB[0] + 16 * 4)
    assert cnt["SO1"] == (SOB[1] + 16 * 4), ("cnt SO1", cnt["SO1"], SOB[1] + 16 * 4)
    ctx.pop()
    ctx.pop()


def declare_ph0(nc, dram, standalone=True):
    dram["x_rows"] = nc.dram_tensor("x_rows", [C, D], F32, kind="ExternalInput").ap()
    if "xres1" in dram:
        _skip_xres1 = True
    dram["wq_r"] = nc.dram_tensor("wq_r", [D, D], F32, kind="ExternalInput").ap()
    dram["wk_r"] = nc.dram_tensor("wk_r", [D, 1024], F32, kind="ExternalInput").ap()
    dram["wv_r"] = nc.dram_tensor("wv_r", [D, 1024], F32, kind="ExternalInput").ap()
    dram["bq"] = nc.dram_tensor("bq", [128, 32], F32, kind="ExternalInput").ap()
    dram["bk"] = nc.dram_tensor("bk", [128, 8], F32, kind="ExternalInput").ap()
    dram["bv"] = nc.dram_tensor("bv", [128, 8], F32, kind="ExternalInput").ap()
    dram["ident"] = nc.dram_tensor("ident", [128, 128], BF16, kind="ExternalInput").ap()
    dram["epsin"] = nc.dram_tensor("epsin", [128, 1], F32, kind="ExternalInput").ap()
    dram["qtok_d"] = nc.dram_tensor("qtok_d", [C, 1024], BF16, kind="Internal").ap()
    dram["ktok_d"] = nc.dram_tensor("ktok_d", [C, 1024], BF16, kind="Internal").ap()
    dram["otok_d"] = nc.dram_tensor("otok_d", [C, 1024], BF16, kind="Internal").ap()
    if "xres1" not in dram:
        dram["xres1"] = nc.dram_tensor(
            "xres1", [C, D], F32,
            kind="ExternalOutput" if standalone else "Internal").ap()


def build_ph0():
    nc = bass.Bass("TRN2", target_bir_lowering=False, debug=False)
    dram = {}
    declare_ph0(nc, dram, standalone=True)
    es = contextlib.ExitStack()
    with es:
        ctx = Ctx(nc, es)
        ctx.sems(*PH1_SEMS)
        epst = ctx.alloc(1, F32)
        dram["epsc"] = epst
        with nc.Block() as block:
            @block.sync
            def _(sync):
                sync.dma_start(out=epst, in_=dram["epsin"]).then_inc(ctx.sem["CN"], 16)
            ctx.cnt["CN"] += 16
            run_twopass(ctx, block, lambda env: emit_ph0(env, dram))

            @block.gpsimd
            def _(gpsimd):
                gpsimd.wait_ge(ctx.sem["SO0"], ctx.cnt["SO0"])
                gpsimd.wait_ge(ctx.sem["SO1"], ctx.cnt["SO1"])
    return nc


class _MockInstr:
    def __init__(self, env):
        self.env = env

    def then_inc(self, semname, k=1):
        self.env.ctx.cnt[semname] += k
        return self


class _MockEngine:
    def __init__(self, env):
        self._env = env

    def wait(self, semname, key=None, val=None):
        pass

    def __getattr__(self, name):
        env = self._env
        def f(*a, **k):
            return _MockInstr(env)
        return f


class _RealInstr:
    def __init__(self, env, instr):
        self.env = env
        self.instr = instr

    def then_inc(self, semname, k=1):
        self.instr.then_inc(self.env.ctx.sem[semname], k)
        self.env.ctx.cnt[semname] += k
        return self


class _RealEngine:
    def __init__(self, env, eng):
        self._env = env
        self._eng = eng

    def wait(self, semname, key=None, val=None):
        if val is None:
            val = self._env.rec[key][semname]
        self._eng.wait_ge(self._env.ctx.sem[semname], val)

    def __getattr__(self, name):
        env = self._env
        m = getattr(self._eng, name)
        def f(*a, **k):
            return _RealInstr(env, m(*a, **k))
        return f


class Env:

    def __init__(self, ctx, block):
        self.ctx = ctx
        self.block = block
        self.rec = {}
        self.pass1 = True

    def mark(self, key):
        if self.pass1:
            assert key not in self.rec, f"dup mark {key}"
            self.rec[key] = dict(self.ctx.cnt)
        else:
            for s, v in self.rec[key].items():
                assert self.ctx.cnt[s] == v, \
                    f"pass2 divergence at {key}: {s} {self.ctx.cnt[s]} != {v}"

    def at(self, key, semname):
        return self.rec[key][semname]

    def _section(self, engname):
        env = self

        def deco(f):
            if env.pass1:
                f(_MockEngine(env))
            else:
                getattr(env.block, engname)(lambda eng: f(_RealEngine(env, eng)))
        return deco

    @property
    def sync(self):
        return self._section("sync")

    @property
    def vector(self):
        return self._section("vector")

    @property
    def scalar(self):
        return self._section("scalar")

    @property
    def tensor(self):
        return self._section("tensor")

    @property
    def gpsimd(self):
        return self._section("gpsimd")


def run_twopass(ctx, block, emit):
    env = Env(ctx, block)
    saved = dict(ctx.cnt)
    amark = ctx.aoff
    mstack = list(ctx.marks)
    emit(env)
    ctx.cnt.clear()
    ctx.cnt.update(saved)
    ctx.aoff = amark
    ctx.marks = mstack
    env.pass1 = False
    emit(env)
    return env



NSUB = 4
HSUB = 344
TBW = 512
NTB = 16
MT = [(0, 128), (128, 128), (256, 88)]


def emit_ph1(env, dram, fence=None):
    ctx = env.ctx
    psb = ctx.psb
    epst = dram["epsc"]

    ctx.push()
    identt = ctx.alloc(128, BF16)
    stats = ctx.alloc(64, F32)
    varsum = ctx.alloc(8, F32)
    xr = [ctx.alloc(4096, F32), ctx.alloc(4096, F32)]
    lnscr = ctx.alloc(4096, BF16)
    agst = [ctx.alloc(4096, BF16), ctx.alloc(4096, BF16)]

    @env.sync
    def _(sync):
        if fence:
            for s_, v_ in fence.items():
                sync.wait(s_, val=v_)
        sync.dma_start(out=identt, in_=dram["ident"]).then_inc("CN", 16)
        env.mark("p1.ident")
        for tt in range(8):
            if tt >= 2:
                sync.wait("VE", ("p1.norm", tt - 2))
                sync.wait("SE", ("p1.sq1", tt - 2))
            sync.dma_start(out=xr[tt % 2],
                           in_=dram["xres1"][128 * tt:128 * (tt + 1)]
                           ).then_inc("X" + "AB"[tt % 2], 16)
            env.mark(("p1.xload", tt))

    @env.vector
    def _(vector):
        vector.wait("CN", "p1.ident")
        vector.memzero(varsum)
        for tt in range(8):
            mus = stats[:, 8 * tt + 0:8 * tt + 1]
            nmu = stats[:, 8 * tt + 1:8 * tt + 2]
            rstd = stats[:, 8 * tt + 2:8 * tt + 3]
            sqv = stats[:, 8 * tt + 3:8 * tt + 4]
            nmur = stats[:, 8 * tt + 4:8 * tt + 5]
            vector.wait("X" + "AB"[tt % 2], ("p1.xload", tt))
            vector.reduce_sum(out=mus, in_=xr[tt % 2], axis=AX).then_inc("VE")
            env.mark(("p1.mus", tt))
            vector.wait("VE", ("p1.mus", tt))
            vector.tensor_scalar_mul(nmu, mus, -1.0 / 4096.0).then_inc("VE")
            env.mark(("p1.nmu", tt))
            vector.wait("SE", ("p1.sq2", tt))
            vector.reciprocal(rstd, sqv).then_inc("VE")
            env.mark(("p1.rstd", tt))
            vector.wait("VE", ("p1.rstd", tt))
            vector.tensor_mul(nmur, nmu, rstd).then_inc("VE")
            env.mark(("p1.nmur", tt))
            vector.wait("VE", ("p1.nmur", tt))
            vector.tensor_scalar(out=lnscr, in0=xr[tt % 2], scalar1=rstd,
                                 scalar2=nmur, op0=ALU.mult, op1=ALU.add
                                 ).then_inc("VE")
            env.mark(("p1.norm", tt))
            for b in range(4):
                vector.wait("TE", ("p1.tp", tt, b))
                if tt >= 2 and b == 0:
                    vector.wait("TS" + "01"[tt % 2], ("p1.agstore", tt - 2))
                vector.tensor_copy(
                    out=agst[tt % 2].rearrange("p (kk t) -> p kk t", t=128)
                        [:, 8 * b:8 * b + 8, :],
                    in_=psb[b][:, :].bitcast(BF16)
                        .rearrange("p (kk t) -> p kk t", t=128),
                ).then_inc("VE")
                env.mark(("p1.evac", tt, b))

    @env.scalar
    def _(scalar):
        for tt in range(8):
            nmu = stats[:, 8 * tt + 1:8 * tt + 2]
            sqv = stats[:, 8 * tt + 3:8 * tt + 4]
            scalar.wait("VE", ("p1.nmu", tt))
            scalar.activation(out=lnscr, in_=xr[tt % 2], func=AF.Square, bias=nmu,
                              scale=1.0, accum_out=varsum[:, tt:tt + 1]
                              ).then_inc("SE")
            env.mark(("p1.sq1", tt))
            scalar.wait("SE", ("p1.sq1", tt))
            scalar.activation(out=sqv, in_=varsum[:, tt:tt + 1], func=AF.Sqrt,
                              scale=1.0 / 4096.0, bias=epst).then_inc("SE")
            env.mark(("p1.sq2", tt))

    @env.tensor
    def _(tensor):
        for tt in range(8):
            tensor.wait("VE", ("p1.norm", tt))
            for b in range(4):
                if 4 * tt + b >= 4:
                    tpp, bp = divmod(4 * tt + b - 4, 4)
                    tensor.wait("VE", ("p1.evac", tpp, bp))
                pbf = psb[b][:, :].bitcast(BF16)
                for u in range(8):
                    kk = 8 * b + u
                    ins = tensor.transpose(pbf[:, 128 * u:128 * (u + 1)],
                                           lnscr[:, 128 * kk:128 * (kk + 1)],
                                           identt)
                ins.then_inc("TE")
                env.mark(("p1.tp", tt, b))

    @env.gpsimd
    def _(gpsimd):
        for tt in range(8):
            gpsimd.wait("VE", ("p1.evac", tt, 3))
            gpsimd.dma_start(
                out=dram["agin%d" % (tt // 2)]
                    .rearrange("(kk p) t -> p kk t", p=128)
                    [:, :, 128 * (tt % 2):128 * (tt % 2 + 1)],
                in_=agst[tt % 2].rearrange("p (kk t) -> p kk t", t=128),
            ).then_inc("TS" + "01"[tt % 2], 16)
            env.mark(("p1.agstore", tt))
            if tt % 2 == 1:
                gpsimd.wait("TS0", ("p1.agstore", tt - 1))
                gpsimd.wait("TS1", ("p1.agstore", tt))
                gpsimd.collective_compute(
                    "AllGather", ALU.bypass,
                    replica_groups=[list(range(NCORES))],
                    ins=[dram["agin%d" % (tt // 2)].opt()],
                    outs=[dram["xg%d" % (tt // 2)].opt()],
                ).then_inc("CC")
                env.mark(("p1.ag", tt // 2))
    ctx.pop()

    ctx.push()
    qwgu_sb = ctx.alloc(32 * 1376, U8)
    qzgu_sb = ctx.alloc(1376, U8)
    sgu_bf = ctx.alloc(2752, BF16)
    zscr = ctx.alloc(2752, U8)
    zscrb = ctx.alloc(2752, BF16)
    sel_sb = ctx.alloc(32 * 128, BF16)
    Wdeq = ctx.alloc(32 * 688, BF16)
    nib = ctx.alloc(688, U8)
    nibbf = ctx.alloc(688, BF16)
    centbf = ctx.alloc(688, BF16)
    xgT = [ctx.alloc(32 * TBW, BF16), ctx.alloc(32 * TBW, BF16)]
    hst = [ctx.alloc(3 * TBW, BF16), ctx.alloc(3 * TBW, BF16)]
    sil = [ctx.alloc(TBW, F32), ctx.alloc(TBW, F32)]

    def qwb(kk, lo, n):
        return qwgu_sb.rearrange("p (kk b) -> p kk b", b=1376)[:, kk, lo:lo + n]

    def xg_tile(u, kk):
        return xgT[u % 2].rearrange("p (kk t) -> p kk t", t=TBW)[:, kk, :]

    @env.sync
    def _(sync):
        sync.wait("TS0", ("p1.agstore", 6))
        sync.wait("TS1", ("p1.agstore", 7))
        sync.dma_start(out=qwgu_sb.rearrange("p (kk b) -> p kk b", b=1376),
                       in_=dram["qwgu_b"].rearrange("(kk p) b -> p kk b", p=128)
                       ).then_inc("CN", 16)
        sync.dma_start(out=qzgu_sb[0:32], in_=dram["qzgu_b"]).then_inc("CN", 16)
        sync.dma_start(out=sel_sb[0:32], in_=dram["sel"]).then_inc("CN", 16)
        env.mark("p1b.res")
        for s in range(NSUB):
            for tb in range(NTB):
                u = NTB * s + tb
                if u >= 2:
                    sync.wait("TE", ("p1b.mt", u - 2, 5))
                r, loc = tb % 8, tb // 8
                if s == 0 and tb % 8 == 0:
                    sync.wait("CC", ("p1.ag", 2 * loc + 1))
                for half in range(2):
                    for jj in range(2):
                        sync.dma_start(
                            out=xgT[u % 2].rearrange("p (kk t) -> p kk t", t=TBW)
                                [:, 16 * half:16 * (half + 1),
                                 256 * jj:256 * (jj + 1)],
                            in_=dram["xg%d" % (2 * loc + jj)]
                                [4096 * r + 2048 * half:
                                 4096 * r + 2048 * (half + 1), :]
                                .rearrange("(kk p) t -> p kk t", p=128),
                        ).then_inc("Q" + "AB"[u % 2], 16)
                env.mark(("p1b.xg", u))

    @env.gpsimd
    def _(gpsimd):
        gpsimd.dma_start(out=sgu_bf[0:32], in_=dram["sgu_c"]).then_inc("CN2", 16)
        env.mark("p1b.sgu")
        for s in range(NSUB):
            for tb in range(NTB):
                u = NTB * s + tb
                tpos = 2 * (tb % 8) + tb // 8
                gpsimd.wait("VE", ("p1b.h", u, 2))
                gpsimd.dma_start(
                    out=dram["h_d"][HSUB * s:HSUB * s + 256,
                                    TBW * tpos:TBW * (tpos + 1)]
                        .rearrange("(kk p) t -> p kk t", p=128),
                    in_=hst[u % 2][:, 0:2 * TBW]
                        .rearrange("p (kk t) -> p kk t", t=TBW),
                ).then_inc("SO" + "01"[u % 2], 16)
                gpsimd.dma_start(
                    out=dram["h_d"][HSUB * s + 256:HSUB * (s + 1),
                                    TBW * tpos:TBW * (tpos + 1)],
                    in_=hst[u % 2][0:88, 2 * TBW:3 * TBW],
                ).then_inc("SO" + "01"[u % 2], 16)
                env.mark(("p1b.hstore", u))

    @env.tensor
    def _(tensor):
        tensor.wait("CN", "p1b.res")
        tensor.wait("CN2", "p1b.sgu")
        tensor.wait("VE", "p1b.zs")
        for s in range(NSUB):
            for kk in range(32):
                if kk >= 1:
                    tensor.wait("VE", ("p1b.deq", s, kk - 1))
                if s >= 1 and kk == 0:
                    tensor.wait("VE", ("p1b.h", NTB * s - 1, 1))
                    tensor.wait("VE", ("p1b.h", NTB * s - 1, 2))
                tensor.matmul(psb[4][:, 0:344],
                              sel_sb[0:32, 128 * kk:128 * (kk + 1)],
                              zscrb[0:32, HSUB * s:HSUB * (s + 1)],
                              start=True, stop=True)
                tensor.matmul(psb[5][:, 0:344],
                              sel_sb[0:32, 128 * kk:128 * (kk + 1)],
                              zscrb[0:32, 1376 + HSUB * s:1376 + HSUB * (s + 1)],
                              start=True, stop=True)
                tensor.matmul(psb[6][:, 0:344],
                              sel_sb[0:32, 128 * kk:128 * (kk + 1)],
                              sgu_bf[0:32, HSUB * s:HSUB * (s + 1)],
                              start=True, stop=True)
                ins = tensor.matmul(psb[7][:, 0:344],
                                    sel_sb[0:32, 128 * kk:128 * (kk + 1)],
                                    sgu_bf[0:32,
                                           1376 + HSUB * s:1376 + HSUB * (s + 1)],
                                    start=True, stop=True)
                ins.then_inc("TE")
                env.mark(("p1b.bcast", s, kk))
            for tb in range(NTB):
                u = NTB * s + tb
                tensor.wait("Q" + "AB"[u % 2], ("p1b.xg", u))
                tensor.wait("VE", ("p1b.deq", s, 31))
                for mt in range(6):
                    gate = mt < 3
                    j = mt % 3
                    off, sz = MT[j]
                    bank = mt
                    if u >= 1:
                        if gate:
                            tensor.wait("SE", ("p1b.sil", u - 1, j))
                        else:
                            tensor.wait("VE", ("p1b.h", u - 1, j))
                    wcol = off if gate else 344 + off
                    for kk in range(32):
                        ins = tensor.matmul(
                            psb[bank][0:sz, 0:TBW],
                            Wdeq.rearrange("p (kk n) -> p kk n", n=688)
                                [:, kk, wcol:wcol + sz],
                            xg_tile(u, kk), start=(kk == 0), stop=(kk == 31))
                    ins.then_inc("TE")
                    env.mark(("p1b.mt", u, mt))
                env.mark(("p1b.zcmm", u))

    @env.vector
    def _(vector):
        vector.wait("CN", "p1b.res")
        vector.wait("CN2", "p1b.sgu")
        vector.tensor_scalar(out=zscr[0:32, 0:2752:2], in0=qzgu_sb[0:32],
                             scalar1=15, scalar2=None, op0=ALU.bitwise_and)
        vector.tensor_scalar(out=zscr[0:32, 1:2752:2], in0=qzgu_sb[0:32],
                             scalar1=4, scalar2=None,
                             op0=ALU.logical_shift_right).then_inc("VE")
        env.mark("p1b.zunpack")
        vector.wait("VE", "p1b.zunpack")
        vector.tensor_copy(out=zscrb[0:32], in_=zscr[0:32]).then_inc("VE")
        env.mark("p1b.zs")
        for s in range(NSUB):
            for kk in range(32):
                if s > 0 and kk == 0:
                    vector.wait("TE", ("p1b.zcmm", NTB * s - 1))
                vector.tensor_scalar(out=nib[:, 0:344:2],
                                     in0=qwb(kk, 172 * s, 172), scalar1=15,
                                     scalar2=None, op0=ALU.bitwise_and)
                vector.tensor_scalar(out=nib[:, 1:344:2],
                                     in0=qwb(kk, 172 * s, 172), scalar1=4,
                                     scalar2=None, op0=ALU.logical_shift_right)
                vector.tensor_scalar(out=nib[:, 344:688:2],
                                     in0=qwb(kk, 688 + 172 * s, 172), scalar1=15,
                                     scalar2=None, op0=ALU.bitwise_and)
                vector.tensor_scalar(out=nib[:, 345:688:2],
                                     in0=qwb(kk, 688 + 172 * s, 172), scalar1=4,
                                     scalar2=None, op0=ALU.logical_shift_right
                                     ).then_inc("VE")
                env.mark(("p1b.unp", s, kk))
                vector.wait("VE", ("p1b.unp", s, kk))
                vector.tensor_copy(out=nibbf, in_=nib).then_inc("VE")
                env.mark(("p1b.cast", s, kk))
                vector.wait("VE", ("p1b.cast", s, kk))
                vector.wait("TE", ("p1b.bcast", s, kk))
                vector.tensor_sub(centbf[:, 0:344], nibbf[:, 0:344],
                                  psb[4][:, 0:344])
                vector.tensor_sub(centbf[:, 344:688], nibbf[:, 344:688],
                                  psb[5][:, 0:344]).then_inc("VE")
                env.mark(("p1b.cent", s, kk))
                vector.wait("VE", ("p1b.cent", s, kk))
                vector.tensor_mul(
                    Wdeq.rearrange("p (kk n) -> p kk n", n=688)[:, kk, 0:344],
                    centbf[:, 0:344], psb[6][:, 0:344])
                vector.tensor_mul(
                    Wdeq.rearrange("p (kk n) -> p kk n", n=688)[:, kk, 344:688],
                    centbf[:, 344:688], psb[7][:, 0:344]).then_inc("VE")
                env.mark(("p1b.deq", s, kk))
            for tb in range(NTB):
                u = NTB * s + tb
                for j in range(3):
                    vector.wait("SE", ("p1b.sil", u, j))
                    vector.wait("TE", ("p1b.mt", u, 3 + j))
                    if u >= 2 and j == 0:
                        vector.wait("SO" + "01"[u % 2], ("p1b.hstore", u - 2))
                    vector.tensor_mul(
                        hst[u % 2][:, TBW * j:TBW * (j + 1)],
                        sil[j % 2][:, 0:TBW],
                        psb[3 + j][:, 0:TBW]).then_inc("VE")
                    env.mark(("p1b.h", u, j))

    @env.scalar
    def _(scalar):
        for s in range(NSUB):
            for tb in range(NTB):
                u = NTB * s + tb
                for j in range(3):
                    scalar.wait("TE", ("p1b.mt", u, j))
                    pu, pj = (u, 0) if j == 2 else (u - 1, 2 if j == 0 else 1)
                    if pu >= 0:
                        scalar.wait("VE", ("p1b.h", pu, pj))
                    scalar.activation(out=sil[j % 2][:, 0:TBW],
                                      in_=psb[j][:, 0:TBW], func=AF.Silu
                                      ).then_inc("SE")
                    env.mark(("p1b.sil", u, j))
    ctx.pop()


def emit_ph1_down(env, dram):
    ctx = env.ctx
    psb = ctx.psb

    ctx.push()
    qwdn_sb = ctx.alloc(11 * 2048, U8)
    qzdn_sb = ctx.alloc(2048, U8)
    sdn_bf = ctx.alloc(4096, BF16)
    zdnscr = ctx.alloc(4096, U8)
    zdnscrb = ctx.alloc(4096, BF16)
    eid_sb = ctx.alloc(11 * 128, BF16)
    Wdnq = ctx.alloc(11 * 2048, BF16)
    dnib = ctx.alloc(2048, U8)
    dnibbf = ctx.alloc(2048, BF16)
    dcent = ctx.alloc(2048, BF16)
    h_ts = [ctx.alloc(11 * 128, BF16), ctx.alloc(11 * 128, BF16)]
    dstage = [ctx.alloc(2048, F32), ctx.alloc(2048, F32)]

    def wdn_bytes(kk, hh):
        return qwdn_sb.rearrange("p (kk b) -> p kk b", b=2048)[:, kk,
                                                              1024 * hh:1024 * (hh + 1)]

    @env.sync
    def _(sync):
        sync.wait("SO0", ("p1b.hstore", 62))
        sync.wait("SO1", ("p1b.hstore", 63))
        sync.wait("TE", ("p1b.zcmm", 63))
        sync.dma_start(out=qwdn_sb.rearrange("p (kk b) -> p kk b", b=2048)[:, 0:10, :],
                       in_=dram["qwdn_b"][0:1280]
                           .rearrange("(kk p) b -> p kk b", p=128)).then_inc("CN", 16)
        sync.dma_start(out=qwdn_sb[0:96].rearrange("p (kk b) -> p kk b", b=2048)
                       [:, 10:11, :],
                       in_=dram["qwdn_b"][1280:1376].unsqueeze(1)
                       ).then_inc("CN", 16)
        sync.dma_start(out=qzdn_sb[0:12], in_=dram["qzdn_b"]).then_inc("CN", 16)
        sync.dma_start(out=eid_sb[0:12].rearrange("p (kk t) -> p kk t", t=128),
                       in_=dram["Eid"].rearrange("kk g t -> g kk t")
                       ).then_inc("CN", 16)
        env.mark("p1c.res")
        for hh in range(2):
            for ts in range(64):
                w = 64 * hh + ts
                if w >= 2:
                    sync.wait("TE", ("p1c.dn", w - 2, 3))
                sync.dma_start(
                    out=h_ts[w % 2].rearrange("p (kk t) -> p kk t", t=128)[:, 0:10, :],
                    in_=dram["h_d"][0:1280, 128 * ts:128 * (ts + 1)]
                        .rearrange("(kk p) t -> p kk t", p=128)
                ).then_inc("R" + "01"[w % 2], 16)
                sync.dma_start(
                    out=h_ts[w % 2][0:96, 10 * 128:11 * 128],
                    in_=dram["h_d"][1280:1376, 128 * ts:128 * (ts + 1)]
                ).then_inc("R" + "01"[w % 2], 16)
                env.mark(("p1c.h", w))

    @env.gpsimd
    def _(gpsimd):
        gpsimd.dma_start(out=sdn_bf[0:12], in_=dram["sdn_c"]).then_inc("CN2", 16)
        env.mark("p1c.sdn")
        for hh in range(2):
            for ts in range(64):
                w = 64 * hh + ts
                gpsimd.wait("SE", ("p1c.ev", w, 3))
                gpsimd.dma_start(
                    out=dram["rsin%d" % hh][128 * ts:128 * (ts + 1), :],
                    in_=dstage[w % 2]).then_inc("TS" + "01"[w % 2], 16)
                env.mark(("p1c.st", w))
            gpsimd.wait("TS0", ("p1c.st", 64 * hh + 62))
            gpsimd.wait("TS1", ("p1c.st", 64 * hh + 63))
            gpsimd.collective_compute(
                "ReduceScatter", ALU.add, replica_groups=[list(range(NCORES))],
                ins=[dram["rsin%d" % hh].opt()],
                outs=[dram["rsout%d" % hh].opt()],
            ).then_inc("CC")
            env.mark(("p1c.rs", hh))

    @env.vector
    def _(vector):
        vector.wait("CN", "p1c.res")
        vector.wait("CN2", "p1c.sdn")
        vector.tensor_scalar(out=zdnscr[0:12, 0:4096:2], in0=qzdn_sb[0:12],
                             scalar1=15, scalar2=None, op0=ALU.bitwise_and)
        vector.tensor_scalar(out=zdnscr[0:12, 1:4096:2], in0=qzdn_sb[0:12],
                             scalar1=4, scalar2=None,
                             op0=ALU.logical_shift_right).then_inc("VE")
        env.mark("p1c.zunp")
        vector.wait("VE", "p1c.zunp")
        vector.tensor_copy(out=zdnscrb[0:12], in_=zdnscr[0:12]).then_inc("VE")
        env.mark("p1c.zs")
        for hh in range(2):
            for kk in range(11):
                np_ = 128 if kk < 10 else 96
                if hh > 0 and kk == 0:
                    vector.wait("TE", ("p1c.dn", 63, 3))
                vector.tensor_scalar(out=dnib[0:np_, 0:2048:2],
                                     in0=wdn_bytes(kk, hh)[0:np_], scalar1=15,
                                     scalar2=None, op0=ALU.bitwise_and)
                vector.tensor_scalar(out=dnib[0:np_, 1:2048:2],
                                     in0=wdn_bytes(kk, hh)[0:np_], scalar1=4,
                                     scalar2=None, op0=ALU.logical_shift_right
                                     ).then_inc("VE")
                env.mark(("p1c.unp", hh, kk))
                vector.wait("VE", ("p1c.unp", hh, kk))
                vector.tensor_copy(out=dnibbf[0:np_], in_=dnib[0:np_]).then_inc("VE")
                env.mark(("p1c.cast", hh, kk))
                vector.wait("VE", ("p1c.cast", hh, kk))
                for m in range(4):
                    q = 4 * kk + m
                    vector.wait("TE", ("p1c.bc", hh, kk, m))
                    vector.tensor_sub(
                        dcent[0:np_, 512 * m:512 * (m + 1)],
                        dnibbf[0:np_, 512 * m:512 * (m + 1)],
                        psb[4 + q % 2][0:np_, :]).then_inc("VE")
                    env.mark(("p1c.cent", hh, kk, m))
                    vector.wait("VE", ("p1c.cent", hh, kk, m))
                    vector.tensor_mul(
                        Wdnq.rearrange("p (kk n) -> p kk n", n=2048)
                            [0:np_, kk, 512 * m:512 * (m + 1)],
                        dcent[0:np_, 512 * m:512 * (m + 1)],
                        psb[6 + q % 2][0:np_, :]
                    ).then_inc("VE")
                    env.mark(("p1c.deq", hh, kk, m))


    @env.tensor
    def _(tensor):
        tensor.wait("CN", "p1c.res")
        tensor.wait("VE", "p1c.zs")
        for hh in range(2):
            for kk in range(11):
                np_ = 128 if kk < 10 else 96
                for m in range(4):
                    q = 4 * kk + m
                    if q >= 2:
                        tensor.wait("VE", ("p1c.deq", hh, (q - 2) // 4, (q - 2) % 4))
                    if hh > 0 and kk == 0 and m < 2:
                        tensor.wait("VE", ("p1c.deq", 0, 10, 2 + m))
                    tensor.matmul(
                        psb[4 + q % 2][0:np_, :],
                        eid_sb[0:12, 128 * kk:128 * kk + np_],
                        zdnscrb[0:12, 2048 * hh + 512 * m:2048 * hh + 512 * (m + 1)],
                        start=True, stop=True)
                    ins = tensor.matmul(
                        psb[6 + q % 2][0:np_, :],
                        eid_sb[0:12, 128 * kk:128 * kk + np_],
                        sdn_bf[0:12, 2048 * hh + 512 * m:2048 * hh + 512 * (m + 1)],
                        start=True, stop=True)
                    ins.then_inc("TE")
                    env.mark(("p1c.bc", hh, kk, m))
            tensor.wait("VE", ("p1c.deq", hh, 10, 3))
            for ts in range(64):
                w = 64 * hh + ts
                tensor.wait("R" + "01"[w % 2], ("p1c.h", w))
                for nt in range(4):
                    if w >= 1:
                        tensor.wait("SE", ("p1c.ev", w - 1, nt))
                    for kk in range(11):
                        np_ = 128 if kk < 10 else 96
                        ins = tensor.matmul(
                            psb[nt][:, :],
                            h_ts[w % 2].rearrange("p (kk t) -> p kk t", t=128)
                                [0:np_, kk, :],
                            Wdnq.rearrange("p (kk n) -> p kk n", n=2048)
                                [0:np_, kk, 512 * nt:512 * (nt + 1)],
                            start=(kk == 0), stop=(kk == 10))
                    ins.then_inc("TE")
                    env.mark(("p1c.dn", w, nt))

    @env.scalar
    def _(scalar):
        for hh in range(2):
            for ts in range(64):
                w = 64 * hh + ts
                for nt in range(4):
                    scalar.wait("TE", ("p1c.dn", w, nt))
                    if w >= 2 and nt == 0:
                        scalar.wait("TS" + "01"[w % 2], ("p1c.st", w - 2))
                    scalar.copy(out=dstage[w % 2][:, 512 * nt:512 * (nt + 1)],
                                in_=psb[nt][:, :]).then_inc("SE")
                    env.mark(("p1c.ev", w, nt))
    ctx.pop()

    ctx.push()
    xr2 = [ctx.alloc(4096, F32), ctx.alloc(4096, F32)]
    rs2 = [ctx.alloc(4096, F32), ctx.alloc(4096, F32)]
    xout = [ctx.alloc(4096, F32), ctx.alloc(4096, F32)]

    @env.sync
    def _(sync):
        sync.wait("CC", ("p1c.rs", 1))
        for tt in range(8):
            if tt >= 2:
                sync.wait("VE", ("p1d.add", tt - 2))
            sync.dma_start(out=xr2[tt % 2],
                           in_=dram["xres1"][128 * tt:128 * (tt + 1)]
                           ).then_inc("R" + "01"[tt % 2], 16)
            for hh in range(2):
                sync.dma_start(out=rs2[tt % 2][:, 2048 * hh:2048 * (hh + 1)],
                               in_=dram["rsout%d" % hh]
                                   [128 * tt:128 * (tt + 1)]
                               ).then_inc("R" + "01"[tt % 2], 16)
            env.mark(("p1d.load", tt))

    @env.vector
    def _(vector):
        for tt in range(8):
            vector.wait("R" + "01"[tt % 2], ("p1d.load", tt))
            if tt >= 2:
                vector.wait("SO" + "01"[tt % 2], ("p1d.store", tt - 2))
            vector.tensor_add(xout[tt % 2], xr2[tt % 2], rs2[tt % 2]).then_inc("VE")
            env.mark(("p1d.add", tt))

    @env.gpsimd
    def _(gpsimd):
        for tt in range(8):
            gpsimd.wait("VE", ("p1d.add", tt))
            gpsimd.dma_start(out=dram["xres2"][128 * tt:128 * (tt + 1)],
                             in_=xout[tt % 2]).then_inc("SO" + "01"[tt % 2], 16)
            env.mark(("p1d.store", tt))
    ctx.pop()


def declare_ph1(nc, dram, standalone=True):
    if "xres1" not in dram:
        dram["xres1"] = nc.dram_tensor(
            "xres1", [C, D], F32,
            kind="ExternalInput" if standalone else "Internal").ap()
    if "xres2" in dram:
        pass
    for nm, shp, dt in [
        ("qwgu_b", [D, 1376], U8), ("qzgu_b", [32, 1376], U8),
        ("sgu_c", [32, 2752], F32), ("qwdn_b", [1376, 2048], U8),
        ("qzdn_b", [12, 2048], U8), ("sdn_c", [12, 4096], F32),
        ("Ei", [11 * 128, 16], BF16), ("Eid", [11, 12, 128], BF16),
        ("Gc", [D, 32], BF16), ("sel", [32, 32 * 128], BF16),
    ]:
        dram[nm] = nc.dram_tensor(nm, shp, dt, kind="ExternalInput").ap()
    if "ident" not in dram:
        dram["ident"] = nc.dram_tensor("ident", [128, 128], BF16,
                                       kind="ExternalInput").ap()
    if "epsin" not in dram:
        dram["epsin"] = nc.dram_tensor("epsin", [128, 1], F32,
                                       kind="ExternalInput").ap()
    for j in range(4):
        dram["agin%d" % j] = nc.dram_tensor("agin%d" % j, [D, C // 4], BF16,
                                            kind="Internal").ap()
        dram["xg%d" % j] = nc.dram_tensor("xg%d" % j, [NCORES * D, C // 4],
                                          BF16, kind="Internal").ap()
    dram["xs_d"] = nc.dram_tensor("xs_d", [32, T], BF16, kind="Internal").ap()
    dram["h_d"] = nc.dram_tensor("h_d", [1376, T], BF16, kind="Internal").ap()
    for hh in range(2):
        dram["rsin%d" % hh] = nc.dram_tensor("rsin%d" % hh, [T, D // 2], F32,
                                             kind="Internal").ap()
        dram["rsout%d" % hh] = nc.dram_tensor("rsout%d" % hh, [C, D // 2], F32,
                                              kind="Internal").ap()
    if "xres2" not in dram:
        dram["xres2"] = nc.dram_tensor(
            "xres2", [C, D], F32,
            kind="ExternalOutput" if standalone else "Internal").ap()


PH1_SEMS = ("CN", "CN2", "XA", "XB", "W0", "W1", "TS0", "TS1", "QA", "QB",
            "R0", "R1", "SO0", "SO1", "GD", "TE", "VE", "SE", "XSA", "XSB",
            "CC", "LD", "ST")


def build_ph1():
    nc = bass.Bass("TRN2", target_bir_lowering=False, debug=False)
    dram = {}
    declare_ph1(nc, dram, standalone=True)
    es = contextlib.ExitStack()
    with es:
        ctx = Ctx(nc, es)
        ctx.sems(*PH1_SEMS)
        epst = ctx.alloc(1, F32)
        dram["epsc"] = epst
        with nc.Block() as block:
            @block.sync
            def _(sync):
                sync.dma_start(out=epst, in_=dram["epsin"]).then_inc(ctx.sem["CN"], 16)
            ctx.cnt["CN"] += 16

            def emit(env):
                emit_ph1(env, dram)
                emit_ph1_down(env, dram)
            run_twopass(ctx, block, emit)

            @block.gpsimd
            def _(gpsimd):
                gpsimd.wait_ge(ctx.sem["SO0"], ctx.cnt["SO0"])
                gpsimd.wait_ge(ctx.sem["SO1"], ctx.cnt["SO1"])
    return nc



def emit_ph2(env, dram, fence=None):
    ctx = env.ctx
    psb = ctx.psb
    epst = dram["epsc"]

    ctx.push()
    identt = ctx.alloc(128, BF16)
    identf = ctx.alloc(128, F32)
    stats = ctx.alloc(64, F32)
    varsum = ctx.alloc(8, F32)
    ctx.push()
    xlnT = ctx.alloc(32 * 1024, BF16)
    ctx.push()
    xr = [ctx.alloc(4096, F32), ctx.alloc(4096, F32)]
    lnscr = ctx.alloc(4096, BF16)

    @env.sync
    def _(sync):
        if fence:
            for s_, v_ in fence.items():
                sync.wait(s_, val=v_)
        sync.dma_start(out=identt, in_=dram["ident"]).then_inc("CN", 16)
        sync.dma_start(out=identf, in_=dram["identf"]).then_inc("CN", 16)
        env.mark("p2.ident")
        for tt in range(8):
            if tt >= 2:
                sync.wait("VE", ("p2.norm", tt - 2))
                sync.wait("SE", ("p2.sq1", tt - 2))
            sync.dma_start(out=xr[tt % 2],
                           in_=dram["xres2"][128 * tt:128 * (tt + 1)]
                           ).then_inc("X" + "AB"[tt % 2], 16)
            env.mark(("p2.xload", tt))

    @env.vector
    def _(vector):
        vector.wait("CN", "p2.ident")
        vector.memzero(varsum)
        for tt in range(8):
            mus = stats[:, 8 * tt + 0:8 * tt + 1]
            nmu = stats[:, 8 * tt + 1:8 * tt + 2]
            rstd = stats[:, 8 * tt + 2:8 * tt + 3]
            sqv = stats[:, 8 * tt + 3:8 * tt + 4]
            nmur = stats[:, 8 * tt + 4:8 * tt + 5]
            vector.wait("X" + "AB"[tt % 2], ("p2.xload", tt))
            vector.reduce_sum(out=mus, in_=xr[tt % 2], axis=AX).then_inc("VE")
            env.mark(("p2.mus", tt))
            vector.wait("VE", ("p2.mus", tt))
            vector.tensor_scalar_mul(nmu, mus, -1.0 / 4096.0).then_inc("VE")
            env.mark(("p2.nmu", tt))
            vector.wait("SE", ("p2.sq2", tt))
            vector.reciprocal(rstd, sqv).then_inc("VE")
            env.mark(("p2.rstd", tt))
            vector.wait("VE", ("p2.rstd", tt))
            vector.tensor_mul(nmur, nmu, rstd).then_inc("VE")
            env.mark(("p2.nmur", tt))
            vector.wait("VE", ("p2.nmur", tt))
            vector.tensor_scalar(out=lnscr, in0=xr[tt % 2], scalar1=rstd,
                                 scalar2=nmur, op0=ALU.mult, op1=ALU.add
                                 ).then_inc("VE")
            env.mark(("p2.norm", tt))
            for b in range(4):
                vector.wait("TE", ("p2.tp", tt, b))
                vector.tensor_copy(
                    out=xlnT.rearrange("p (kk t) -> p kk t", t=1024)
                        [:, 8 * b:8 * b + 8, 128 * tt:128 * (tt + 1)],
                    in_=psb[b][:, :].bitcast(BF16)
                        .rearrange("p (kk t) -> p kk t", t=128),
                ).then_inc("VE")
                env.mark(("p2.evac", tt, b))

    @env.scalar
    def _(scalar):
        for tt in range(8):
            nmu = stats[:, 8 * tt + 1:8 * tt + 2]
            sqv = stats[:, 8 * tt + 3:8 * tt + 4]
            scalar.wait("VE", ("p2.nmu", tt))
            scalar.activation(out=lnscr, in_=xr[tt % 2], func=AF.Square, bias=nmu,
                              scale=1.0, accum_out=varsum[:, tt:tt + 1]
                              ).then_inc("SE")
            env.mark(("p2.sq1", tt))
            scalar.wait("SE", ("p2.sq1", tt))
            scalar.activation(out=sqv, in_=varsum[:, tt:tt + 1], func=AF.Sqrt,
                              scale=1.0 / 4096.0, bias=epst).then_inc("SE")
            env.mark(("p2.sq2", tt))

    @env.tensor
    def _(tensor):
        for tt in range(8):
            tensor.wait("VE", ("p2.norm", tt))
            for b in range(4):
                if 4 * tt + b >= 4:
                    tpp, bp = divmod(4 * tt + b - 4, 4)
                    tensor.wait("VE", ("p2.evac", tpp, bp))
                pbf = psb[b][:, :].bitcast(BF16)
                for u in range(8):
                    kk = 8 * b + u
                    ins = tensor.transpose(pbf[:, 128 * u:128 * (u + 1)],
                                           lnscr[:, 128 * kk:128 * (kk + 1)],
                                           identt)
                ins.then_inc("TE")
                env.mark(("p2.tp", tt, b))
    @env.gpsimd
    def _(gpsimd):
        if "dbg1" in dram:
            gpsimd.wait("VE", ("p2.evac", 7, 3))
            gpsimd.dma_start(out=dram["dbg1"],
                             in_=xlnT.rearrange("p (kk t) -> p kk t", t=1024)
                             ).then_inc("GD", 16)
            env.mark("p2.dbg1")

    ctx.pop()

    ctx.push()
    wbuf = [ctx.alloc(32 * 512, BF16), ctx.alloc(32 * 512, BF16)]
    ptile = ctx.alloc(512, BF16)
    rptile = ctx.alloc(512, BF16)
    rs1 = ctx.alloc(512, BF16)
    rs2 = ctx.alloc(512, BF16)
    cosP = ctx.alloc(8 * 64, BF16)
    sinP = ctx.alloc(8 * 64, BF16)
    qstg = [ctx.alloc(512, BF16), ctx.alloc(512, BF16)]
    kstg = [ctx.alloc(512, BF16), ctx.alloc(512, BF16)]

    SG = ([("q", g) for g in range(8)] + [("k", g) for g in range(2)] +
          [("v", g) for g in range(2)])

    def h4(tile):
        return tile.rearrange("p (h d) -> p h d", d=128)

    @env.sync
    def _(sync):
        sync.wait("VE", ("p2.evac", 7, 3))
        sync.wait("SE", ("p2.sq2", 7))
        sync.wait("TE", ("p2.tp", 7, 3))
        sync.dma_start(out=cosP, in_=dram["cosP_c"]).then_inc("CN", 16)
        sync.dma_start(out=sinP, in_=dram["sinP_c"]).then_inc("CN", 16)
        env.mark("p2b.cs")

    @env.gpsimd
    def _(gpsimd):
        def load_slab(i):
            kind, g = SG[i]
            if i >= 2:
                gpsimd.wait("TE", ("p2b.mm", i - 2, 7))
            w = {"q": dram["l1wq"], "k": dram["l1wk"], "v": dram["l1wv"]}[kind]
            gpsimd.dma_start(
                out=wbuf[i % 2].rearrange("p (kk n) -> p kk n", n=512),
                in_=w[:, 512 * g:512 * (g + 1)]
                    .rearrange("(kk p) n -> p kk n", p=128)).then_inc(
                        "W" + "01"[i % 2], 16)
            env.mark(("p2b.wl", i))
        gpsimd.wait("VE", ("p2.evac", 7, 3))
        gpsimd.wait("SE", ("p2.sq2", 7))
        gpsimd.wait("TE", ("p2.tp", 7, 3))
        load_slab(0)
        load_slab(1)
        for i, (kind, g) in enumerate(SG):
            for tm in range(8):
                u = 8 * i + tm
                if kind == "q":
                    gpsimd.wait("VE", ("p2b.qevac", i, tm))
                    gpsimd.dma_start(
                        out=dram["qT_d"]
                            .rearrange("(h d) t -> d h t", d=128)
                            [:, 4 * g:4 * (g + 1), 128 * tm:128 * (tm + 1)],
                        in_=qstg[u % 2].rearrange("p (h t) -> p h t", t=128),
                    ).then_inc("SO" + "01"[u % 2], 16)
                    env.mark(("p2b.qst", i, tm))
                else:
                    gpsimd.wait("VE", ("p2b.kevac", i, tm))
                    col = 512 * g if kind == "k" else 1024 + 512 * g
                    gpsimd.dma_start(
                        out=dram["agin2_%d" % tm][0:128, col:col + 512],
                        in_=kstg[u % 2],
                    ).then_inc("SO" + "01"[u % 2], 16)
                    env.mark(("p2b.kst", i, tm))
            if i + 2 < len(SG):
                load_slab(i + 2)
        gpsimd.wait("SO0", ("p2b.kst", 11, 6))
        gpsimd.wait("SO1", ("p2b.kst", 11, 7))
        if "agdump_0" in dram:
            for b in range(8):
                gpsimd.dma_start(out=dram["agdump_%d" % b],
                                 in_=dram["agin2_%d" % b]).then_inc("GD", 16)
                env.mark(("p2b.agd", b))
            gpsimd.wait("GD", ("p2b.agd", 7))
        for b in range(8):
            gpsimd.collective_compute(
                "AllGather", ALU.bypass, replica_groups=[list(range(NCORES))],
                ins=[dram["agin2_%d" % b].opt()],
                outs=[dram["kvb_%d" % b].opt()],
            ).then_inc("CC")
            env.mark(("p2b.ag", b))

    @env.tensor
    def _(tensor):
        tensor.wait("VE", ("p2.evac", 7, 3))
        tensor.wait("CN", "p2b.cs")
        for i, (kind, g) in enumerate(SG):
            tensor.wait("W" + "01"[i % 2], ("p2b.wl", i))
            for tm in range(8):
                u = 8 * i + tm
                bank = u % 2
                if u >= 2:
                    iu, tu = divmod(u - 2, 8)
                    tensor.wait("VE", ("p2b.pevac", iu, tu))
                for kk in range(32):
                    ins = tensor.matmul(
                        psb[bank][:, :],
                        xlnT[:, 1024 * kk + 128 * tm:1024 * kk + 128 * (tm + 1)],
                        wbuf[i % 2].rearrange("p (kk n) -> p kk n", n=512)[:, kk, :],
                        start=(kk == 0), stop=(kk == 31))
                ins.then_inc("TE")
                env.mark(("p2b.mm", i, tm))
                if kind == "q":
                    tensor.wait("VE", ("p2b.rope", i, tm))
                    if u >= 2:
                        iu, tu = divmod(u - 2, 8)
                        tensor.wait("VE", ("p2b.qevac", iu, tu))
                    pbf = psb[4 + u % 2][:, :].bitcast(BF16)
                    for j in range(4):
                        ins = tensor.transpose(
                            pbf[:, 128 * j:128 * (j + 1)],
                            h4(rptile)[:, j, :], identt)
                    ins.then_inc("TE")
                    env.mark(("p2b.qtp", i, tm))

    @env.vector
    def _(vector):
        vector.wait("CN", "p2b.cs")
        for i, (kind, g) in enumerate(SG):
            for tm in range(8):
                u = 8 * i + tm
                bank = u % 2
                vector.wait("TE", ("p2b.mm", i, tm))
                if u >= 1:
                    vector.wait("VE", ("p2b.ropedone", *divmod(u - 1, 8)))
                vector.tensor_copy(out=ptile, in_=psb[bank][:, :]).then_inc("VE")
                env.mark(("p2b.pevac", i, tm))
                ct = cosP.rearrange("p (tm d) -> p tm d", d=64)[:, tm, :]
                st = sinP.rearrange("p (tm d) -> p tm d", d=64)[:, tm, :]
                ctb = ct.unsqueeze(1).broadcast_to([128, 4, 64])
                stb = st.unsqueeze(1).broadcast_to([128, 4, 64])
                x1 = h4(ptile)[:, :, 0:64]
                x2 = h4(ptile)[:, :, 64:128]
                if kind != "v":
                    vector.wait("VE", ("p2b.pevac", i, tm))
                    if kind == "q" and u >= 1:
                        vector.wait("TE", ("p2b.qtp", *divmod(u - 1, 8)))
                    vector.tensor_mul(h4(rs1)[:, :, 0:64], x1, ctb)
                    vector.tensor_mul(h4(rs1)[:, :, 64:128], x2, stb)
                    vector.tensor_mul(h4(rs2)[:, :, 0:64], x1, stb)
                    vector.tensor_mul(h4(rs2)[:, :, 64:128], x2, ctb).then_inc("VE")
                    env.mark(("p2b.r1", i, tm))
                    vector.wait("VE", ("p2b.r1", i, tm))
                    vector.tensor_sub(h4(rptile)[:, :, 0:64],
                                      h4(rs1)[:, :, 0:64], h4(rs1)[:, :, 64:128])
                    vector.tensor_add(h4(rptile)[:, :, 64:128],
                                      h4(rs2)[:, :, 0:64], h4(rs2)[:, :, 64:128]
                                      ).then_inc("VE")
                    env.mark(("p2b.rope", i, tm))
                if kind == "q":
                    vector.wait("TE", ("p2b.qtp", i, tm))
                    if u >= 2:
                        vector.wait("SO" + "01"[u % 2], ("p2b.qst", *divmod(u - 2, 8)))
                    vector.tensor_copy(out=qstg[u % 2],
                                       in_=psb[4 + u % 2][:, 0:256].bitcast(BF16)
                                       ).then_inc("VE")
                    env.mark(("p2b.qevac", i, tm))
                else:
                    srct = rptile if kind == "k" else ptile
                    vector.wait("VE", ("p2b.rope", i, tm) if kind == "k"
                                else ("p2b.pevac", i, tm))
                    iu, tu = divmod(u - 2, 8)
                    if u >= 2 and SG[iu][0] != "q":
                        vector.wait("SO" + "01"[u % 2], ("p2b.kst", iu, tu))
                    vector.tensor_copy(out=kstg[u % 2], in_=srct).then_inc("VE")
                    env.mark(("p2b.kevac", i, tm))
                env.mark(("p2b.ropedone", i, tm))

    ctx.pop()
    ctx.pop()

    ctx.push()
    mask01 = ctx.alloc(8192, BF16)
    KT = [ctx.alloc(8192, BF16), ctx.alloc(8192, BF16)]
    VH = [ctx.alloc(64 * 128, BF16), ctx.alloc(64 * 128, BF16)]
    P = [ctx.alloc(8192, BF16), ctx.alloc(8192, BF16)]
    PTb = ctx.alloc(4 * 1024, BF16)
    qtile = [ctx.alloc(128, BF16), ctx.alloc(128, BF16)]
    ctxf = [ctx.alloc(128, F32), ctx.alloc(128, F32)]
    ctxb = [ctx.alloc(128, BF16), ctx.alloc(128, BF16)]
    sums = ctx.alloc(4, F32)
    recf = ctx.alloc(4, F32)
    recr = ctx.alloc(256, F32)
    onesf = ctx.alloc(128, F32)

    def Pn(w):
        return P[w % 2]

    ST = []
    sc_list = []
    pt_list = []
    sl_map = {}
    for w in range(256):
        v, qh = divmod(w, 4)
        qt, hp = divmod(v, 8)
        nck, nbt = 2 * (qt + 1), qt + 1
        ST.append(dict(v=v, qh=qh, qt=qt, hp=hp, nck=nck, nbt=nbt))
        for ck in range(nck):
            sc_list.append((v, qh, ck, len(sc_list) % 4))
        for bt in range(nbt):
            g = len(pt_list)
            owner = "SE" if bt % 4 == 3 else "VE"
            pt_list.append((v, qh, bt, 4 + g % 2, g % 4, owner))
            sl_map[(v, qh, bt)] = (4 + g % 2, g % 4, owner)
    sc_of, pt_of = {}, {}
    si = pi = 0
    for w in range(256):
        s = ST[w]
        sc_of[w] = si
        pt_of[w] = pi
        si += s["nck"]
        pi += s["nbt"]

    @env.sync
    def _(sync):
        sync.wait("SO0", ("p2b.qst", 7, 6))
        sync.wait("SO1", ("p2b.qst", 7, 7))
        sync.wait("TE", ("p2b.mm", 11, 7))
        env.mark("p2c.const")
        for qt in range(8):
            L = 1024 * (qt + 1)
            if qt >= 1:
                sync.wait("VE", ("p2c.msk", 8 * qt - 1, 3))
            sync.dma_start(out=mask01[:, 0:L],
                           in_=dram["mask_c"][qt][:, 0:L]).then_inc("CN", 16)
            env.mark(("p2c.mask", qt))
            sync.wait("CC", ("p2b.ag", qt))
            for hp in range(8):
                v = 8 * qt + hp
                if v >= 2:
                    sync.wait("TE", ("p2c.pv", v - 2, 3))
                for bp in range(qt + 1):
                    sync.dma_start(
                        out=KT[v % 2][:, 1024 * bp:1024 * (bp + 1)]
                            .rearrange("p (a b) -> p a b", b=1024),
                        in_=dram["kvb_%d" % bp][:, 128 * hp:128 * (hp + 1)],
                        transpose=True).then_inc("Q" + "AB"[v % 2], 16)
                    sync.dma_start(
                        out=VH[v % 2].rearrange("p (kt d) -> p kt d", d=128)
                            [:, 8 * bp:8 * (bp + 1), :],
                        in_=dram["kvb_%d" % bp]
                            [:, 1024 + 128 * hp:1024 + 128 * (hp + 1)]
                            .rearrange("(kt p) d -> p kt d", p=128),
                    ).then_inc("Q" + "AB"[v % 2], 16)
                env.mark(("p2c.kv", v))
                for qh in range(4):
                    w = 4 * v + qh
                    if w >= 2:
                        s2 = ST[w - 2]
                        sync.wait("TE", ("p2c.sc", s2["v"], s2["qh"],
                                         s2["nck"] - 1))
                    sync.dma_start(
                        out=qtile[w % 2],
                        in_=dram["qT_d"][128 * (4 * hp + qh):128 * (4 * hp + qh) + 128,
                                         128 * qt:128 * (qt + 1)],
                    ).then_inc("R" + "01"[w % 2], 16)
                    env.mark(("p2c.q", v, qh))

    @env.tensor
    def _(tensor):
        tensor.wait("CN", "p2c.const")
        for wi in range(257):
            if wi < 256:
                w = wi
                s = ST[w]
                v, qh = s["v"], s["qh"]
                if qh == 0:
                    tensor.wait("Q" + "AB"[v % 2], ("p2c.kv", v))
                tensor.wait("R" + "01"[w % 2], ("p2c.q", v, qh))
                for ck in range(s["nck"]):
                    g = sc_of[w] + ck
                    if g >= 4:
                        vp, qp, cp, _ = sc_list[g - 4]
                        tensor.wait("SE", ("p2c.ev", vp, qp, cp))
                    ins = tensor.matmul(
                        psb[sc_list[g][3]][:, :], qtile[w % 2],
                        KT[v % 2][:, 512 * ck:512 * (ck + 1)],
                        start=True, stop=True)
                    ins.then_inc("TE")
                    env.mark(("p2c.sc", v, qh, ck))
            if wi >= 1:
                w = wi - 1
                s = ST[w]
                v, qh, nbt, nkt = s["v"], s["qh"], s["nbt"], 8 * s["nbt"]
                tensor.wait("VE", ("p2c.msk", v, qh))
                if w >= 1:
                    tensor.wait("SE", ("p2c.cev", *divmod(w - 1, 4)))
                for bt in range(nbt):
                    g = pt_of[w] + bt
                    bank, slot, owner = sl_map[(v, qh, bt)]
                    if g >= 2:
                        vp, qp, bp2, _, _, op = pt_list[g - 2]
                        tensor.wait(op, ("p2c.ptev", vp, qp, bp2))
                    pbf = psb[bank][:, :].bitcast(BF16)
                    for u2 in range(8):
                        kt = 8 * bt + u2
                        ins = tensor.transpose(
                            pbf[:, 128 * u2:128 * (u2 + 1)],
                            Pn(w)[:, 128 * kt:128 * (kt + 1)], identt)
                    ins.then_inc("TE")
                    env.mark(("p2c.pt", v, qh, bt))
                    if bt >= 1:
                        _, pslot, powner = sl_map[(v, qh, bt - 1)]
                        tensor.wait(powner, ("p2c.ptev", v, qh, bt - 1))
                        for u2 in range(8):
                            kt = 8 * (bt - 1) + u2
                            ins = tensor.matmul(
                                psb[6][:, 0:128],
                                VH[v % 2].rearrange("p (kt d) -> p kt d",
                                                    d=128)[:, kt, :],
                                PTb[:, 1024 * pslot:1024 * pslot + 1024]
                                    .rearrange("p (a b) -> p a b", b=128)[:, u2, :],
                                start=(kt == 0), stop=False)
                _, fslot, fowner = sl_map[(v, qh, nbt - 1)]
                tensor.wait(fowner, ("p2c.ptev", v, qh, nbt - 1))
                for u2 in range(8):
                    kt = 8 * (nbt - 1) + u2
                    ins = tensor.matmul(
                        psb[6][:, 0:128],
                        VH[v % 2].rearrange("p (kt d) -> p kt d", d=128)[:, kt, :],
                        PTb[:, 1024 * fslot:1024 * fslot + 1024]
                            .rearrange("p (a b) -> p a b", b=128)[:, u2, :],
                        start=(nbt == 1 and u2 == 0), stop=(u2 == 7))
                ins.then_inc("TE")
                env.mark(("p2c.pv", v, qh))
                tensor.wait("VE", ("p2c.rec", v, qh))
                if w >= 1:
                    tensor.wait("VE", ("p2c.ctt", *divmod(w - 1, 4)))
                ins = tensor.transpose(psb[7][0:1, 0:128],
                                       recf[:, 2 * (w % 2):2 * (w % 2) + 1],
                                       identf)
                ins.then_inc("TE")
                env.mark(("p2c.rtp", v, qh))
                tensor.wait("VE", ("p2c.rrow", v, qh))
                ins = tensor.matmul(psb[7][:, 128:256], onesf[0:1, :],
                                    recr[0:1, 128 * (w % 2):128 * (w % 2) + 128],
                                    start=True, stop=True)
                ins.then_inc("TE")
                env.mark(("p2c.rbc", v, qh))

    @env.vector
    def _(vector):
        vector.wait("CN", "p2c.const")
        vector.wait("TE", ("p2b.mm", 11, 7))
        vector.memset(onesf[0:1, :], 1.0)
        for w in range(256):
            s = ST[w]
            v, qh, qt = s["v"], s["qh"], s["qt"]
            L = 1024 * (qt + 1)
            p = w % 2
            vector.wait("SE", ("p2c.ev", v, qh, s["nck"] - 1))
            vector.wait("CN", ("p2c.mask", qt))
            vector.memzero(sums[:, p:p + 1]).then_inc("VE")
            env.mark(("p2c.sz", v, qh))
            vector.wait("VE", ("p2c.sz", v, qh))
            vector.scalar_tensor_tensor(
                out=Pn(w)[:, 0:L], in0=Pn(w)[:, 0:L], scalar=1.0,
                in1=mask01[:, 0:L], op0=ALU.mult, op1=ALU.mult,
                accum_out=sums[:, p:p + 1]).then_inc("VE")
            env.mark(("p2c.msk", v, qh))
            vector.wait("VE", ("p2c.msk", v, qh))
            vector.reciprocal(recf[:, 2 * p:2 * p + 1],
                              sums[:, p:p + 1]).then_inc("VE")
            env.mark(("p2c.rec", v, qh))
            for bt in range(s["nbt"]):
                bank, slot, owner = sl_map[(v, qh, bt)]
                if owner != "VE":
                    continue
                vector.wait("TE", ("p2c.pt", v, qh, bt))
                vector.tensor_copy(
                    out=PTb[:, 1024 * slot:1024 * slot + 1024],
                    in_=psb[bank][:, :].bitcast(BF16)).then_inc("VE")
                env.mark(("p2c.ptev", v, qh, bt))
            vector.wait("TE", ("p2c.rtp", v, qh))
            vector.tensor_copy(out=recr[0:1, 128 * p:128 * p + 128],
                               in_=psb[7][0:1, 0:128]).then_inc("VE")
            env.mark(("p2c.rrow", v, qh))
            vector.wait("SE", ("p2c.cev", v, qh))
            vector.wait("TE", ("p2c.rbc", v, qh))
            if w >= 2:
                vector.wait("SO" + "01"[w % 2], ("p2c.cst", *divmod(w - 2, 4)))
            vector.tensor_mul(ctxb[p], ctxf[p], psb[7][:, 128:256]
                              ).then_inc("VE")
            env.mark(("p2c.ctt", v, qh))

    @env.scalar
    def _(scalar):
        for wi in range(257):
            if wi < 256:
                w = wi
                s = ST[w]
                v, qh = s["v"], s["qh"]
                for ck in range(s["nck"]):
                    g = sc_of[w] + ck
                    scalar.wait("TE", ("p2c.sc", v, qh, ck))
                    if ck == 0 and w >= 2:
                        s2 = ST[w - 2]
                        scalar.wait("TE", ("p2c.pt", s2["v"], s2["qh"],
                                           s2["nbt"] - 1))
                    scalar.activation(out=Pn(w)[:, 512 * ck:512 * (ck + 1)],
                                      in_=psb[sc_list[g][3]][:, :],
                                      func=AF.Exp, scale=ISQ,
                                      bias=0.0).then_inc("SE")
                    env.mark(("p2c.ev", v, qh, ck))
            if wi >= 1:
                w = wi - 1
                s = ST[w]
                v, qh = s["v"], s["qh"]
                p = w % 2
                for bt in range(s["nbt"]):
                    bank, slot, owner = sl_map[(v, qh, bt)]
                    if owner != "SE":
                        continue
                    scalar.wait("TE", ("p2c.pt", v, qh, bt))
                    scalar.copy(
                        out=PTb[:, 1024 * slot:1024 * slot + 1024],
                        in_=psb[bank][:, :].bitcast(BF16)).then_inc("SE")
                    env.mark(("p2c.ptev", v, qh, bt))
                scalar.wait("TE", ("p2c.pv", v, qh))
                scalar.copy(out=ctxf[p], in_=psb[6][:, 0:128]).then_inc("SE")
                env.mark(("p2c.cev", v, qh))

    @env.gpsimd
    def _(gpsimd):
        for qt in range(8):
            for hp in range(8):
                v = 8 * qt + hp
                for qh in range(4):
                    w = 4 * v + qh
                    h = 4 * hp + qh
                    gpsimd.wait("VE", ("p2c.ctt", v, qh))
                    gpsimd.dma_start(
                        out=dram["ctxT_d"][128 * h:128 * (h + 1),
                                           128 * qt:128 * (qt + 1)],
                        in_=ctxb[w % 2]).then_inc("SO" + "01"[w % 2], 16)
                    env.mark(("p2c.cst", v, qh))
        if "kvdump_0" in dram:
            for b in range(8):
                gpsimd.dma_start(out=dram["kvdump_%d" % b],
                                 in_=dram["kvb_%d" % b]).then_inc("GD", 16)
                env.mark(("p2c.kvd", b))
            gpsimd.wait("GD", ("p2c.kvd", 7))
    ctx.pop()

    ctx.push()
    ctxT_sb = ctx.alloc(32 * 1024, BF16)
    wbuf2 = [ctx.alloc(32 * 128, BF16), ctx.alloc(32 * 128, BF16)]
    wstg = ctx.alloc(512, BF16)
    tstg = [ctx.alloc(512, BF16), ctx.alloc(512, BF16)]

    @env.sync
    def _(sync):
        sync.wait("SO0", ("p2c.cst", 62, 3))
        sync.wait("SO1", ("p2c.cst", 63, 3))
        sync.dma_start(out=ctxT_sb.rearrange("p (kk t) -> p kk t", t=1024),
                       in_=dram["ctxT_d"].rearrange("(kk p) t -> p kk t", p=128)
                       ).then_inc("CN", 16)
        env.mark("p2d.ctx")

    @env.gpsimd
    def _(gpsimd):
        def load_wo(om):
            if om >= 2:
                gpsimd.wait("TE", ("p2d.mm", om - 2, 1))
            gpsimd.dma_start(
                out=wbuf2[om % 2].rearrange("p (kk n) -> p kk n", n=128),
                in_=dram["l1wo"][:, 128 * om:128 * (om + 1)]
                    .rearrange("(kk p) n -> p kk n", p=128)).then_inc(
                        "W" + "01"[om % 2], 16)
            env.mark(("p2d.wl", om))
        load_wo(0)
        load_wo(1)
        for om in range(32):
            for tb in range(2):
                u = 2 * om + tb
                gpsimd.wait("VE", ("p2d.tev", om, tb))
                gpsimd.dma_start(
                    out=dram["att_d"][512 * tb:512 * (tb + 1),
                                      128 * om:128 * (om + 1)]
                        .rearrange("(j p) o -> p j o", p=128),
                    in_=tstg[u % 2].rearrange("p (j o) -> p j o", o=128),
                ).then_inc("SO" + "01"[u % 2], 16)
                env.mark(("p2d.ast", om, tb))
            if om + 2 < 32:
                load_wo(om + 2)

    @env.tensor
    def _(tensor):
        tensor.wait("CN", "p2d.ctx")
        for om in range(32):
            tensor.wait("W" + "01"[om % 2], ("p2d.wl", om))
            for tb in range(2):
                u = 2 * om + tb
                bank = u % 2
                if u >= 2:
                    tensor.wait("VE", ("p2d.wev", *divmod(u - 2, 2)))
                for kk in range(32):
                    ins = tensor.matmul(
                        psb[bank][:, :],
                        wbuf2[om % 2].rearrange("p (kk n) -> p kk n", n=128)[:, kk, :],
                        ctxT_sb[:, 1024 * kk + 512 * tb:1024 * kk + 512 * (tb + 1)],
                        start=(kk == 0), stop=(kk == 31))
                ins.then_inc("TE")
                env.mark(("p2d.mm", om, tb))
                tensor.wait("VE", ("p2d.wev", om, tb))
                if u >= 2:
                    tensor.wait("VE", ("p2d.tev", *divmod(u - 2, 2)))
                pbf = psb[2 + u % 2][:, :].bitcast(BF16)
                for j in range(4):
                    ins = tensor.transpose(pbf[:, 128 * j:128 * (j + 1)],
                                           wstg[:, 128 * j:128 * (j + 1)], identt)
                ins.then_inc("TE")
                env.mark(("p2d.tp", om, tb))

    @env.vector
    def _(vector):
        for om in range(32):
            for tb in range(2):
                u = 2 * om + tb
                vector.wait("TE", ("p2d.mm", om, tb))
                if u >= 1:
                    vector.wait("TE", ("p2d.tp", *divmod(u - 1, 2)))
                vector.tensor_copy(out=wstg, in_=psb[u % 2][:, :]).then_inc("VE")
                env.mark(("p2d.wev", om, tb))
                vector.wait("TE", ("p2d.tp", om, tb))
                if u >= 2:
                    vector.wait("SO" + "01"[u % 2], ("p2d.ast", *divmod(u - 2, 2)))
                vector.tensor_copy(out=tstg[u % 2],
                                   in_=psb[2 + u % 2][:, 0:256].bitcast(BF16)
                                   ).then_inc("VE")
                env.mark(("p2d.tev", om, tb))
    ctx.pop()

    ctx.push()
    xr3 = [ctx.alloc(4096, F32), ctx.alloc(4096, F32)]
    at3 = [ctx.alloc(4096, BF16), ctx.alloc(4096, BF16)]
    xo3 = [ctx.alloc(4096, F32), ctx.alloc(4096, F32)]

    @env.sync
    def _(sync):
        sync.wait("SO0", ("p2d.ast", 31, 0))
        sync.wait("SO1", ("p2d.ast", 31, 1))
        for tt in range(8):
            if tt >= 2:
                sync.wait("VE", ("p2e.add", tt - 2))
            sync.dma_start(out=xr3[tt % 2],
                           in_=dram["xres2"][128 * tt:128 * (tt + 1)]
                           ).then_inc("R" + "01"[tt % 2], 16)
            sync.dma_start(out=at3[tt % 2],
                           in_=dram["att_d"][128 * tt:128 * (tt + 1)]
                           ).then_inc("R" + "01"[tt % 2], 16)
            env.mark(("p2e.load", tt))

    @env.vector
    def _(vector):
        for tt in range(8):
            vector.wait("R" + "01"[tt % 2], ("p2e.load", tt))
            if tt >= 2:
                vector.wait("SO" + "01"[tt % 2], ("p2e.store", tt - 2))
            vector.tensor_add(xo3[tt % 2], xr3[tt % 2], at3[tt % 2]).then_inc("VE")
            env.mark(("p2e.add", tt))

    @env.gpsimd
    def _(gpsimd):
        for tt in range(8):
            gpsimd.wait("VE", ("p2e.add", tt))
            gpsimd.dma_start(out=dram["xres3"][128 * tt:128 * (tt + 1)],
                             in_=xo3[tt % 2]).then_inc("SO" + "01"[tt % 2], 16)
            env.mark(("p2e.store", tt))
    ctx.pop()
    ctx.pop()


def declare_ph2(nc, dram, standalone=True, debug=False):
    if "xres2" not in dram:
        dram["xres2"] = nc.dram_tensor(
            "xres2", [C, D], F32,
            kind="ExternalInput" if standalone else "Internal").ap()
    for nm, shp in [("l1wq", [D, D]), ("l1wk", [D, 1024]), ("l1wv", [D, 1024]),
                    ("l1wo", [D, D])]:
        dram[nm] = nc.dram_tensor(nm, shp, F32, kind="ExternalInput").ap()
    dram["cosP_c"] = nc.dram_tensor("cosP_c", [128, 512], BF16,
                                    kind="ExternalInput").ap()
    dram["sinP_c"] = nc.dram_tensor("sinP_c", [128, 512], BF16,
                                    kind="ExternalInput").ap()
    dram["mask_c"] = nc.dram_tensor("mask_c", [8, 128, 8192], BF16,
                                    kind="ExternalInput").ap()
    dram["identf"] = nc.dram_tensor("identf", [128, 128], F32,
                                    kind="ExternalInput").ap()
    if "ident" not in dram:
        dram["ident"] = nc.dram_tensor("ident", [128, 128], BF16,
                                       kind="ExternalInput").ap()
    if "epsin" not in dram:
        dram["epsin"] = nc.dram_tensor("epsin", [128, 1], F32,
                                       kind="ExternalInput").ap()
    dk = "ExternalOutput" if debug else "Internal"
    if debug:
        dram["dbg1"] = nc.dram_tensor("dbg1", [128, 32, C], BF16, kind=dk).ap()
    dram["qT_d"] = nc.dram_tensor("qT_d", [D, C], BF16, kind=dk).ap()
    for b in range(8):
        dram["agin2_%d" % b] = nc.dram_tensor(
            "agin2_%d" % b, [128, 2048], BF16, kind="Internal").ap()
        dram["kvb_%d" % b] = nc.dram_tensor(
            "kvb_%d" % b, [1024, 2048], BF16, kind="Internal").ap()
        if debug:
            dram["kvdump_%d" % b] = nc.dram_tensor(
                "kvdump_%d" % b, [1024, 2048], BF16, kind="ExternalOutput").ap()
            dram["agdump_%d" % b] = nc.dram_tensor(
                "agdump_%d" % b, [128, 2048], BF16, kind="ExternalOutput").ap()
    dram["ctxT_d"] = nc.dram_tensor("ctxT_d", [D, C], BF16, kind=dk).ap()
    dram["att_d"] = nc.dram_tensor("att_d", [C, D], BF16, kind=dk).ap()
    if "xres3" not in dram:
        dram["xres3"] = nc.dram_tensor(
            "xres3", [C, D], F32,
            kind="ExternalOutput" if standalone else "Internal").ap()


def host_prep_ph2(inp):
    sh = {}
    sh["l1wq"] = np.ascontiguousarray(np.asarray(inp["L1_attn_Wq"]))
    sh["l1wk"] = np.ascontiguousarray(np.asarray(inp["L1_attn_Wk"]))
    sh["l1wv"] = np.ascontiguousarray(np.asarray(inp["L1_attn_Wv"]))
    sh["l1wo"] = np.ascontiguousarray(np.asarray(inp["L1_attn_Wo"]))
    inv = 1.0 / 500000.0 ** (np.arange(0, 128, 2, dtype=np.float32) / 128)
    tpos = np.arange(T, dtype=np.float32)
    fr = np.outer(tpos, inv)
    cosf = np.cos(fr).astype(np.float32)
    sinf = np.sin(fr).astype(np.float32)
    sh["identf"] = np.eye(128, dtype=np.float32)
    ss = np.arange(T)
    keypos = 128 * (8 * (ss // 1024) + (ss % 1024) // 128) + ss % 128
    pc = []
    for c in range(NCORES):
        m = {}
        rows = zz_rows(c)
        m["cosP_c"] = np.ascontiguousarray(
            cosf[rows].reshape(8, 128, 64).transpose(1, 0, 2).reshape(128, 512)
        ).astype(BF)
        m["sinP_c"] = np.ascontiguousarray(
            sinf[rows].reshape(8, 128, 64).transpose(1, 0, 2).reshape(128, 512)
        ).astype(BF)
        msk = np.zeros((8, 128, T), np.float32)
        for qt in range(8):
            pos = 128 * (8 * qt + c) + np.arange(128)
            msk[qt] = (keypos[None, :] <= pos[:, None]).astype(np.float32)
        m["mask_c"] = msk.astype(BF)
        pc.append(m)
    return sh, pc


def build_ph2(debug=False):
    nc = bass.Bass("TRN2", target_bir_lowering=False, debug=False)
    dram = {}
    declare_ph2(nc, dram, standalone=True, debug=debug)
    es = contextlib.ExitStack()
    with es:
        ctx = Ctx(nc, es)
        ctx.sems(*PH1_SEMS)
        epst = ctx.alloc(1, F32)
        dram["epsc"] = epst
        with nc.Block() as block:
            @block.sync
            def _(sync):
                sync.dma_start(out=epst, in_=dram["epsin"]).then_inc(ctx.sem["CN"], 16)
            ctx.cnt["CN"] += 16

            run_twopass(ctx, block, lambda env: emit_ph2(env, dram))

            @block.gpsimd
            def _(gpsimd):
                gpsimd.wait_ge(ctx.sem["SO0"], ctx.cnt["SO0"])
                gpsimd.wait_ge(ctx.sem["SO1"], ctx.cnt["SO1"])
    return nc



def emit_ph3(env, dram, fence=None):
    ctx = env.ctx
    psb = ctx.psb
    epst = dram["epsc"]

    ctx.push()
    identt = ctx.alloc(128, BF16)
    stats = ctx.alloc(64, F32)
    varsum = ctx.alloc(8, F32)
    ctx.push()
    xlnT = ctx.alloc(32 * 1024, BF16)
    ctx.push()
    xr = [ctx.alloc(4096, F32), ctx.alloc(4096, F32)]
    lnscr = ctx.alloc(4096, BF16)

    @env.sync
    def _(sync):
        if fence:
            for s_, v_ in fence.items():
                sync.wait(s_, val=v_)
        sync.dma_start(out=identt, in_=dram["ident"]).then_inc("CN", 16)
        env.mark("p3.ident")
        for tt in range(8):
            if tt >= 2:
                sync.wait("VE", ("p3.norm", tt - 2))
                sync.wait("SE", ("p3.sq1", tt - 2))
            sync.dma_start(out=xr[tt % 2],
                           in_=dram["xres3"][128 * tt:128 * (tt + 1)]
                           ).then_inc("X" + "AB"[tt % 2], 16)
            env.mark(("p3.xload", tt))

    @env.vector
    def _(vector):
        vector.wait("CN", "p3.ident")
        vector.memzero(varsum)
        for tt in range(8):
            mus = stats[:, 8 * tt + 0:8 * tt + 1]
            nmu = stats[:, 8 * tt + 1:8 * tt + 2]
            rstd = stats[:, 8 * tt + 2:8 * tt + 3]
            sqv = stats[:, 8 * tt + 3:8 * tt + 4]
            nmur = stats[:, 8 * tt + 4:8 * tt + 5]
            vector.wait("X" + "AB"[tt % 2], ("p3.xload", tt))
            vector.reduce_sum(out=mus, in_=xr[tt % 2], axis=AX).then_inc("VE")
            env.mark(("p3.mus", tt))
            vector.wait("VE", ("p3.mus", tt))
            vector.tensor_scalar_mul(nmu, mus, -1.0 / 4096.0).then_inc("VE")
            env.mark(("p3.nmu", tt))
            vector.wait("SE", ("p3.sq2", tt))
            vector.reciprocal(rstd, sqv).then_inc("VE")
            env.mark(("p3.rstd", tt))
            vector.wait("VE", ("p3.rstd", tt))
            vector.tensor_mul(nmur, nmu, rstd).then_inc("VE")
            env.mark(("p3.nmur", tt))
            vector.wait("VE", ("p3.nmur", tt))
            vector.tensor_scalar(out=lnscr, in0=xr[tt % 2], scalar1=rstd,
                                 scalar2=nmur, op0=ALU.mult, op1=ALU.add
                                 ).then_inc("VE")
            env.mark(("p3.norm", tt))
            for b in range(4):
                vector.wait("TE", ("p3.tp", tt, b))
                vector.tensor_copy(
                    out=xlnT.rearrange("p (kk t) -> p kk t", t=1024)
                        [:, 8 * b:8 * b + 8, 128 * tt:128 * (tt + 1)],
                    in_=psb[b][:, :].bitcast(BF16)
                        .rearrange("p (kk t) -> p kk t", t=128),
                ).then_inc("VE")
                env.mark(("p3.evac", tt, b))

    @env.scalar
    def _(scalar):
        for tt in range(8):
            nmu = stats[:, 8 * tt + 1:8 * tt + 2]
            sqv = stats[:, 8 * tt + 3:8 * tt + 4]
            scalar.wait("VE", ("p3.nmu", tt))
            scalar.activation(out=lnscr, in_=xr[tt % 2], func=AF.Square, bias=nmu,
                              scale=1.0, accum_out=varsum[:, tt:tt + 1]
                              ).then_inc("SE")
            env.mark(("p3.sq1", tt))
            scalar.wait("SE", ("p3.sq1", tt))
            scalar.activation(out=sqv, in_=varsum[:, tt:tt + 1], func=AF.Sqrt,
                              scale=1.0 / 4096.0, bias=epst).then_inc("SE")
            env.mark(("p3.sq2", tt))

    @env.tensor
    def _(tensor):
        for tt in range(8):
            tensor.wait("VE", ("p3.norm", tt))
            for b in range(4):
                if 4 * tt + b >= 4:
                    tpp, bp = divmod(4 * tt + b - 4, 4)
                    tensor.wait("VE", ("p3.evac", tpp, bp))
                pbf = psb[b][:, :].bitcast(BF16)
                for u in range(8):
                    kk = 8 * b + u
                    ins = tensor.transpose(pbf[:, 128 * u:128 * (u + 1)],
                                           lnscr[:, 128 * kk:128 * (kk + 1)],
                                           identt)
                ins.then_inc("TE")
                env.mark(("p3.tp", tt, b))
    ctx.pop()

    ctx.push()
    wgt = ctx.alloc(32 * 8, BF16)
    at_sb = ctx.alloc(32 * 128, BF16)
    b_sb = ctx.alloc(4096, BF16)
    i8_sb = ctx.alloc(128, BF16)
    bbase = ctx.alloc(32, F32)
    lg = ctx.alloc(8, F32)
    m1n = ctx.alloc(8, F32)
    pe_ = ctx.alloc(8, F32)
    mxm = ctx.alloc(8, F32)
    lgm = ctx.alloc(8, F32)
    wexp = ctx.alloc(8, F32)
    route = ctx.alloc(8, BF16)
    routeT = ctx.alloc(1024, BF16)
    r_sb = ctx.alloc(1024, BF16)
    hw_sb = ctx.alloc(1024, BF16)
    wbuf3 = [ctx.alloc(32 * 128, BF16), ctx.alloc(32 * 128, BF16)]
    wstg = ctx.alloc(512, BF16)
    tstg = [ctx.alloc(512, BF16), ctx.alloc(512, BF16)]

    @env.sync
    def _(sync):
        sync.wait("VE", ("p3.evac", 7, 3))
        sync.wait("SE", ("p3.sq2", 7))
        sync.dma_start(out=i8_sb[0:8], in_=dram["i8c"]).then_inc("CN", 16)
        sync.dma_start(out=bbase, in_=dram["bbase_c"]).then_inc("CN", 16)
        env.mark("p3b.res")

    @env.gpsimd
    def _(gpsimd):
        gpsimd.wait("VE", ("p3.evac", 7, 3))
        gpsimd.wait("SE", ("p3.sq2", 7))
        gpsimd.dma_start(out=at_sb.rearrange("p (kk n) -> p kk n", n=128),
                         in_=dram["l1A"].rearrange("(kk p) n -> p kk n", p=128)
                         ).then_inc("CN2", 16)
        gpsimd.dma_start(out=b_sb[0:128], in_=dram["l1B"]).then_inc("CN2", 16)
        gpsimd.dma_start(out=wgt.rearrange("p (kk n) -> p kk n", n=8),
                         in_=dram["wgateT"].rearrange("(kk p) n -> p kk n", p=128)
                         ).then_inc("CN2", 16)
        env.mark("p3b.res2")
        def load_wb(om):
            if om >= 2:
                gpsimd.wait("TE", ("p3b.mm", om - 2, 1))
            gpsimd.dma_start(
                out=wbuf3[om % 2].rearrange("p (kk n) -> p kk n", n=128),
                in_=dram["wbaseT"][:, 128 * om:128 * (om + 1)]
                    .rearrange("(kk p) n -> p kk n", p=128)).then_inc(
                        "W" + "01"[om % 2], 16)
            env.mark(("p3b.wl", om))
        load_wb(0)
        load_wb(1)
        for om in range(32):
            for tb in range(2):
                u = 2 * om + tb
                gpsimd.wait("VE", ("p3b.tev", om, tb))
                gpsimd.dma_start(
                    out=dram["att2_d"][512 * tb:512 * (tb + 1),
                                       128 * om:128 * (om + 1)]
                        .rearrange("(j p) o -> p j o", p=128),
                    in_=tstg[u % 2].rearrange("p (j o) -> p j o", o=128),
                ).then_inc("SO" + "01"[u % 2], 16)
                env.mark(("p3b.ast", om, tb))
            if om + 2 < 32:
                load_wb(om + 2)

    @env.tensor
    def _(tensor):
        tensor.wait("CN", "p3b.res")
        tensor.wait("CN2", "p3b.res2")
        tensor.wait("VE", ("p3.evac", 7, 3))
        tensor.wait("CN2", "p3b.res2")
        for tm in range(8):
            if tm >= 1:
                tensor.wait("VE", ("p3b.lgev", tm - 1))
            for kk in range(32):
                ins = tensor.matmul(
                    psb[4][:, 0:8],
                    xlnT[:, 1024 * kk + 128 * tm:1024 * kk + 128 * (tm + 1)],
                    wgt.rearrange("p (kk n) -> p kk n", n=8)[:, kk, :],
                    start=(kk == 0), stop=(kk == 31))
            ins.then_inc("TE")
            env.mark(("p3b.lg", tm))
            tensor.wait("VE", ("p3b.route", tm))
            ins = tensor.transpose(psb[5][0:8, 0:64].bitcast(BF16)[0:8, 0:128],
                                   route, identt)
            ins.then_inc("TE")
            env.mark(("p3b.rtp", tm))
        for tb in range(2):
            tensor.wait("VE", ("p3b.rtev", 7))
            if tb >= 0:
                pass
            for kk in range(32):
                ins = tensor.matmul(
                    psb[6][:, :],
                    at_sb.rearrange("p (kk n) -> p kk n", n=128)[:, kk, :],
                    xlnT[:, 1024 * kk + 512 * tb:1024 * kk + 512 * (tb + 1)],
                    start=(kk == 0), stop=(kk == 31))
            ins.then_inc("TE")
            env.mark(("p3b.rmm", tb))
            tensor.wait("VE", ("p3b.rev", tb))
            ins = tensor.matmul(psb[7][:, :], i8_sb[0:8, :],
                                routeT[0:8, 512 * tb:512 * (tb + 1)],
                                start=True, stop=True)
            ins.then_inc("TE")
            env.mark(("p3b.rwb", tb))
        for om in range(32):
            tensor.wait("W" + "01"[om % 2], ("p3b.wl", om))
            for tb in range(2):
                u = 2 * om + tb
                bank = u % 2
                if u >= 2:
                    tensor.wait("VE", ("p3b.wev", *divmod(u - 2, 2)))
                for kk in range(32):
                    tensor.matmul(
                        psb[bank][:, :],
                        wbuf3[om % 2].rearrange("p (kk n) -> p kk n", n=128)[:, kk, :],
                        xlnT[:, 1024 * kk + 512 * tb:1024 * kk + 512 * (tb + 1)],
                        start=(kk == 0), stop=False)
                tensor.wait("VE", ("p3b.hw", tb))
                ins = tensor.matmul(
                    psb[bank][:, :], b_sb[0:128, 128 * om:128 * (om + 1)],
                    hw_sb[0:128, 512 * tb:512 * (tb + 1)],
                    start=False, stop=True)
                ins.then_inc("TE")
                env.mark(("p3b.mm", om, tb))
                tensor.wait("VE", ("p3b.wev", om, tb))
                if u >= 2:
                    tensor.wait("VE", ("p3b.tev", *divmod(u - 2, 2)))
                pbf = psb[2 + u % 2][:, :].bitcast(BF16)
                for j in range(4):
                    ins = tensor.transpose(pbf[:, 128 * j:128 * (j + 1)],
                                           wstg[:, 128 * j:128 * (j + 1)], identt)
                ins.then_inc("TE")
                env.mark(("p3b.tp", om, tb))

    @env.vector
    def _(vector):
        vector.wait("CN", "p3b.res")
        for tm in range(8):
            vector.wait("TE", ("p3b.lg", tm))
            vector.tensor_copy(out=lg, in_=psb[4][:, 0:8]).then_inc("VE")
            env.mark(("p3b.lgev", tm))
            vector.wait("VE", ("p3b.lgev", tm))
            vector.reduce_max(out=m1n[:, 0:1], in_=lg, axis=AX, negate=True
                              ).then_inc("VE")
            env.mark(("p3b.m1", tm))
            vector.wait("VE", ("p3b.m1", tm))
            vector.tensor_scalar_mul(m1n[:, 1:2], m1n[:, 0:1], -1.0).then_inc("VE")
            env.mark(("p3b.m1p", tm))
            vector.wait("VE", ("p3b.m1p", tm))
            vector.tensor_single_scalar(out=mxm, in_=lg, scalar=m1n[:, 1:2],
                                        op=ALU.is_ge).then_inc("VE")
            env.mark(("p3b.mx", tm))
            vector.wait("VE", ("p3b.mx", tm))
            vector.scalar_tensor_tensor(out=lgm, in0=mxm, scalar=-1e30, in1=lg,
                                        op0=ALU.mult, op1=ALU.add).then_inc("VE")
            env.mark(("p3b.lgm", tm))
            vector.wait("VE", ("p3b.lgm", tm))
            vector.reduce_max(out=m1n[:, 2:3], in_=lgm, axis=AX, negate=True
                              ).then_inc("VE")
            env.mark(("p3b.m2", tm))
            vector.wait("VE", ("p3b.m2", tm))
            vector.tensor_scalar_mul(m1n[:, 3:4], m1n[:, 2:3], -1.0).then_inc("VE")
            env.mark(("p3b.m2p", tm))
            vector.wait("VE", ("p3b.m2p", tm))
            vector.tensor_single_scalar(out=mxm, in_=lg, scalar=m1n[:, 3:4],
                                        op=ALU.is_ge).then_inc("VE")
            env.mark(("p3b.mask2", tm))
            vector.wait("SE", ("p3b.pe", tm))
            vector.wait("VE", ("p3b.mask2", tm))
            vector.tensor_mul(wexp, pe_, mxm).then_inc("VE")
            env.mark(("p3b.w", tm))
            vector.wait("VE", ("p3b.w", tm))
            vector.reduce_sum(out=m1n[:, 4:5], in_=wexp, axis=AX).then_inc("VE")
            env.mark(("p3b.den", tm))
            vector.wait("VE", ("p3b.den", tm))
            vector.reciprocal(m1n[:, 5:6], m1n[:, 4:5]).then_inc("VE")
            env.mark(("p3b.rden", tm))
            vector.wait("VE", ("p3b.rden", tm))
            vector.tensor_single_scalar(out=route, in_=wexp, scalar=m1n[:, 5:6],
                                        op=ALU.mult).then_inc("VE")
            env.mark(("p3b.route", tm))
            vector.wait("TE", ("p3b.rtp", tm))
            vector.tensor_copy(out=routeT[0:8, 128 * tm:128 * (tm + 1)],
                               in_=psb[5][0:8, 0:64].bitcast(BF16)[0:8, 0:128]
                               ).then_inc("VE")
            env.mark(("p3b.rtev", tm))
        for tb in range(2):
            vector.wait("TE", ("p3b.rmm", tb))
            vector.tensor_copy(out=r_sb[0:128, 512 * tb:512 * (tb + 1)],
                               in_=psb[6][:, :]).then_inc("VE")
            env.mark(("p3b.rev", tb))
            vector.wait("TE", ("p3b.rwb", tb))
            vector.wait("VE", ("p3b.rev", tb))
            vector.tensor_mul(hw_sb[0:128, 512 * tb:512 * (tb + 1)],
                              r_sb[0:128, 512 * tb:512 * (tb + 1)],
                              psb[7][:, :]).then_inc("VE")
            env.mark(("p3b.hw", tb))
        for om in range(32):
            for tb in range(2):
                u = 2 * om + tb
                vector.wait("TE", ("p3b.mm", om, tb))
                if u >= 1:
                    vector.wait("TE", ("p3b.tp", *divmod(u - 1, 2)))
                vector.tensor_scalar(out=wstg, in0=psb[u % 2][:, :],
                                     scalar1=bbase[:, om:om + 1], scalar2=None,
                                     op0=ALU.add).then_inc("VE")
                env.mark(("p3b.wev", om, tb))
                vector.wait("TE", ("p3b.tp", om, tb))
                if u >= 2:
                    vector.wait("SO" + "01"[u % 2], ("p3b.ast", *divmod(u - 2, 2)))
                vector.tensor_copy(out=tstg[u % 2],
                                   in_=psb[2 + u % 2][:, 0:256].bitcast(BF16)
                                   ).then_inc("VE")
                env.mark(("p3b.tev", om, tb))

    @env.scalar
    def _(scalar):
        for tm in range(8):
            scalar.wait("VE", ("p3b.m1", tm))
            scalar.activation(out=pe_, in_=lg, func=AF.Exp, scale=1.0,
                              bias=m1n[:, 0:1]).then_inc("SE")
            env.mark(("p3b.pe", tm))
    ctx.pop()
    ctx.pop()

    ctx.push()
    xr4 = [ctx.alloc(4096, F32), ctx.alloc(4096, F32)]
    at4 = [ctx.alloc(4096, BF16), ctx.alloc(4096, BF16)]
    xo4 = [ctx.alloc(4096, F32), ctx.alloc(4096, F32)]

    @env.sync
    def _(sync):
        sync.wait("SO0", ("p3b.ast", 31, 0))
        sync.wait("SO1", ("p3b.ast", 31, 1))
        for tt in range(8):
            if tt >= 2:
                sync.wait("VE", ("p3c.add", tt - 2))
            sync.dma_start(out=xr4[tt % 2],
                           in_=dram["xres3"][128 * tt:128 * (tt + 1)]
                           ).then_inc("R" + "01"[tt % 2], 16)
            sync.dma_start(out=at4[tt % 2],
                           in_=dram["att2_d"][128 * tt:128 * (tt + 1)]
                           ).then_inc("R" + "01"[tt % 2], 16)
            env.mark(("p3c.load", tt))

    @env.vector
    def _(vector):
        for tt in range(8):
            vector.wait("R" + "01"[tt % 2], ("p3c.load", tt))
            if tt >= 2:
                vector.wait("SO" + "01"[tt % 2], ("p3c.store", tt - 2))
            vector.tensor_add(xo4[tt % 2], xr4[tt % 2], at4[tt % 2]).then_inc("VE")
            env.mark(("p3c.add", tt))

    @env.gpsimd
    def _(gpsimd):
        for tt in range(8):
            gpsimd.wait("VE", ("p3c.add", tt))
            gpsimd.dma_start(out=dram["y_rows"][128 * tt:128 * (tt + 1)],
                             in_=xo4[tt % 2]).then_inc("SO" + "01"[tt % 2], 16)
            env.mark(("p3c.store", tt))
    ctx.pop()
    ctx.pop()


def declare_ph3(nc, dram, standalone=True):
    if "xres3" not in dram:
        dram["xres3"] = nc.dram_tensor(
            "xres3", [C, D], F32,
            kind="ExternalInput" if standalone else "Internal").ap()
    dram["wbaseT"] = nc.dram_tensor("wbaseT", [D, D], F32, kind="ExternalInput").ap()
    dram["wgateT"] = nc.dram_tensor("wgateT", [D, 8], F32, kind="ExternalInput").ap()
    dram["l1A"] = nc.dram_tensor("l1A", [D, 128], F32, kind="ExternalInput").ap()
    dram["l1B"] = nc.dram_tensor("l1B", [128, D], F32, kind="ExternalInput").ap()
    dram["i8c"] = nc.dram_tensor("i8c", [8, 128], BF16, kind="ExternalInput").ap()
    dram["bbase_c"] = nc.dram_tensor("bbase_c", [128, 32], F32,
                                     kind="ExternalInput").ap()
    if "ident" not in dram:
        dram["ident"] = nc.dram_tensor("ident", [128, 128], BF16,
                                       kind="ExternalInput").ap()
    if "epsin" not in dram:
        dram["epsin"] = nc.dram_tensor("epsin", [128, 1], F32,
                                       kind="ExternalInput").ap()
    dram["att2_d"] = nc.dram_tensor("att2_d", [C, D], BF16, kind="Internal").ap()
    dram["y_rows"] = nc.dram_tensor("y_rows", [C, D], F32,
                                    kind="ExternalOutput").ap()


def host_prep_ph3(inp):
    sh = {}
    sh["wbaseT"] = np.ascontiguousarray(np.asarray(inp["L1_ffn_W_base"]).T)
    sh["wgateT"] = np.ascontiguousarray(np.asarray(inp["L1_ffn_W_gate"]).T)
    A = np.asarray(inp["L1_ffn_A"])
    sh["l1A"] = np.ascontiguousarray(A.transpose(2, 0, 1).reshape(D, 128))
    B = np.asarray(inp["L1_ffn_B"])
    sh["l1B"] = np.ascontiguousarray(B.transpose(0, 2, 1).reshape(128, D))
    i8 = np.zeros((8, 128), np.float32)
    for e in range(8):
        i8[e, 16 * e:16 * (e + 1)] = 1.0
    sh["i8c"] = i8.astype(BF)
    sh["bbase_c"] = np.ascontiguousarray(
        np.asarray(inp["L1_ffn_b_base"]).reshape(32, 128).T)
    return sh


def build_ph3():
    nc = bass.Bass("TRN2", target_bir_lowering=False, debug=False)
    dram = {}
    declare_ph3(nc, dram, standalone=True)
    es = contextlib.ExitStack()
    with es:
        ctx = Ctx(nc, es)
        ctx.sems(*PH1_SEMS)
        epst = ctx.alloc(1, F32)
        dram["epsc"] = epst
        with nc.Block() as block:
            @block.sync
            def _(sync):
                sync.dma_start(out=epst, in_=dram["epsin"]).then_inc(ctx.sem["CN"], 16)
            ctx.cnt["CN"] += 16
            run_twopass(ctx, block, lambda env: emit_ph3(env, dram))

            @block.gpsimd
            def _(gpsimd):
                gpsimd.wait_ge(ctx.sem["SO0"], ctx.cnt["SO0"])
                gpsimd.wait_ge(ctx.sem["SO1"], ctx.cnt["SO1"])
    return nc


def build_full(dbg=False, nph=4):
    nc = bass.Bass("TRN2", target_bir_lowering=False, debug=False)
    dram = {}
    if dbg:
        dram["xres1"] = nc.dram_tensor("xres1", [C, D], F32, kind="ExternalOutput").ap()
        dram["xres2"] = nc.dram_tensor("xres2", [C, D], F32, kind="ExternalOutput").ap()
        if nph >= 3:
            dram["xres3"] = nc.dram_tensor("xres3", [C, D], F32, kind="ExternalOutput").ap()
    declare_ph0(nc, dram, standalone=False)
    declare_ph1(nc, dram, standalone=(nph == 2) and not dbg)
    if nph >= 3:
        declare_ph2(nc, dram, standalone=False)
    if nph >= 4:
        declare_ph3(nc, dram, standalone=False)
    es = contextlib.ExitStack()
    with es:
        ctx = Ctx(nc, es)
        ctx.sems(*PH1_SEMS)
        epst = ctx.alloc(1, F32)
        dram["epsc"] = epst
        with nc.Block() as block:
            @block.sync
            def _(sync):
                sync.dma_start(out=epst, in_=dram["epsin"]).then_inc(ctx.sem["CN"], 16)
            ctx.cnt["CN"] += 16

            def emit(env):
                emit_ph0(env, dram)
                f1 = {"SO0": ctx.cnt["SO0"], "SO1": ctx.cnt["SO1"]}
                emit_ph1(env, dram, fence=f1)
                emit_ph1_down(env, dram)
                if nph >= 3:
                    f2 = {"SO0": ctx.cnt["SO0"], "SO1": ctx.cnt["SO1"]}
                    emit_ph2(env, dram, fence=f2)
                if nph >= 4:
                    f3 = {"SO0": ctx.cnt["SO0"], "SO1": ctx.cnt["SO1"]}
                    emit_ph3(env, dram, fence=f3)
            run_twopass(ctx, block, emit)

            @block.gpsimd
            def _(gpsimd):
                gpsimd.wait_ge(ctx.sem["SO0"], ctx.cnt["SO0"])
                gpsimd.wait_ge(ctx.sem["SO1"], ctx.cnt["SO1"])
    return nc

def kernel(**inputs):
    ident = np.eye(128, dtype=np.float32).astype(BF)
    eps = np.full((128, 1), 1e-5, np.float32)
    sh0, pc0 = host_prep(inputs)
    sh1, pc1 = host_prep_ph1(inputs)
    sh2, pc2 = host_prep_ph2(inputs)
    sh3 = host_prep_ph3(inputs)
    nc = build_full()
    in_maps = []
    for c in range(NCORES):
        m = {}
        m.update(pc0[c])
        m.update(pc1[c])
        m.update(pc2[c])
        for k in ["wq_r", "wk_r", "wv_r", "bq", "bk", "bv"]:
            m[k] = sh0[k]
        m["Gc"] = sh1["Gc"]
        m["sel"] = sh1["sel"]
        for k in ["l1wq", "l1wk", "l1wv", "l1wo", "identf"]:
            m[k] = sh2[k]
        m.update(sh3)
        m["ident"] = ident
        m["epsin"] = eps
        in_maps.append(m)
    res = run_bass_kernel_spmd(nc, in_maps, core_ids=list(range(NCORES)))
    y = np.empty((T, D), np.float32)
    for c in range(NCORES):
        y[zz_rows(c)] = np.asarray(res.results[c]["y_rows"])
    return y.reshape(1, T, D).astype(np.float32)


# version 18
# speedup vs baseline: 1.2801x; 1.0033x over previous
import contextlib
import numpy as np
import ml_dtypes
import concourse.bass as bass
import concourse.mybir as mybir
from concourse.bass_utils import run_bass_kernel_spmd

F32 = mybir.dt.float32
BF16 = mybir.dt.bfloat16
U8 = mybir.dt.uint8
AF = mybir.ActivationFunctionType
ALU = mybir.AluOpType
AX = mybir.AxisListType.X
BF = ml_dtypes.bfloat16

NCORES = 8
T = 8192
D = 4096
C = T // NCORES
ISQ = 0.08838834764831843


def zz_rows(c):
    return np.concatenate([np.arange(128) + 128 * (8 * b + c)
                           for b in range(8)])


def host_prep(inp):
    x = np.ascontiguousarray(np.asarray(inp["L0_attn_x"]).reshape(T, D))
    sh = {}
    Wq = np.asarray(inp["L0_attn_Wq"])
    sh["wq_r"] = np.ascontiguousarray(
        Wq.reshape(D, 128, 8, 4).transpose(0, 2, 3, 1).reshape(D, D))
    Wk = np.asarray(inp["L0_attn_Wk"])
    sh["wk_r"] = np.ascontiguousarray(Wk.transpose(0, 2, 1).reshape(D, 1024))
    sh["wv_r"] = np.ascontiguousarray(
        np.asarray(inp["L0_attn_Wv"]).transpose(0, 2, 1).reshape(D, 1024))
    sh["bq"] = np.ascontiguousarray(np.asarray(inp["L0_attn_bq"]))
    sh["bk"] = np.ascontiguousarray(np.asarray(inp["L0_attn_bk"]))
    sh["bv"] = np.ascontiguousarray(np.asarray(inp["L0_attn_bv"]))
    sh["ident"] = np.eye(128, dtype=np.float32).astype(BF)
    sh["epsin"] = np.full((128, 1), 1e-5, np.float32)
    percore = [{"x_rows": np.ascontiguousarray(x[zz_rows(c)])}
               for c in range(NCORES)]
    return sh, percore



def host_prep_ph1(inp):
    qwgu = np.asarray(inp["L0_ffn_qweight_gu"])
    qzgu = np.asarray(inp["L0_ffn_qzeros_gu"])
    sgu = np.asarray(inp["L0_ffn_scales_gu"])
    qwdn = np.asarray(inp["L0_ffn_qweight_dn"])
    qzdn = np.asarray(inp["L0_ffn_qzeros_dn"])
    sdn = np.asarray(inp["L0_ffn_scales_dn"])
    qwgu_B = np.ascontiguousarray(qwgu).view(np.uint8).reshape(D, 11008)
    qzgu_B = np.ascontiguousarray(qzgu).view(np.uint8).reshape(32, 11008)
    qwdn_B = np.ascontiguousarray(qwdn).view(np.uint8).reshape(11008, 2048)
    qzdn_B = np.ascontiguousarray(qzdn).view(np.uint8).reshape(86, 2048)
    HSL = 1376
    BSL = HSL // 2
    pc = []
    for c in range(NCORES):
        m = {}
        m["qwgu_b"] = np.ascontiguousarray(
            np.concatenate([qwgu_B[:, BSL * c:BSL * (c + 1)],
                            qwgu_B[:, 5504 + BSL * c:5504 + BSL * (c + 1)]], axis=1))
        m["qzgu_b"] = np.ascontiguousarray(
            np.concatenate([qzgu_B[:, BSL * c:BSL * (c + 1)],
                            qzgu_B[:, 5504 + BSL * c:5504 + BSL * (c + 1)]], axis=1))
        m["sgu_c"] = np.ascontiguousarray(
            np.concatenate([sgu[:, HSL * c:HSL * (c + 1)],
                            sgu[:, 11008 + HSL * c:11008 + HSL * (c + 1)]], axis=1))
        m["qwdn_b"] = np.ascontiguousarray(qwdn_B[HSL * c:HSL * (c + 1)])
        g_lo = (HSL * c) // 128
        g_hi = (HSL * (c + 1) - 1) // 128
        ng = g_hi - g_lo + 1
        qz = np.zeros((12, 2048), np.uint8)
        qz[:ng] = qzdn_B[g_lo:g_hi + 1]
        m["qzdn_b"] = qz
        sd = np.zeros((12, 4096), np.float32)
        sd[:ng] = sdn[g_lo:g_hi + 1]
        m["sdn_c"] = sd
        Ei = np.zeros((11 * 128, 16), np.float32)
        Eid = np.zeros((12, 11 * 128), np.float32)
        for r in range(HSL):
            gl = (HSL * c + r) // 128 - g_lo
            Ei[r, gl] = 1.0
            Eid[gl, r] = 1.0
        m["Ei"] = Ei.astype(BF)
        m["Eid"] = np.ascontiguousarray(
            Eid.reshape(12, 11, 128).transpose(1, 0, 2)).astype(BF)
        pc.append(m)
    shared = {}
    Gc = np.zeros((D, 32), np.float32)
    for k in range(D):
        Gc[k, k // 128] = 1.0
    shared["Gc"] = Gc.astype(BF)
    sel = np.zeros((32, 32 * 128), np.float32)
    for kk in range(32):
        sel[kk, 128 * kk:128 * (kk + 1)] = 1.0
    shared["sel"] = sel.astype(BF)
    return shared, pc


class Ctx:
    def __init__(self, nc, es):
        self.nc = nc
        self.es = es
        self.sem = {}
        self.cnt = {}
        arena_cols = 51200
        self.arena = es.enter_context(nc.sbuf_tensor("arena", [128, arena_cols], F32))
        self.aoff = 0
        self.amax = arena_cols * 4
        self.marks = []
        self.psb = [es.enter_context(nc.psum_tensor(f"psb{i}", [128, 512], F32))
                    for i in range(8)]

    def sems(self, *names):
        for n in names:
            self.sem[n] = self.es.enter_context(self.nc.semaphore(n))
            self.cnt[n] = 0

    def alloc(self, cols, dtype):
        sz = {F32: 4, BF16: 2, U8: 1}[dtype]
        nbytes = cols * sz
        self.aoff = (self.aoff + 63) & ~63
        assert self.aoff + nbytes <= self.amax, f"SBUF arena overflow at {self.aoff + nbytes}"
        st = self.aoff // 4
        n32 = (nbytes + 3) // 4
        ap = self.arena[:, st:st + n32]
        self.aoff += n32 * 4
        if dtype != F32:
            ap = ap.bitcast(dtype)
        return ap

    def push(self):
        self.marks.append(self.aoff)

    def pop(self):
        self.aoff = self.marks.pop()



def emit_ph0(env, dram):
    ctx = env.ctx
    cnt = ctx.cnt
    psb = ctx.psb
    CN, XA, XB, W0, W1 = "CN", "XA", "XB", "W0", "W1"
    TS0, TS1, QA, QB = "TS0", "TS1", "QA", "QB"
    R0, R1, SO0, SO1, GD, TE, VE, SE = "R0", "R1", "SO0", "SO1", "GD", "TE", "VE", "SE"
    XS = [XA, XB]
    WS = [W0, W1]
    TSs = [TS0, TS1]
    QS = [QA, QB]
    RS = [R0, R1]
    SOS = [SO0, SO1]

    ctx.push()
    identt = ctx.alloc(128, BF16)
    stats = ctx.alloc(64, F32)
    bqs = ctx.alloc(32, F32)
    bqt = ctx.alloc(8, F32)
    bks = ctx.alloc(8, F32)
    bvs = ctx.alloc(8, F32)
    varsum = ctx.alloc(8, F32)
    qT = ctx.alloc(8 * 1024, BF16)
    kT = ctx.alloc(8 * 1024, BF16)
    vT = ctx.alloc(8 * 1024, BF16)
    epst = dram["epsc"]

    ctx.push()
    xlnT = ctx.alloc(32 * 1024, BF16)
    ctx.push()
    xr = [ctx.alloc(4096, F32), ctx.alloc(4096, F32)]
    lnscr = ctx.alloc(4096, BF16)

    V0 = cnt["VE"]; S0 = cnt["SE"]; E0 = cnt["TE"]
    CN0 = cnt["CN"]; XA0 = cnt["XA"]; XB0 = cnt["XB"]
    XBASE = [XA0, XB0]
    def ve_s1(tt, k):
        return V0 + 1 + 9 * tt + k

    @env.sync
    def _(sync):
        sync.dma_start(out=identt, in_=dram["ident"]).then_inc(CN, 16)
        sync.dma_start(out=bqs, in_=dram["bq"]).then_inc(CN, 16)
        sync.dma_start(out=bks, in_=dram["bk"]).then_inc(CN, 16)
        sync.dma_start(out=bvs, in_=dram["bv"]).then_inc(CN, 16)
        for tt in range(8):
            if tt >= 2:
                sync.wait(VE, val=ve_s1(tt - 2, 5))
                sync.wait(SE, val=S0 + 2 * (tt - 2) + 1)
            sync.dma_start(out=xr[tt % 2],
                           in_=dram["x_rows"][128 * tt:128 * (tt + 1)]
                           ).then_inc(XS[tt % 2], 16)
    assert cnt["CN"] == (CN0 + 16 * 4), ("cnt CN", cnt["CN"], CN0 + 16 * 4)
    assert cnt["XA"] == (XA0 + 16 * 4), ("cnt XA", cnt["XA"], XA0 + 16 * 4)
    assert cnt["XB"] == (XB0 + 16 * 4), ("cnt XB", cnt["XB"], XB0 + 16 * 4)

    @env.vector
    def _(vector):
        vector.wait(CN, val=CN0 + 64)
        vector.memzero(varsum)
        vector.tensor_reduce(out=bqt.rearrange("p (a b) -> p a b", b=1),
                             in_=bqs.rearrange("p (a b) -> p a b", b=4),
                             axis=AX, op=ALU.add).then_inc(VE, 1)
        for tt in range(8):
            mus = stats[:, 8 * tt + 0:8 * tt + 1]
            nmu = stats[:, 8 * tt + 1:8 * tt + 2]
            rstd = stats[:, 8 * tt + 2:8 * tt + 3]
            sqv = stats[:, 8 * tt + 3:8 * tt + 4]
            nmur = stats[:, 8 * tt + 4:8 * tt + 5]
            vector.wait(XS[tt % 2], val=XBASE[tt % 2] + 16 * (tt // 2 + 1))
            vector.reduce_sum(out=mus, in_=xr[tt % 2], axis=AX).then_inc(VE, 1)
            vector.wait(VE, val=ve_s1(tt, 1))
            vector.tensor_scalar_mul(nmu, mus, -1.0 / 4096.0).then_inc(VE, 1)
            vector.wait(SE, val=S0 + 2 * tt + 2)
            vector.reciprocal(rstd, sqv).then_inc(VE, 1)
            vector.wait(VE, val=ve_s1(tt, 3))
            vector.tensor_mul(nmur, nmu, rstd).then_inc(VE, 1)
            vector.wait(VE, val=ve_s1(tt, 4))
            vector.tensor_scalar(out=lnscr, in0=xr[tt % 2], scalar1=rstd,
                                 scalar2=nmur, op0=ALU.mult, op1=ALU.add
                                 ).then_inc(VE, 1)
            for b in range(4):
                vector.wait(TE, val=E0 + 4 * tt + b + 1)
                vector.tensor_copy(
                    out=xlnT.rearrange("p (kk t) -> p kk t", t=1024)
                        [:, 8 * b:8 * b + 8, 128 * tt:128 * (tt + 1)],
                    in_=psb[b][:, :].bitcast(BF16).rearrange("p (kk t) -> p kk t", t=128),
                ).then_inc(VE, 1)
    assert cnt["VE"] == (V0 + 1 + 72), ("cnt VE", cnt["VE"], V0 + 1 + 72)

    @env.scalar
    def _(scalar):
        for tt in range(8):
            nmu = stats[:, 8 * tt + 1:8 * tt + 2]
            sqv = stats[:, 8 * tt + 3:8 * tt + 4]
            scalar.wait(VE, val=ve_s1(tt, 2))
            scalar.activation(out=lnscr, in_=xr[tt % 2], func=AF.Square, bias=nmu,
                              scale=1.0, accum_out=varsum[:, tt:tt + 1]
                              ).then_inc(SE, 1)
            scalar.wait(SE, val=S0 + 2 * tt + 1)
            scalar.activation(out=sqv, in_=varsum[:, tt:tt + 1], func=AF.Sqrt,
                              scale=1.0 / 4096.0, bias=epst).then_inc(SE, 1)
    assert cnt["SE"] == (S0 + 16), ("cnt SE", cnt["SE"], S0 + 16)

    @env.tensor
    def _(tensor):
        for tt in range(8):
            tensor.wait(VE, val=ve_s1(tt, 5))
            for b in range(4):
                if 4 * tt + b >= 4:
                    tpp, bp = divmod(4 * tt + b - 4, 4)
                    tensor.wait(VE, val=ve_s1(tpp, 6 + bp))
                pbf = psb[b][:, :].bitcast(BF16)
                for u in range(8):
                    kk = 8 * b + u
                    ins = tensor.transpose(pbf[:, 128 * u:128 * (u + 1)],
                                           lnscr[:, 128 * kk:128 * (kk + 1)], identt)
                ins.then_inc(TE, 1)
    assert cnt["TE"] == (E0 + 32), ("cnt TE", cnt["TE"], E0 + 32)

    ctx.pop()

    ctx.push()
    wbuf = [ctx.alloc(32 * 512, BF16), ctx.alloc(32 * 512, BF16)]
    E1 = cnt["TE"]; V1 = cnt["VE"]
    WB = [cnt["W0"], cnt["W1"]]

    slabs = ([("q", h) for h in range(8)] + [("k", h) for h in range(8)] +
             [("v", h) for h in range(8)])

    @env.gpsimd
    def _(gpsimd):
        gpsimd.wait(XA, val=cnt["XA"])
        gpsimd.wait(XB, val=cnt["XB"])
        gpsimd.wait(TE, val=E1)
        gpsimd.wait(VE, val=V1)
        gpsimd.wait(SE, val=cnt["SE"])
        for i, (kind, h) in enumerate(slabs):
            if i >= 2:
                gpsimd.wait(TE, val=E1 + 2 * (i - 1))
            w = {"q": dram["wq_r"], "k": dram["wk_r"], "v": dram["wv_r"]}[kind]
            ncols = 512 if kind == "q" else 128
            dst = wbuf[i % 2].rearrange("p (kk n) -> p kk n", n=512)[:, :, 0:ncols]
            gpsimd.dma_start(
                out=dst,
                in_=w[:, ncols * h:ncols * (h + 1)]
                    .rearrange("(kk p) n -> p kk n", p=128)).then_inc(WS[i % 2], 16)
    assert cnt["W0"] == (WB[0] + 16 * 12), ("cnt W0", cnt["W0"], WB[0] + 16 * 12)
    assert cnt["W1"] == (WB[1] + 16 * 12), ("cnt W1", cnt["W1"], WB[1] + 16 * 12)

    @env.tensor
    def _(tensor):
        tensor.wait(VE, val=V1)
        for i, (kind, h) in enumerate(slabs):
            tensor.wait(WS[i % 2], val=WB[i % 2] + 16 * (i // 2 + 1))
            for tb in range(2):
                u = 2 * i + tb
                if u >= 2:
                    tensor.wait(VE, val=V1 + u - 1)
                njs = 4 if kind == "q" else 1
                n = 0
                for j in range(njs):
                    for kk in range(32):
                        n += 1
                        ins = tensor.matmul(
                            psb[u % 2][:, :],
                            wbuf[i % 2].rearrange("p (kk n) -> p kk n", n=512)
                                [:, kk, 128 * j:128 * (j + 1)],
                            xlnT[:, 1024 * kk + 512 * tb:1024 * kk + 512 * (tb + 1)],
                            start=(n == 1), stop=(n == njs * 32))
                ins.then_inc(TE, 1)
    assert cnt["TE"] == (E1 + 48), ("cnt TE", cnt["TE"], E1 + 48)

    @env.vector
    def _(vector):
        for i, (kind, h) in enumerate(slabs):
            dstf = {"q": qT, "k": kT, "v": vT}[kind]
            biasf = {"q": bqt, "k": bks, "v": bvs}[kind]
            for tb in range(2):
                u = 2 * i + tb
                vector.wait(TE, val=E1 + u + 1)
                vector.tensor_scalar(
                    out=dstf[:, 1024 * h + 512 * tb:1024 * h + 512 * (tb + 1)],
                    in0=psb[u % 2][:, :], scalar1=biasf[:, h:h + 1],
                    scalar2=None, op0=ALU.add).then_inc(VE, 1)
    assert cnt["VE"] == (V1 + 48), ("cnt VE", cnt["VE"], V1 + 48)

    ctx.pop()
    ctx.pop()

    ctx.push()
    tokstage = [ctx.alloc(1024, BF16), ctx.alloc(1024, BF16)]
    E2 = cnt["TE"]; V2 = cnt["VE"]
    TB = [cnt["TS0"], cnt["TS1"]]

    @env.tensor
    def _(tensor):
        tensor.wait(VE, val=V2)
        for idx in range(16):
            src = qT if idx < 8 else kT
            tt = idx % 8
            bank = 4 + idx % 2
            if idx >= 2:
                tensor.wait(VE, val=V2 + idx - 1)
            pbf = psb[bank][:, :].bitcast(BF16)
            for pt in range(8):
                ins = tensor.transpose(
                    pbf[:, 128 * pt:128 * (pt + 1)],
                    src[:, 1024 * pt + 128 * tt:1024 * pt + 128 * (tt + 1)], identt)
            ins.then_inc(TE, 1)
    assert cnt["TE"] == (E2 + 16), ("cnt TE", cnt["TE"], E2 + 16)

    @env.vector
    def _(vector):
        for idx in range(16):
            vector.wait(TE, val=E2 + idx + 1)
            if idx >= 2:
                vector.wait(TSs[idx % 2], val=TB[idx % 2] + 16 * (idx // 2))
            vector.tensor_copy(out=tokstage[idx % 2],
                               in_=psb[4 + idx % 2][:, :].bitcast(BF16)).then_inc(VE, 1)
    assert cnt["VE"] == (V2 + 16), ("cnt VE", cnt["VE"], V2 + 16)

    @env.gpsimd
    def _(gpsimd):
        for idx in range(16):
            dstd = dram["qtok_d"] if idx < 8 else dram["ktok_d"]
            tt = idx % 8
            gpsimd.wait(VE, val=V2 + idx + 1)
            gpsimd.dma_start(out=dstd[128 * tt:128 * (tt + 1)],
                             in_=tokstage[idx % 2]).then_inc(TSs[idx % 2], 16)
    assert cnt["TS0"] == (TB[0] + 16 * 8), ("cnt TS0", cnt["TS0"], TB[0] + 16 * 8)
    assert cnt["TS1"] == (TB[1] + 16 * 8), ("cnt TS1", cnt["TS1"], TB[1] + 16 * 8)
    ctx.pop()

    ctx.push()
    bd3 = ctx.alloc(32 * 1024, BF16)
    attn = ctx.alloc(16 * 1024, BF16)
    attnT = ctx.alloc(16 * 1024, BF16)
    sums = ctx.alloc(128, F32)
    recip = ctx.alloc(128, F32)
    nmx = ctx.alloc(64, F32)

    E3 = cnt["TE"]; V3 = cnt["VE"]; S3c = cnt["SE"]; G3 = cnt["GD"]
    QB0 = [cnt["QA"], cnt["QB"]]
    TS_done = [cnt["TS0"], cnt["TS1"]]
    TE_CH = 32 + 16 + 32
    VE_CH = 146
    SE_CH = 32

    def te4(ch, k):
        return E3 + TE_CH * ch + k

    def ve4(ch, k):
        return V3 + VE_CH * ch + k

    @env.sync
    def _(sync):
        sync.wait(TS0, val=TS_done[0])
        sync.wait(TS1, val=TS_done[1])
        for ch in range(8):
            if ch >= 1:
                sync.wait(TE, val=te4(ch - 1, 32))
            for dstt, srcd in ((bd3[0:8, 0:16384], dram["qtok_d"]),
                               (bd3[0:8, 16384:32768], dram["ktok_d"])):
                sync.dma_start(
                    out=dstt.rearrange("p (t d) -> p t d", d=128),
                    in_=srcd[128 * ch:128 * (ch + 1)]
                        .rearrange("t (h d) -> h t d", d=128)
                    ).then_inc(QS[ch % 2], 16)
    assert cnt["QA"] == (QB0[0] + 16 * 8), ("cnt QA", cnt["QA"], QB0[0] + 16 * 8)
    assert cnt["QB"] == (QB0[1] + 16 * 8), ("cnt QB", cnt["QB"], QB0[1] + 16 * 8)

    @env.tensor
    def _(tensor):
        for ch in range(8):
            tensor.wait(QS[ch % 2], val=QB0[ch % 2] + 32 * (ch // 2 + 1))
            for g in range(32):
                nglob = 32 * ch + g
                if nglob >= 2:
                    tensor.wait(SE, val=S3c + nglob - 1)
                for tau in range(4):
                    t = 4 * g + tau
                    ins = tensor.matmul(
                        psb[g % 2][:, 128 * tau:128 * (tau + 1)],
                        bd3[0:8, 128 * t:128 * (t + 1)],
                        bd3[0:8, 16384 + 128 * t:16384 + 128 * (t + 1)],
                        start=True, stop=True)
                ins.then_inc(TE, 1)
            tensor.wait(VE, val=ve4(ch, 98))
            for bt in range(16):
                nglob = 16 * ch + bt
                if nglob >= 2:
                    chp, btp = divmod(nglob - 2, 16)
                    tensor.wait(VE, val=ve4(chp, 98 + btp + 1))
                pbf = psb[2 + bt % 2][:, :].bitcast(BF16)
                for u in range(8):
                    t = 8 * bt + u
                    ins = tensor.transpose(pbf[:, 128 * u:128 * (u + 1)],
                                           attn[:, 128 * t:128 * (t + 1)], identt)
                ins.then_inc(TE, 1)
            for g in range(32):
                nglob = 32 * ch + g
                if nglob >= 2:
                    chp, gp = divmod(nglob - 2, 32)
                    tensor.wait(VE, val=ve4(chp, 114 + gp + 1))
                if g == 0 and ch > 0:
                    tensor.wait(GD, val=G3 + 16 * ch)
                bt_need = (4 * g + 3) // 8 + 1
                tensor.wait(VE, val=ve4(ch, 98 + bt_need))
                for tau in range(4):
                    t = 4 * g + tau
                    glob_t = 128 * ch + t
                    v_t = vT.rearrange("p (h t) -> p h t", t=1024)[:, :, glob_t]
                    ins = tensor.matmul(
                        psb[4 + g % 2][0:8, 128 * tau:128 * (tau + 1)],
                        v_t, attnT[:, 128 * t:128 * (t + 1)],
                        start=True, stop=True)
                ins.then_inc(TE, 1)
    assert cnt["TE"] == (E3 + 8 * TE_CH), ("cnt TE", cnt["TE"], E3 + 8 * TE_CH)

    @env.vector
    def _(vector):
        for ch in range(8):
            for g in range(32):
                vector.wait(TE, val=te4(ch, g + 1))
                vector.tensor_reduce(out=nmx[:, 2 * g:2 * g + 1], in_=psb[g % 2][:, :],
                                     axis=AX, op=ALU.max, negate=True).then_inc(VE, 1)
                vector.wait(VE, val=ve4(ch, 3 * g + 1))
                vector.tensor_scalar_mul(nmx[:, 2 * g + 1:2 * g + 2],
                                         nmx[:, 2 * g:2 * g + 1], ISQ).then_inc(VE, 1)
                vector.wait(SE, val=S3c + 32 * ch + g + 1)
                vector.tensor_reduce(
                    out=sums.rearrange("p (g t o) -> p g t o", t=4, o=1)[:, g],
                    in_=attn.rearrange("p (g t e) -> p g t e", t=4, e=128)[:, g],
                    axis=AX, op=ALU.add).then_inc(VE, 1)
            vector.wait(VE, val=ve4(ch, 96))
            vector.reciprocal(recip, sums).then_inc(VE, 1)
            vector.wait(VE, val=ve4(ch, 97))
            vector.tensor_tensor(
                out=attn.rearrange("p (t e) -> p t e", e=128),
                in0=attn.rearrange("p (t e) -> p t e", e=128),
                in1=recip.rearrange("p (t o) -> p t o", o=1)
                    .broadcast_to([128, 128, 128]),
                op=ALU.mult).then_inc(VE, 1)
            for bt in range(16):
                vector.wait(TE, val=te4(ch, 32 + bt + 1))
                vector.tensor_copy(out=attnT[:, 1024 * bt:1024 * (bt + 1)],
                                   in_=psb[2 + bt % 2][:, :].bitcast(BF16)).then_inc(VE, 1)
            for g in range(32):
                if g == 0 and ch > 0:
                    vector.wait(GD, val=G3 + 16 * ch)
                vector.wait(TE, val=te4(ch, 48 + g + 1))
                vector.tensor_copy(out=bd3[64:72, 512 * g:512 * (g + 1)],
                                   in_=psb[4 + g % 2][0:8, :]).then_inc(VE, 1)
    assert cnt["VE"] == (V3 + 8 * VE_CH), ("cnt VE", cnt["VE"], V3 + 8 * VE_CH)

    @env.scalar
    def _(scalar):
        for ch in range(8):
            for g in range(32):
                scalar.wait(VE, val=ve4(ch, 3 * g + 2))
                scalar.activation(out=attn[:, 512 * g:512 * (g + 1)],
                                  in_=psb[g % 2][:, :], func=AF.Exp,
                                  scale=ISQ, bias=nmx[:, 2 * g + 1:2 * g + 2]
                                  ).then_inc(SE, 1)
    assert cnt["SE"] == (S3c + 8 * SE_CH), ("cnt SE", cnt["SE"], S3c + 8 * SE_CH)

    @env.gpsimd
    def _(gpsimd):
        for ch in range(8):
            gpsimd.wait(VE, val=ve4(ch, VE_CH))
            gpsimd.dma_start(
                out=dram["otok_d"][128 * ch:128 * (ch + 1)]
                    .rearrange("t (h d) -> h t d", d=128),
                in_=bd3[64:72, 0:16384].rearrange("p (t d) -> p t d", d=128)).then_inc(GD, 16)
    assert cnt["GD"] == (G3 + 16 * 8), ("cnt GD", cnt["GD"], G3 + 16 * 8)
    ctx.pop()

    ctx.push()
    xr2 = [ctx.alloc(4096, F32), ctx.alloc(4096, F32)]
    ot2 = [ctx.alloc(1024, BF16), ctx.alloc(1024, BF16)]
    xout = [ctx.alloc(4096, F32), ctx.alloc(4096, F32)]
    V5 = cnt["VE"]; G5 = cnt["GD"]
    RB = [cnt["R0"], cnt["R1"]]
    SOB = [cnt["SO0"], cnt["SO1"]]

    @env.sync
    def _(sync):
        sync.wait(GD, val=G5)
        for tt in range(8):
            if tt >= 2:
                sync.wait(VE, val=V5 + tt - 1)
            sync.dma_start(out=xr2[tt % 2],
                           in_=dram["x_rows"][128 * tt:128 * (tt + 1)]
                           ).then_inc(RS[tt % 2], 16)
            sync.dma_start(out=ot2[tt % 2],
                           in_=dram["otok_d"][128 * tt:128 * (tt + 1)]
                           ).then_inc(RS[tt % 2], 16)
    assert cnt["R0"] == (RB[0] + 16 * 8), ("cnt R0", cnt["R0"], RB[0] + 16 * 8)
    assert cnt["R1"] == (RB[1] + 16 * 8), ("cnt R1", cnt["R1"], RB[1] + 16 * 8)

    @env.vector
    def _(vector):
        for tt in range(8):
            vector.wait(RS[tt % 2], val=RB[tt % 2] + 32 * (tt // 2 + 1))
            if tt >= 2:
                vector.wait(SOS[tt % 2], val=SOB[tt % 2] + 16 * (tt // 2))
            vector.tensor_tensor(
                out=xout[tt % 2].rearrange("p (d h j) -> p d h j", h=8, j=4),
                in0=xr2[tt % 2].rearrange("p (d h j) -> p d h j", h=8, j=4),
                in1=ot2[tt % 2].rearrange("p (h d) -> p h d", d=128)
                    .transpose([0, 2, 1]).unsqueeze(3).broadcast_to([128, 128, 8, 4]),
                op=ALU.add).then_inc(VE, 1)
    assert cnt["VE"] == (V5 + 8), ("cnt VE", cnt["VE"], V5 + 8)

    @env.gpsimd
    def _(gpsimd):
        for tt in range(8):
            gpsimd.wait(VE, val=V5 + tt + 1)
            gpsimd.dma_start(out=dram["xres1"][128 * tt:128 * (tt + 1)],
                             in_=xout[tt % 2]).then_inc(SOS[tt % 2], 16)
    assert cnt["SO0"] == (SOB[0] + 16 * 4), ("cnt SO0", cnt["SO0"], SOB[0] + 16 * 4)
    assert cnt["SO1"] == (SOB[1] + 16 * 4), ("cnt SO1", cnt["SO1"], SOB[1] + 16 * 4)
    ctx.pop()
    ctx.pop()


def declare_ph0(nc, dram, standalone=True):
    dram["x_rows"] = nc.dram_tensor("x_rows", [C, D], F32, kind="ExternalInput").ap()
    if "xres1" in dram:
        _skip_xres1 = True
    dram["wq_r"] = nc.dram_tensor("wq_r", [D, D], F32, kind="ExternalInput").ap()
    dram["wk_r"] = nc.dram_tensor("wk_r", [D, 1024], F32, kind="ExternalInput").ap()
    dram["wv_r"] = nc.dram_tensor("wv_r", [D, 1024], F32, kind="ExternalInput").ap()
    dram["bq"] = nc.dram_tensor("bq", [128, 32], F32, kind="ExternalInput").ap()
    dram["bk"] = nc.dram_tensor("bk", [128, 8], F32, kind="ExternalInput").ap()
    dram["bv"] = nc.dram_tensor("bv", [128, 8], F32, kind="ExternalInput").ap()
    dram["ident"] = nc.dram_tensor("ident", [128, 128], BF16, kind="ExternalInput").ap()
    dram["epsin"] = nc.dram_tensor("epsin", [128, 1], F32, kind="ExternalInput").ap()
    dram["qtok_d"] = nc.dram_tensor("qtok_d", [C, 1024], BF16, kind="Internal").ap()
    dram["ktok_d"] = nc.dram_tensor("ktok_d", [C, 1024], BF16, kind="Internal").ap()
    dram["otok_d"] = nc.dram_tensor("otok_d", [C, 1024], BF16, kind="Internal").ap()
    if "xres1" not in dram:
        dram["xres1"] = nc.dram_tensor(
            "xres1", [C, D], F32,
            kind="ExternalOutput" if standalone else "Internal").ap()


def build_ph0():
    nc = bass.Bass("TRN2", target_bir_lowering=False, debug=False)
    dram = {}
    declare_ph0(nc, dram, standalone=True)
    es = contextlib.ExitStack()
    with es:
        ctx = Ctx(nc, es)
        ctx.sems(*PH1_SEMS)
        epst = ctx.alloc(1, F32)
        dram["epsc"] = epst
        with nc.Block() as block:
            @block.sync
            def _(sync):
                sync.dma_start(out=epst, in_=dram["epsin"]).then_inc(ctx.sem["CN"], 16)
            ctx.cnt["CN"] += 16
            run_twopass(ctx, block, lambda env: emit_ph0(env, dram))

            @block.gpsimd
            def _(gpsimd):
                gpsimd.wait_ge(ctx.sem["SO0"], ctx.cnt["SO0"])
                gpsimd.wait_ge(ctx.sem["SO1"], ctx.cnt["SO1"])
    return nc


class _MockInstr:
    def __init__(self, env):
        self.env = env

    def then_inc(self, semname, k=1):
        self.env.ctx.cnt[semname] += k
        return self


class _MockEngine:
    def __init__(self, env):
        self._env = env

    def wait(self, semname, key=None, val=None):
        pass

    def __getattr__(self, name):
        env = self._env
        def f(*a, **k):
            return _MockInstr(env)
        return f


class _RealInstr:
    def __init__(self, env, instr):
        self.env = env
        self.instr = instr

    def then_inc(self, semname, k=1):
        self.instr.then_inc(self.env.ctx.sem[semname], k)
        self.env.ctx.cnt[semname] += k
        return self


class _RealEngine:
    def __init__(self, env, eng):
        self._env = env
        self._eng = eng

    def wait(self, semname, key=None, val=None):
        if val is None:
            val = self._env.rec[key][semname]
        self._eng.wait_ge(self._env.ctx.sem[semname], val)

    def __getattr__(self, name):
        env = self._env
        m = getattr(self._eng, name)
        def f(*a, **k):
            return _RealInstr(env, m(*a, **k))
        return f


class Env:

    def __init__(self, ctx, block):
        self.ctx = ctx
        self.block = block
        self.rec = {}
        self.pass1 = True

    def mark(self, key):
        if self.pass1:
            assert key not in self.rec, f"dup mark {key}"
            self.rec[key] = dict(self.ctx.cnt)
        else:
            for s, v in self.rec[key].items():
                assert self.ctx.cnt[s] == v, \
                    f"pass2 divergence at {key}: {s} {self.ctx.cnt[s]} != {v}"

    def at(self, key, semname):
        return self.rec[key][semname]

    def _section(self, engname):
        env = self

        def deco(f):
            if env.pass1:
                f(_MockEngine(env))
            else:
                getattr(env.block, engname)(lambda eng: f(_RealEngine(env, eng)))
        return deco

    @property
    def sync(self):
        return self._section("sync")

    @property
    def vector(self):
        return self._section("vector")

    @property
    def scalar(self):
        return self._section("scalar")

    @property
    def tensor(self):
        return self._section("tensor")

    @property
    def gpsimd(self):
        return self._section("gpsimd")


def run_twopass(ctx, block, emit):
    env = Env(ctx, block)
    saved = dict(ctx.cnt)
    amark = ctx.aoff
    mstack = list(ctx.marks)
    emit(env)
    ctx.cnt.clear()
    ctx.cnt.update(saved)
    ctx.aoff = amark
    ctx.marks = mstack
    env.pass1 = False
    emit(env)
    return env



NSUB = 4
HSUB = 344
TBW = 512
NTB = 16
MT = [(0, 128), (128, 128), (256, 88)]


def emit_ph1(env, dram, fence=None):
    ctx = env.ctx
    psb = ctx.psb
    epst = dram["epsc"]

    ctx.push()
    identt = ctx.alloc(128, BF16)
    stats = ctx.alloc(64, F32)
    varsum = ctx.alloc(8, F32)
    xr = [ctx.alloc(4096, F32), ctx.alloc(4096, F32)]
    lnscr = ctx.alloc(4096, BF16)
    agst = [ctx.alloc(4096, BF16), ctx.alloc(4096, BF16)]

    @env.sync
    def _(sync):
        if fence:
            for s_, v_ in fence.items():
                sync.wait(s_, val=v_)
        sync.dma_start(out=identt, in_=dram["ident"]).then_inc("CN", 16)
        env.mark("p1.ident")
        for tt in range(8):
            if tt >= 2:
                sync.wait("VE", ("p1.norm", tt - 2))
                sync.wait("SE", ("p1.sq1", tt - 2))
            sync.dma_start(out=xr[tt % 2],
                           in_=dram["xres1"][128 * tt:128 * (tt + 1)]
                           ).then_inc("X" + "AB"[tt % 2], 16)
            env.mark(("p1.xload", tt))

    @env.vector
    def _(vector):
        vector.wait("CN", "p1.ident")
        vector.memzero(varsum)
        for tt in range(8):
            mus = stats[:, 8 * tt + 0:8 * tt + 1]
            nmu = stats[:, 8 * tt + 1:8 * tt + 2]
            rstd = stats[:, 8 * tt + 2:8 * tt + 3]
            sqv = stats[:, 8 * tt + 3:8 * tt + 4]
            nmur = stats[:, 8 * tt + 4:8 * tt + 5]
            vector.wait("X" + "AB"[tt % 2], ("p1.xload", tt))
            vector.reduce_sum(out=mus, in_=xr[tt % 2], axis=AX).then_inc("VE")
            env.mark(("p1.mus", tt))
            vector.wait("VE", ("p1.mus", tt))
            vector.tensor_scalar_mul(nmu, mus, -1.0 / 4096.0).then_inc("VE")
            env.mark(("p1.nmu", tt))
            vector.wait("SE", ("p1.sq2", tt))
            vector.reciprocal(rstd, sqv).then_inc("VE")
            env.mark(("p1.rstd", tt))
            vector.wait("VE", ("p1.rstd", tt))
            vector.tensor_mul(nmur, nmu, rstd).then_inc("VE")
            env.mark(("p1.nmur", tt))
            vector.wait("VE", ("p1.nmur", tt))
            vector.tensor_scalar(out=lnscr, in0=xr[tt % 2], scalar1=rstd,
                                 scalar2=nmur, op0=ALU.mult, op1=ALU.add
                                 ).then_inc("VE")
            env.mark(("p1.norm", tt))
            for b in range(4):
                vector.wait("TE", ("p1.tp", tt, b))
                if tt >= 2 and b == 0:
                    vector.wait("TS" + "01"[tt % 2], ("p1.agstore", tt - 2))
                vector.tensor_copy(
                    out=agst[tt % 2].rearrange("p (kk t) -> p kk t", t=128)
                        [:, 8 * b:8 * b + 8, :],
                    in_=psb[b][:, :].bitcast(BF16)
                        .rearrange("p (kk t) -> p kk t", t=128),
                ).then_inc("VE")
                env.mark(("p1.evac", tt, b))

    @env.scalar
    def _(scalar):
        for tt in range(8):
            nmu = stats[:, 8 * tt + 1:8 * tt + 2]
            sqv = stats[:, 8 * tt + 3:8 * tt + 4]
            scalar.wait("VE", ("p1.nmu", tt))
            scalar.activation(out=lnscr, in_=xr[tt % 2], func=AF.Square, bias=nmu,
                              scale=1.0, accum_out=varsum[:, tt:tt + 1]
                              ).then_inc("SE")
            env.mark(("p1.sq1", tt))
            scalar.wait("SE", ("p1.sq1", tt))
            scalar.activation(out=sqv, in_=varsum[:, tt:tt + 1], func=AF.Sqrt,
                              scale=1.0 / 4096.0, bias=epst).then_inc("SE")
            env.mark(("p1.sq2", tt))

    @env.tensor
    def _(tensor):
        for tt in range(8):
            tensor.wait("VE", ("p1.norm", tt))
            for b in range(4):
                if 4 * tt + b >= 4:
                    tpp, bp = divmod(4 * tt + b - 4, 4)
                    tensor.wait("VE", ("p1.evac", tpp, bp))
                pbf = psb[b][:, :].bitcast(BF16)
                for u in range(8):
                    kk = 8 * b + u
                    ins = tensor.transpose(pbf[:, 128 * u:128 * (u + 1)],
                                           lnscr[:, 128 * kk:128 * (kk + 1)],
                                           identt)
                ins.then_inc("TE")
                env.mark(("p1.tp", tt, b))

    @env.gpsimd
    def _(gpsimd):
        for tt in range(8):
            gpsimd.wait("VE", ("p1.evac", tt, 3))
            gpsimd.dma_start(
                out=dram["agin%d" % (tt // 2)]
                    .rearrange("(kk p) t -> p kk t", p=128)
                    [:, :, 128 * (tt % 2):128 * (tt % 2 + 1)],
                in_=agst[tt % 2].rearrange("p (kk t) -> p kk t", t=128),
            ).then_inc("TS" + "01"[tt % 2], 16)
            env.mark(("p1.agstore", tt))
            if tt % 2 == 1:
                gpsimd.wait("TS0", ("p1.agstore", tt - 1))
                gpsimd.wait("TS1", ("p1.agstore", tt))
                gpsimd.collective_compute(
                    "AllGather", ALU.bypass,
                    replica_groups=[list(range(NCORES))],
                    ins=[dram["agin%d" % (tt // 2)].opt()],
                    outs=[dram["xg%d" % (tt // 2)].opt()],
                ).then_inc("CC")
                env.mark(("p1.ag", tt // 2))
    ctx.pop()

    ctx.push()
    qwgu_sb = ctx.alloc(32 * 1376, U8)
    qzgu_sb = ctx.alloc(1376, U8)
    sgu_bf = ctx.alloc(2752, BF16)
    zscr = ctx.alloc(2752, U8)
    zscrb = ctx.alloc(2752, BF16)
    sel_sb = ctx.alloc(32 * 128, BF16)
    Wdeq = ctx.alloc(32 * 688, BF16)
    nib = ctx.alloc(688, U8)
    nibbf = ctx.alloc(688, BF16)
    centbf = ctx.alloc(688, BF16)
    xgT = [ctx.alloc(32 * TBW, BF16), ctx.alloc(32 * TBW, BF16)]
    hst = [ctx.alloc(3 * TBW, BF16), ctx.alloc(3 * TBW, BF16)]
    sil = [ctx.alloc(TBW, F32), ctx.alloc(TBW, F32)]

    def qwb(kk, lo, n):
        return qwgu_sb.rearrange("p (kk b) -> p kk b", b=1376)[:, kk, lo:lo + n]

    def xg_tile(u, kk):
        return xgT[u % 2].rearrange("p (kk t) -> p kk t", t=TBW)[:, kk, :]

    @env.sync
    def _(sync):
        sync.wait("TS0", ("p1.agstore", 6))
        sync.wait("TS1", ("p1.agstore", 7))
        sync.dma_start(out=qwgu_sb.rearrange("p (kk b) -> p kk b", b=1376),
                       in_=dram["qwgu_b"].rearrange("(kk p) b -> p kk b", p=128)
                       ).then_inc("CN", 16)
        sync.dma_start(out=qzgu_sb[0:32], in_=dram["qzgu_b"]).then_inc("CN", 16)
        sync.dma_start(out=sel_sb[0:32], in_=dram["sel"]).then_inc("CN", 16)
        env.mark("p1b.res")
        for s in range(NSUB):
            for tb in range(NTB):
                u = NTB * s + tb
                if u >= 2:
                    sync.wait("TE", ("p1b.mt", u - 2, 5))
                r, loc = tb % 8, tb // 8
                if s == 0 and tb % 8 == 0:
                    sync.wait("CC", ("p1.ag", 2 * loc + 1))
                for half in range(2):
                    for jj in range(2):
                        sync.dma_start(
                            out=xgT[u % 2].rearrange("p (kk t) -> p kk t", t=TBW)
                                [:, 16 * half:16 * (half + 1),
                                 256 * jj:256 * (jj + 1)],
                            in_=dram["xg%d" % (2 * loc + jj)]
                                [4096 * r + 2048 * half:
                                 4096 * r + 2048 * (half + 1), :]
                                .rearrange("(kk p) t -> p kk t", p=128),
                        ).then_inc("Q" + "AB"[u % 2], 16)
                env.mark(("p1b.xg", u))

    @env.gpsimd
    def _(gpsimd):
        gpsimd.dma_start(out=sgu_bf[0:32], in_=dram["sgu_c"]).then_inc("CN2", 16)
        env.mark("p1b.sgu")
        for s in range(NSUB):
            for tb in range(NTB):
                u = NTB * s + tb
                tpos = 2 * (tb % 8) + tb // 8
                gpsimd.wait("VE", ("p1b.h", u, 2))
                gpsimd.dma_start(
                    out=dram["h_d"][HSUB * s:HSUB * s + 256,
                                    TBW * tpos:TBW * (tpos + 1)]
                        .rearrange("(kk p) t -> p kk t", p=128),
                    in_=hst[u % 2][:, 0:2 * TBW]
                        .rearrange("p (kk t) -> p kk t", t=TBW),
                ).then_inc("SO" + "01"[u % 2], 16)
                gpsimd.dma_start(
                    out=dram["h_d"][HSUB * s + 256:HSUB * (s + 1),
                                    TBW * tpos:TBW * (tpos + 1)],
                    in_=hst[u % 2][0:88, 2 * TBW:3 * TBW],
                ).then_inc("SO" + "01"[u % 2], 16)
                env.mark(("p1b.hstore", u))

    @env.tensor
    def _(tensor):
        tensor.wait("CN", "p1b.res")
        tensor.wait("CN2", "p1b.sgu")
        tensor.wait("VE", "p1b.zs")
        for s in range(NSUB):
            for kk in range(32):
                if kk >= 1:
                    tensor.wait("VE", ("p1b.deq", s, kk - 1))
                if s >= 1 and kk == 0:
                    tensor.wait("VE", ("p1b.h", NTB * s - 1, 1))
                    tensor.wait("VE", ("p1b.h", NTB * s - 1, 2))
                tensor.matmul(psb[4][:, 0:344],
                              sel_sb[0:32, 128 * kk:128 * (kk + 1)],
                              zscrb[0:32, HSUB * s:HSUB * (s + 1)],
                              start=True, stop=True)
                tensor.matmul(psb[5][:, 0:344],
                              sel_sb[0:32, 128 * kk:128 * (kk + 1)],
                              zscrb[0:32, 1376 + HSUB * s:1376 + HSUB * (s + 1)],
                              start=True, stop=True)
                tensor.matmul(psb[6][:, 0:344],
                              sel_sb[0:32, 128 * kk:128 * (kk + 1)],
                              sgu_bf[0:32, HSUB * s:HSUB * (s + 1)],
                              start=True, stop=True)
                ins = tensor.matmul(psb[7][:, 0:344],
                                    sel_sb[0:32, 128 * kk:128 * (kk + 1)],
                                    sgu_bf[0:32,
                                           1376 + HSUB * s:1376 + HSUB * (s + 1)],
                                    start=True, stop=True)
                ins.then_inc("TE")
                env.mark(("p1b.bcast", s, kk))
            for tb in range(NTB):
                u = NTB * s + tb
                tensor.wait("Q" + "AB"[u % 2], ("p1b.xg", u))
                tensor.wait("VE", ("p1b.deq", s, 31))
                for mt in range(6):
                    gate = mt < 3
                    j = mt % 3
                    off, sz = MT[j]
                    bank = mt
                    if u >= 1:
                        if gate:
                            tensor.wait("SE", ("p1b.sil", u - 1, j))
                        else:
                            tensor.wait("VE", ("p1b.h", u - 1, j))
                    wcol = off if gate else 344 + off
                    for kk in range(32):
                        ins = tensor.matmul(
                            psb[bank][0:sz, 0:TBW],
                            Wdeq.rearrange("p (kk n) -> p kk n", n=688)
                                [:, kk, wcol:wcol + sz],
                            xg_tile(u, kk), start=(kk == 0), stop=(kk == 31))
                    ins.then_inc("TE")
                    env.mark(("p1b.mt", u, mt))
                env.mark(("p1b.zcmm", u))

    @env.vector
    def _(vector):
        vector.wait("CN", "p1b.res")
        vector.wait("CN2", "p1b.sgu")
        vector.tensor_scalar(out=zscr[0:32, 0:2752:2], in0=qzgu_sb[0:32],
                             scalar1=15, scalar2=None, op0=ALU.bitwise_and)
        vector.tensor_scalar(out=zscr[0:32, 1:2752:2], in0=qzgu_sb[0:32],
                             scalar1=4, scalar2=None,
                             op0=ALU.logical_shift_right).then_inc("VE")
        env.mark("p1b.zunpack")
        vector.wait("VE", "p1b.zunpack")
        vector.tensor_copy(out=zscrb[0:32], in_=zscr[0:32]).then_inc("VE")
        env.mark("p1b.zs")
        for s in range(NSUB):
            for kk in range(32):
                if s > 0 and kk == 0:
                    vector.wait("TE", ("p1b.zcmm", NTB * s - 1))
                vector.tensor_scalar(out=nib[:, 0:344:2],
                                     in0=qwb(kk, 172 * s, 172), scalar1=15,
                                     scalar2=None, op0=ALU.bitwise_and)
                vector.tensor_scalar(out=nib[:, 1:344:2],
                                     in0=qwb(kk, 172 * s, 172), scalar1=4,
                                     scalar2=None, op0=ALU.logical_shift_right)
                vector.tensor_scalar(out=nib[:, 344:688:2],
                                     in0=qwb(kk, 688 + 172 * s, 172), scalar1=15,
                                     scalar2=None, op0=ALU.bitwise_and)
                vector.tensor_scalar(out=nib[:, 345:688:2],
                                     in0=qwb(kk, 688 + 172 * s, 172), scalar1=4,
                                     scalar2=None, op0=ALU.logical_shift_right
                                     ).then_inc("VE")
                env.mark(("p1b.unp", s, kk))
                vector.wait("VE", ("p1b.unp", s, kk))
                vector.tensor_copy(out=nibbf, in_=nib).then_inc("VE")
                env.mark(("p1b.cast", s, kk))
                vector.wait("VE", ("p1b.cast", s, kk))
                vector.wait("TE", ("p1b.bcast", s, kk))
                vector.tensor_sub(centbf[:, 0:344], nibbf[:, 0:344],
                                  psb[4][:, 0:344])
                vector.tensor_sub(centbf[:, 344:688], nibbf[:, 344:688],
                                  psb[5][:, 0:344]).then_inc("VE")
                env.mark(("p1b.cent", s, kk))
                vector.wait("VE", ("p1b.cent", s, kk))
                vector.tensor_mul(
                    Wdeq.rearrange("p (kk n) -> p kk n", n=688)[:, kk, 0:344],
                    centbf[:, 0:344], psb[6][:, 0:344])
                vector.tensor_mul(
                    Wdeq.rearrange("p (kk n) -> p kk n", n=688)[:, kk, 344:688],
                    centbf[:, 344:688], psb[7][:, 0:344]).then_inc("VE")
                env.mark(("p1b.deq", s, kk))
            for tb in range(NTB):
                u = NTB * s + tb
                for j in range(3):
                    vector.wait("SE", ("p1b.sil", u, j))
                    vector.wait("TE", ("p1b.mt", u, 3 + j))
                    if u >= 2 and j == 0:
                        vector.wait("SO" + "01"[u % 2], ("p1b.hstore", u - 2))
                    vector.tensor_mul(
                        hst[u % 2][:, TBW * j:TBW * (j + 1)],
                        sil[j % 2][:, 0:TBW],
                        psb[3 + j][:, 0:TBW]).then_inc("VE")
                    env.mark(("p1b.h", u, j))

    @env.scalar
    def _(scalar):
        for s in range(NSUB):
            for tb in range(NTB):
                u = NTB * s + tb
                for j in range(3):
                    scalar.wait("TE", ("p1b.mt", u, j))
                    pu, pj = (u, 0) if j == 2 else (u - 1, 2 if j == 0 else 1)
                    if pu >= 0:
                        scalar.wait("VE", ("p1b.h", pu, pj))
                    scalar.activation(out=sil[j % 2][:, 0:TBW],
                                      in_=psb[j][:, 0:TBW], func=AF.Silu
                                      ).then_inc("SE")
                    env.mark(("p1b.sil", u, j))
    ctx.pop()


def emit_ph1_down(env, dram):
    ctx = env.ctx
    psb = ctx.psb

    ctx.push()
    qwdn_sb = ctx.alloc(11 * 2048, U8)
    qzdn_sb = ctx.alloc(2048, U8)
    sdn_bf = ctx.alloc(4096, BF16)
    zdnscr = ctx.alloc(4096, U8)
    zdnscrb = ctx.alloc(4096, BF16)
    eid_sb = ctx.alloc(11 * 128, BF16)
    Wdnq = ctx.alloc(11 * 2048, BF16)
    dnib = ctx.alloc(2048, U8)
    dnibbf = ctx.alloc(2048, BF16)
    dcent = ctx.alloc(2048, BF16)
    h_ts = [ctx.alloc(11 * 128, BF16), ctx.alloc(11 * 128, BF16)]
    dstage = [ctx.alloc(2048, F32), ctx.alloc(2048, F32)]

    def wdn_bytes(kk, hh):
        return qwdn_sb.rearrange("p (kk b) -> p kk b", b=2048)[:, kk,
                                                              1024 * hh:1024 * (hh + 1)]

    @env.sync
    def _(sync):
        sync.wait("SO0", ("p1b.hstore", 62))
        sync.wait("SO1", ("p1b.hstore", 63))
        sync.wait("TE", ("p1b.zcmm", 63))
        def st_w(sync, w):
            hh_, ts_ = divmod(w, 64)
            sync.wait("SE", ("p1c.ev", w, 3))
            sync.dma_start(
                out=dram["rsin%d" % hh_][128 * ts_:128 * (ts_ + 1), :],
                in_=dstage[w % 2]).then_inc("TS" + "01"[w % 2], 16)
            env.mark(("p1c.st", w))
        sync.dma_start(out=qwdn_sb.rearrange("p (kk b) -> p kk b", b=2048)[:, 0:10, :],
                       in_=dram["qwdn_b"][0:1280]
                           .rearrange("(kk p) b -> p kk b", p=128)).then_inc("CN", 16)
        sync.dma_start(out=qwdn_sb[0:96].rearrange("p (kk b) -> p kk b", b=2048)
                       [:, 10:11, :],
                       in_=dram["qwdn_b"][1280:1376].unsqueeze(1)
                       ).then_inc("CN", 16)
        sync.dma_start(out=qzdn_sb[0:12], in_=dram["qzdn_b"]).then_inc("CN", 16)
        sync.dma_start(out=eid_sb[0:12].rearrange("p (kk t) -> p kk t", t=128),
                       in_=dram["Eid"].rearrange("kk g t -> g kk t")
                       ).then_inc("CN", 16)
        env.mark("p1c.res")
        for hh in range(2):
            for ts in range(64):
                w = 64 * hh + ts
                if w >= 2:
                    sync.wait("TE", ("p1c.dn", w - 2, 3))
                sync.dma_start(
                    out=h_ts[w % 2].rearrange("p (kk t) -> p kk t", t=128)[:, 0:10, :],
                    in_=dram["h_d"][0:1280, 128 * ts:128 * (ts + 1)]
                        .rearrange("(kk p) t -> p kk t", p=128)
                ).then_inc("R" + "01"[w % 2], 16)
                sync.dma_start(
                    out=h_ts[w % 2][0:96, 10 * 128:11 * 128],
                    in_=dram["h_d"][1280:1376, 128 * ts:128 * (ts + 1)]
                ).then_inc("R" + "01"[w % 2], 16)
                env.mark(("p1c.h", w))
                if w >= 2:
                    st_w(sync, w - 2)
        st_w(sync, 126)
        st_w(sync, 127)

    @env.gpsimd
    def _(gpsimd):
        gpsimd.dma_start(out=sdn_bf[0:12], in_=dram["sdn_c"]).then_inc("CN2", 16)
        env.mark("p1c.sdn")
        for hh in range(2):
            gpsimd.wait("TS0", ("p1c.st", 64 * hh + 62))
            gpsimd.wait("TS1", ("p1c.st", 64 * hh + 63))
            gpsimd.collective_compute(
                "ReduceScatter", ALU.add, replica_groups=[list(range(NCORES))],
                ins=[dram["rsin%d" % hh].opt()],
                outs=[dram["rsout%d" % hh].opt()],
            ).then_inc("CC")
            env.mark(("p1c.rs", hh))

    @env.vector
    def _(vector):
        vector.wait("CN", "p1c.res")
        vector.wait("CN2", "p1c.sdn")
        vector.tensor_scalar(out=zdnscr[0:12, 0:4096:2], in0=qzdn_sb[0:12],
                             scalar1=15, scalar2=None, op0=ALU.bitwise_and)
        vector.tensor_scalar(out=zdnscr[0:12, 1:4096:2], in0=qzdn_sb[0:12],
                             scalar1=4, scalar2=None,
                             op0=ALU.logical_shift_right).then_inc("VE")
        env.mark("p1c.zunp")
        vector.wait("VE", "p1c.zunp")
        vector.tensor_copy(out=zdnscrb[0:12], in_=zdnscr[0:12]).then_inc("VE")
        env.mark("p1c.zs")
        for hh in range(2):
            for kk in range(11):
                np_ = 128 if kk < 10 else 96
                if hh > 0 and kk == 0:
                    vector.wait("TE", ("p1c.dn", 63, 3))
                vector.tensor_scalar(out=dnib[0:np_, 0:2048:2],
                                     in0=wdn_bytes(kk, hh)[0:np_], scalar1=15,
                                     scalar2=None, op0=ALU.bitwise_and)
                vector.tensor_scalar(out=dnib[0:np_, 1:2048:2],
                                     in0=wdn_bytes(kk, hh)[0:np_], scalar1=4,
                                     scalar2=None, op0=ALU.logical_shift_right
                                     ).then_inc("VE")
                env.mark(("p1c.unp", hh, kk))
                vector.wait("VE", ("p1c.unp", hh, kk))
                vector.tensor_copy(out=dnibbf[0:np_], in_=dnib[0:np_]).then_inc("VE")
                env.mark(("p1c.cast", hh, kk))
                vector.wait("VE", ("p1c.cast", hh, kk))
                for m in range(4):
                    q = 4 * kk + m
                    vector.wait("TE", ("p1c.bc", hh, kk, m))
                    vector.tensor_sub(
                        dcent[0:np_, 512 * m:512 * (m + 1)],
                        dnibbf[0:np_, 512 * m:512 * (m + 1)],
                        psb[4 + q % 2][0:np_, :]).then_inc("VE")
                    env.mark(("p1c.cent", hh, kk, m))
                    vector.wait("VE", ("p1c.cent", hh, kk, m))
                    vector.tensor_mul(
                        Wdnq.rearrange("p (kk n) -> p kk n", n=2048)
                            [0:np_, kk, 512 * m:512 * (m + 1)],
                        dcent[0:np_, 512 * m:512 * (m + 1)],
                        psb[6 + q % 2][0:np_, :]
                    ).then_inc("VE")
                    env.mark(("p1c.deq", hh, kk, m))


    @env.tensor
    def _(tensor):
        tensor.wait("CN", "p1c.res")
        tensor.wait("VE", "p1c.zs")
        for hh in range(2):
            for kk in range(11):
                np_ = 128 if kk < 10 else 96
                for m in range(4):
                    q = 4 * kk + m
                    if q >= 2:
                        tensor.wait("VE", ("p1c.deq", hh, (q - 2) // 4, (q - 2) % 4))
                    if hh > 0 and kk == 0 and m < 2:
                        tensor.wait("VE", ("p1c.deq", 0, 10, 2 + m))
                    tensor.matmul(
                        psb[4 + q % 2][0:np_, :],
                        eid_sb[0:12, 128 * kk:128 * kk + np_],
                        zdnscrb[0:12, 2048 * hh + 512 * m:2048 * hh + 512 * (m + 1)],
                        start=True, stop=True)
                    ins = tensor.matmul(
                        psb[6 + q % 2][0:np_, :],
                        eid_sb[0:12, 128 * kk:128 * kk + np_],
                        sdn_bf[0:12, 2048 * hh + 512 * m:2048 * hh + 512 * (m + 1)],
                        start=True, stop=True)
                    ins.then_inc("TE")
                    env.mark(("p1c.bc", hh, kk, m))
            tensor.wait("VE", ("p1c.deq", hh, 10, 3))
            for ts in range(64):
                w = 64 * hh + ts
                tensor.wait("R" + "01"[w % 2], ("p1c.h", w))
                for nt in range(4):
                    if w >= 1:
                        tensor.wait("SE", ("p1c.ev", w - 1, nt))
                    for kk in range(11):
                        np_ = 128 if kk < 10 else 96
                        ins = tensor.matmul(
                            psb[nt][:, :],
                            h_ts[w % 2].rearrange("p (kk t) -> p kk t", t=128)
                                [0:np_, kk, :],
                            Wdnq.rearrange("p (kk n) -> p kk n", n=2048)
                                [0:np_, kk, 512 * nt:512 * (nt + 1)],
                            start=(kk == 0), stop=(kk == 10))
                    ins.then_inc("TE")
                    env.mark(("p1c.dn", w, nt))

    @env.scalar
    def _(scalar):
        for hh in range(2):
            for ts in range(64):
                w = 64 * hh + ts
                for nt in range(4):
                    scalar.wait("TE", ("p1c.dn", w, nt))
                    if w >= 2 and nt == 0:
                        scalar.wait("TS" + "01"[w % 2], ("p1c.st", w - 2))
                    scalar.copy(out=dstage[w % 2][:, 512 * nt:512 * (nt + 1)],
                                in_=psb[nt][:, :]).then_inc("SE")
                    env.mark(("p1c.ev", w, nt))
    ctx.pop()

    ctx.push()
    xr2 = [ctx.alloc(4096, F32), ctx.alloc(4096, F32)]
    rs2 = [ctx.alloc(4096, F32), ctx.alloc(4096, F32)]
    xout = [ctx.alloc(4096, F32), ctx.alloc(4096, F32)]

    @env.sync
    def _(sync):
        sync.wait("CC", ("p1c.rs", 1))
        for tt in range(8):
            if tt >= 2:
                sync.wait("VE", ("p1d.add", tt - 2))
            sync.dma_start(out=xr2[tt % 2],
                           in_=dram["xres1"][128 * tt:128 * (tt + 1)]
                           ).then_inc("R" + "01"[tt % 2], 16)
            for hh in range(2):
                sync.dma_start(out=rs2[tt % 2][:, 2048 * hh:2048 * (hh + 1)],
                               in_=dram["rsout%d" % hh]
                                   [128 * tt:128 * (tt + 1)]
                               ).then_inc("R" + "01"[tt % 2], 16)
            env.mark(("p1d.load", tt))

    @env.vector
    def _(vector):
        for tt in range(8):
            vector.wait("R" + "01"[tt % 2], ("p1d.load", tt))
            if tt >= 2:
                vector.wait("SO" + "01"[tt % 2], ("p1d.store", tt - 2))
            vector.tensor_add(xout[tt % 2], xr2[tt % 2], rs2[tt % 2]).then_inc("VE")
            env.mark(("p1d.add", tt))

    @env.gpsimd
    def _(gpsimd):
        for tt in range(8):
            gpsimd.wait("VE", ("p1d.add", tt))
            gpsimd.dma_start(out=dram["xres2"][128 * tt:128 * (tt + 1)],
                             in_=xout[tt % 2]).then_inc("SO" + "01"[tt % 2], 16)
            env.mark(("p1d.store", tt))
    ctx.pop()


def declare_ph1(nc, dram, standalone=True):
    if "xres1" not in dram:
        dram["xres1"] = nc.dram_tensor(
            "xres1", [C, D], F32,
            kind="ExternalInput" if standalone else "Internal").ap()
    if "xres2" in dram:
        pass
    for nm, shp, dt in [
        ("qwgu_b", [D, 1376], U8), ("qzgu_b", [32, 1376], U8),
        ("sgu_c", [32, 2752], F32), ("qwdn_b", [1376, 2048], U8),
        ("qzdn_b", [12, 2048], U8), ("sdn_c", [12, 4096], F32),
        ("Ei", [11 * 128, 16], BF16), ("Eid", [11, 12, 128], BF16),
        ("Gc", [D, 32], BF16), ("sel", [32, 32 * 128], BF16),
    ]:
        dram[nm] = nc.dram_tensor(nm, shp, dt, kind="ExternalInput").ap()
    if "ident" not in dram:
        dram["ident"] = nc.dram_tensor("ident", [128, 128], BF16,
                                       kind="ExternalInput").ap()
    if "epsin" not in dram:
        dram["epsin"] = nc.dram_tensor("epsin", [128, 1], F32,
                                       kind="ExternalInput").ap()
    for j in range(4):
        dram["agin%d" % j] = nc.dram_tensor("agin%d" % j, [D, C // 4], BF16,
                                            kind="Internal").ap()
        dram["xg%d" % j] = nc.dram_tensor("xg%d" % j, [NCORES * D, C // 4],
                                          BF16, kind="Internal").ap()
    dram["xs_d"] = nc.dram_tensor("xs_d", [32, T], BF16, kind="Internal").ap()
    dram["h_d"] = nc.dram_tensor("h_d", [1376, T], BF16, kind="Internal").ap()
    for hh in range(2):
        dram["rsin%d" % hh] = nc.dram_tensor("rsin%d" % hh, [T, D // 2], F32,
                                             kind="Internal").ap()
        dram["rsout%d" % hh] = nc.dram_tensor("rsout%d" % hh, [C, D // 2], F32,
                                              kind="Internal").ap()
    if "xres2" not in dram:
        dram["xres2"] = nc.dram_tensor(
            "xres2", [C, D], F32,
            kind="ExternalOutput" if standalone else "Internal").ap()


PH1_SEMS = ("CN", "CN2", "XA", "XB", "W0", "W1", "TS0", "TS1", "QA", "QB",
            "R0", "R1", "SO0", "SO1", "GD", "TE", "VE", "SE", "XSA", "XSB",
            "CC", "LD", "ST")


def build_ph1():
    nc = bass.Bass("TRN2", target_bir_lowering=False, debug=False)
    dram = {}
    declare_ph1(nc, dram, standalone=True)
    es = contextlib.ExitStack()
    with es:
        ctx = Ctx(nc, es)
        ctx.sems(*PH1_SEMS)
        epst = ctx.alloc(1, F32)
        dram["epsc"] = epst
        with nc.Block() as block:
            @block.sync
            def _(sync):
                sync.dma_start(out=epst, in_=dram["epsin"]).then_inc(ctx.sem["CN"], 16)
            ctx.cnt["CN"] += 16

            def emit(env):
                emit_ph1(env, dram)
                emit_ph1_down(env, dram)
            run_twopass(ctx, block, emit)

            @block.gpsimd
            def _(gpsimd):
                gpsimd.wait_ge(ctx.sem["SO0"], ctx.cnt["SO0"])
                gpsimd.wait_ge(ctx.sem["SO1"], ctx.cnt["SO1"])
    return nc



def emit_ph2(env, dram, fence=None):
    ctx = env.ctx
    psb = ctx.psb
    epst = dram["epsc"]

    ctx.push()
    identt = ctx.alloc(128, BF16)
    identf = ctx.alloc(128, F32)
    stats = ctx.alloc(64, F32)
    varsum = ctx.alloc(8, F32)
    ctx.push()
    xlnT = ctx.alloc(32 * 1024, BF16)
    ctx.push()
    xr = [ctx.alloc(4096, F32), ctx.alloc(4096, F32)]
    lnscr = ctx.alloc(4096, BF16)

    @env.sync
    def _(sync):
        if fence:
            for s_, v_ in fence.items():
                sync.wait(s_, val=v_)
        sync.dma_start(out=identt, in_=dram["ident"]).then_inc("CN", 16)
        sync.dma_start(out=identf, in_=dram["identf"]).then_inc("CN", 16)
        env.mark("p2.ident")
        for tt in range(8):
            if tt >= 2:
                sync.wait("VE", ("p2.norm", tt - 2))
                sync.wait("SE", ("p2.sq1", tt - 2))
            sync.dma_start(out=xr[tt % 2],
                           in_=dram["xres2"][128 * tt:128 * (tt + 1)]
                           ).then_inc("X" + "AB"[tt % 2], 16)
            env.mark(("p2.xload", tt))

    @env.vector
    def _(vector):
        vector.wait("CN", "p2.ident")
        vector.memzero(varsum)
        for tt in range(8):
            mus = stats[:, 8 * tt + 0:8 * tt + 1]
            nmu = stats[:, 8 * tt + 1:8 * tt + 2]
            rstd = stats[:, 8 * tt + 2:8 * tt + 3]
            sqv = stats[:, 8 * tt + 3:8 * tt + 4]
            nmur = stats[:, 8 * tt + 4:8 * tt + 5]
            vector.wait("X" + "AB"[tt % 2], ("p2.xload", tt))
            vector.reduce_sum(out=mus, in_=xr[tt % 2], axis=AX).then_inc("VE")
            env.mark(("p2.mus", tt))
            vector.wait("VE", ("p2.mus", tt))
            vector.tensor_scalar_mul(nmu, mus, -1.0 / 4096.0).then_inc("VE")
            env.mark(("p2.nmu", tt))
            vector.wait("SE", ("p2.sq2", tt))
            vector.reciprocal(rstd, sqv).then_inc("VE")
            env.mark(("p2.rstd", tt))
            vector.wait("VE", ("p2.rstd", tt))
            vector.tensor_mul(nmur, nmu, rstd).then_inc("VE")
            env.mark(("p2.nmur", tt))
            vector.wait("VE", ("p2.nmur", tt))
            vector.tensor_scalar(out=lnscr, in0=xr[tt % 2], scalar1=rstd,
                                 scalar2=nmur, op0=ALU.mult, op1=ALU.add
                                 ).then_inc("VE")
            env.mark(("p2.norm", tt))
            for b in range(4):
                vector.wait("TE", ("p2.tp", tt, b))
                vector.tensor_copy(
                    out=xlnT.rearrange("p (kk t) -> p kk t", t=1024)
                        [:, 8 * b:8 * b + 8, 128 * tt:128 * (tt + 1)],
                    in_=psb[b][:, :].bitcast(BF16)
                        .rearrange("p (kk t) -> p kk t", t=128),
                ).then_inc("VE")
                env.mark(("p2.evac", tt, b))

    @env.scalar
    def _(scalar):
        for tt in range(8):
            nmu = stats[:, 8 * tt + 1:8 * tt + 2]
            sqv = stats[:, 8 * tt + 3:8 * tt + 4]
            scalar.wait("VE", ("p2.nmu", tt))
            scalar.activation(out=lnscr, in_=xr[tt % 2], func=AF.Square, bias=nmu,
                              scale=1.0, accum_out=varsum[:, tt:tt + 1]
                              ).then_inc("SE")
            env.mark(("p2.sq1", tt))
            scalar.wait("SE", ("p2.sq1", tt))
            scalar.activation(out=sqv, in_=varsum[:, tt:tt + 1], func=AF.Sqrt,
                              scale=1.0 / 4096.0, bias=epst).then_inc("SE")
            env.mark(("p2.sq2", tt))

    @env.tensor
    def _(tensor):
        for tt in range(8):
            tensor.wait("VE", ("p2.norm", tt))
            for b in range(4):
                if 4 * tt + b >= 4:
                    tpp, bp = divmod(4 * tt + b - 4, 4)
                    tensor.wait("VE", ("p2.evac", tpp, bp))
                pbf = psb[b][:, :].bitcast(BF16)
                for u in range(8):
                    kk = 8 * b + u
                    ins = tensor.transpose(pbf[:, 128 * u:128 * (u + 1)],
                                           lnscr[:, 128 * kk:128 * (kk + 1)],
                                           identt)
                ins.then_inc("TE")
                env.mark(("p2.tp", tt, b))
    @env.gpsimd
    def _(gpsimd):
        if "dbg1" in dram:
            gpsimd.wait("VE", ("p2.evac", 7, 3))
            gpsimd.dma_start(out=dram["dbg1"],
                             in_=xlnT.rearrange("p (kk t) -> p kk t", t=1024)
                             ).then_inc("GD", 16)
            env.mark("p2.dbg1")

    ctx.pop()

    ctx.push()
    wbuf = [ctx.alloc(32 * 512, BF16), ctx.alloc(32 * 512, BF16)]
    ptile = ctx.alloc(512, BF16)
    rptile = ctx.alloc(512, BF16)
    rs1 = ctx.alloc(512, BF16)
    rs2 = ctx.alloc(512, BF16)
    cosP = ctx.alloc(8 * 64, BF16)
    sinP = ctx.alloc(8 * 64, BF16)
    qstg = [ctx.alloc(512, BF16), ctx.alloc(512, BF16)]
    kstg = [ctx.alloc(512, BF16), ctx.alloc(512, BF16)]

    SG = ([("q", g) for g in range(8)] + [("k", g) for g in range(2)] +
          [("v", g) for g in range(2)])

    def h4(tile):
        return tile.rearrange("p (h d) -> p h d", d=128)

    @env.sync
    def _(sync):
        sync.wait("VE", ("p2.evac", 7, 3))
        sync.wait("SE", ("p2.sq2", 7))
        sync.wait("TE", ("p2.tp", 7, 3))
        sync.dma_start(out=cosP, in_=dram["cosP_c"]).then_inc("CN", 16)
        sync.dma_start(out=sinP, in_=dram["sinP_c"]).then_inc("CN", 16)
        env.mark("p2b.cs")

    @env.gpsimd
    def _(gpsimd):
        def load_slab(i):
            kind, g = SG[i]
            if i >= 2:
                gpsimd.wait("TE", ("p2b.mm", i - 2, 7))
            w = {"q": dram["l1wq"], "k": dram["l1wk"], "v": dram["l1wv"]}[kind]
            gpsimd.dma_start(
                out=wbuf[i % 2].rearrange("p (kk n) -> p kk n", n=512),
                in_=w[:, 512 * g:512 * (g + 1)]
                    .rearrange("(kk p) n -> p kk n", p=128)).then_inc(
                        "W" + "01"[i % 2], 16)
            env.mark(("p2b.wl", i))
        gpsimd.wait("VE", ("p2.evac", 7, 3))
        gpsimd.wait("SE", ("p2.sq2", 7))
        gpsimd.wait("TE", ("p2.tp", 7, 3))
        load_slab(0)
        load_slab(1)
        for i, (kind, g) in enumerate(SG):
            for tm in range(8):
                u = 8 * i + tm
                if kind == "q":
                    gpsimd.wait("VE", ("p2b.qevac", i, tm))
                    gpsimd.dma_start(
                        out=dram["qT_d"]
                            .rearrange("(h d) t -> d h t", d=128)
                            [:, 4 * g:4 * (g + 1), 128 * tm:128 * (tm + 1)],
                        in_=qstg[u % 2].rearrange("p (h t) -> p h t", t=128),
                    ).then_inc("SO" + "01"[u % 2], 16)
                    env.mark(("p2b.qst", i, tm))
                else:
                    gpsimd.wait("VE", ("p2b.kevac", i, tm))
                    col = 512 * g if kind == "k" else 1024 + 512 * g
                    gpsimd.dma_start(
                        out=dram["agin2_%d" % tm][0:128, col:col + 512],
                        in_=kstg[u % 2],
                    ).then_inc("SO" + "01"[u % 2], 16)
                    env.mark(("p2b.kst", i, tm))
            if i + 2 < len(SG):
                load_slab(i + 2)
        if "agdump_0" in dram:
            gpsimd.wait("SO0", ("p2b.kst", 11, 6))
            gpsimd.wait("SO1", ("p2b.kst", 11, 7))
            for b in range(8):
                gpsimd.dma_start(out=dram["agdump_%d" % b],
                                 in_=dram["agin2_%d" % b]).then_inc("GD", 16)
                env.mark(("p2b.agd", b))
            gpsimd.wait("GD", ("p2b.agd", 7))
        for b in range(8):
            gpsimd.wait("SO0", ("p2b.kst", 11, b))
            gpsimd.wait("SO1", ("p2b.kst", 11, b))
            gpsimd.collective_compute(
                "AllGather", ALU.bypass, replica_groups=[list(range(NCORES))],
                ins=[dram["agin2_%d" % b].opt()],
                outs=[dram["kvb_%d" % b].opt()],
            ).then_inc("CC")
            env.mark(("p2b.ag", b))

    @env.tensor
    def _(tensor):
        tensor.wait("VE", ("p2.evac", 7, 3))
        tensor.wait("CN", "p2b.cs")
        for i, (kind, g) in enumerate(SG):
            tensor.wait("W" + "01"[i % 2], ("p2b.wl", i))
            for tm in range(8):
                u = 8 * i + tm
                bank = u % 2
                if u >= 2:
                    iu, tu = divmod(u - 2, 8)
                    tensor.wait("VE", ("p2b.pevac", iu, tu))
                for kk in range(32):
                    ins = tensor.matmul(
                        psb[bank][:, :],
                        xlnT[:, 1024 * kk + 128 * tm:1024 * kk + 128 * (tm + 1)],
                        wbuf[i % 2].rearrange("p (kk n) -> p kk n", n=512)[:, kk, :],
                        start=(kk == 0), stop=(kk == 31))
                ins.then_inc("TE")
                env.mark(("p2b.mm", i, tm))
                if kind == "q":
                    tensor.wait("VE", ("p2b.rope", i, tm))
                    if u >= 2:
                        iu, tu = divmod(u - 2, 8)
                        tensor.wait("VE", ("p2b.qevac", iu, tu))
                    pbf = psb[4 + u % 2][:, :].bitcast(BF16)
                    for j in range(4):
                        ins = tensor.transpose(
                            pbf[:, 128 * j:128 * (j + 1)],
                            h4(rptile)[:, j, :], identt)
                    ins.then_inc("TE")
                    env.mark(("p2b.qtp", i, tm))

    @env.vector
    def _(vector):
        vector.wait("CN", "p2b.cs")
        for i, (kind, g) in enumerate(SG):
            for tm in range(8):
                u = 8 * i + tm
                bank = u % 2
                vector.wait("TE", ("p2b.mm", i, tm))
                if u >= 1:
                    vector.wait("VE", ("p2b.ropedone", *divmod(u - 1, 8)))
                vector.tensor_copy(out=ptile, in_=psb[bank][:, :]).then_inc("VE")
                env.mark(("p2b.pevac", i, tm))
                ct = cosP.rearrange("p (tm d) -> p tm d", d=64)[:, tm, :]
                st = sinP.rearrange("p (tm d) -> p tm d", d=64)[:, tm, :]
                ctb = ct.unsqueeze(1).broadcast_to([128, 4, 64])
                stb = st.unsqueeze(1).broadcast_to([128, 4, 64])
                x1 = h4(ptile)[:, :, 0:64]
                x2 = h4(ptile)[:, :, 64:128]
                if kind != "v":
                    vector.wait("VE", ("p2b.pevac", i, tm))
                    if kind == "q" and u >= 1:
                        vector.wait("TE", ("p2b.qtp", *divmod(u - 1, 8)))
                    vector.tensor_mul(h4(rs1)[:, :, 0:64], x1, ctb)
                    vector.tensor_mul(h4(rs1)[:, :, 64:128], x2, stb)
                    vector.tensor_mul(h4(rs2)[:, :, 0:64], x1, stb)
                    vector.tensor_mul(h4(rs2)[:, :, 64:128], x2, ctb).then_inc("VE")
                    env.mark(("p2b.r1", i, tm))
                    vector.wait("VE", ("p2b.r1", i, tm))
                    vector.tensor_sub(h4(rptile)[:, :, 0:64],
                                      h4(rs1)[:, :, 0:64], h4(rs1)[:, :, 64:128])
                    vector.tensor_add(h4(rptile)[:, :, 64:128],
                                      h4(rs2)[:, :, 0:64], h4(rs2)[:, :, 64:128]
                                      ).then_inc("VE")
                    env.mark(("p2b.rope", i, tm))
                if kind == "q":
                    vector.wait("TE", ("p2b.qtp", i, tm))
                    if u >= 2:
                        vector.wait("SO" + "01"[u % 2], ("p2b.qst", *divmod(u - 2, 8)))
                    vector.tensor_copy(out=qstg[u % 2],
                                       in_=psb[4 + u % 2][:, 0:256].bitcast(BF16)
                                       ).then_inc("VE")
                    env.mark(("p2b.qevac", i, tm))
                else:
                    srct = rptile if kind == "k" else ptile
                    vector.wait("VE", ("p2b.rope", i, tm) if kind == "k"
                                else ("p2b.pevac", i, tm))
                    iu, tu = divmod(u - 2, 8)
                    if u >= 2 and SG[iu][0] != "q":
                        vector.wait("SO" + "01"[u % 2], ("p2b.kst", iu, tu))
                    vector.tensor_copy(out=kstg[u % 2], in_=srct).then_inc("VE")
                    env.mark(("p2b.kevac", i, tm))
                env.mark(("p2b.ropedone", i, tm))

    ctx.pop()
    ctx.pop()

    ctx.push()
    mask01 = ctx.alloc(8192, BF16)
    KT = [ctx.alloc(8192, BF16), ctx.alloc(8192, BF16)]
    VH = [ctx.alloc(64 * 128, BF16), ctx.alloc(64 * 128, BF16)]
    P = [ctx.alloc(8192, BF16), ctx.alloc(8192, BF16)]
    PTb = ctx.alloc(4 * 1024, BF16)
    qtile = [ctx.alloc(128, BF16), ctx.alloc(128, BF16)]
    ctxf = [ctx.alloc(128, F32), ctx.alloc(128, F32)]
    ctxb = [ctx.alloc(128, BF16), ctx.alloc(128, BF16)]
    sums = ctx.alloc(4, F32)
    recf = ctx.alloc(4, F32)
    recr = ctx.alloc(256, F32)
    onesf = ctx.alloc(128, F32)

    def Pn(w):
        return P[w % 2]

    ST = []
    sc_list = []
    pt_list = []
    sl_map = {}
    for w in range(256):
        v, qh = divmod(w, 4)
        qt, hp = divmod(v, 8)
        nck, nbt = 2 * (qt + 1), qt + 1
        ST.append(dict(v=v, qh=qh, qt=qt, hp=hp, nck=nck, nbt=nbt))
        for ck in range(nck):
            sc_list.append((v, qh, ck, len(sc_list) % 4))
        for bt in range(nbt):
            g = len(pt_list)
            owner = "SE" if bt % 4 == 3 else "VE"
            pt_list.append((v, qh, bt, 4 + g % 2, g % 4, owner))
            sl_map[(v, qh, bt)] = (4 + g % 2, g % 4, owner)
    sc_of, pt_of = {}, {}
    si = pi = 0
    for w in range(256):
        s = ST[w]
        sc_of[w] = si
        pt_of[w] = pi
        si += s["nck"]
        pi += s["nbt"]

    @env.sync
    def _(sync):
        sync.wait("SO0", ("p2b.qst", 7, 6))
        sync.wait("SO1", ("p2b.qst", 7, 7))
        sync.wait("TE", ("p2b.mm", 11, 7))
        env.mark("p2c.const")
        for qt in range(8):
            L = 1024 * (qt + 1)
            if qt >= 1:
                sync.wait("VE", ("p2c.msk", 8 * qt - 1, 3))
            sync.dma_start(out=mask01[:, 0:L],
                           in_=dram["mask_c"][qt][:, 0:L]).then_inc("CN", 16)
            env.mark(("p2c.mask", qt))
            sync.wait("CC", ("p2b.ag", qt))
            for hp in range(8):
                v = 8 * qt + hp
                if v >= 2:
                    sync.wait("TE", ("p2c.pv", v - 2, 3))
                for bp in range(qt + 1):
                    sync.dma_start(
                        out=KT[v % 2][:, 1024 * bp:1024 * (bp + 1)]
                            .rearrange("p (a b) -> p a b", b=1024),
                        in_=dram["kvb_%d" % bp][:, 128 * hp:128 * (hp + 1)],
                        transpose=True).then_inc("Q" + "AB"[v % 2], 16)
                    sync.dma_start(
                        out=VH[v % 2].rearrange("p (kt d) -> p kt d", d=128)
                            [:, 8 * bp:8 * (bp + 1), :],
                        in_=dram["kvb_%d" % bp]
                            [:, 1024 + 128 * hp:1024 + 128 * (hp + 1)]
                            .rearrange("(kt p) d -> p kt d", p=128),
                    ).then_inc("Q" + "AB"[v % 2], 16)
                env.mark(("p2c.kv", v))
                for qh in range(4):
                    w = 4 * v + qh
                    if w >= 2:
                        s2 = ST[w - 2]
                        sync.wait("TE", ("p2c.sc", s2["v"], s2["qh"],
                                         s2["nck"] - 1))
                    sync.dma_start(
                        out=qtile[w % 2],
                        in_=dram["qT_d"][128 * (4 * hp + qh):128 * (4 * hp + qh) + 128,
                                         128 * qt:128 * (qt + 1)],
                    ).then_inc("R" + "01"[w % 2], 16)
                    env.mark(("p2c.q", v, qh))

    @env.tensor
    def _(tensor):
        tensor.wait("CN", "p2c.const")
        for wi in range(257):
            if wi < 256:
                w = wi
                s = ST[w]
                v, qh = s["v"], s["qh"]
                if qh == 0:
                    tensor.wait("Q" + "AB"[v % 2], ("p2c.kv", v))
                tensor.wait("R" + "01"[w % 2], ("p2c.q", v, qh))
                for ck in range(s["nck"]):
                    g = sc_of[w] + ck
                    if g >= 4:
                        vp, qp, cp, _ = sc_list[g - 4]
                        tensor.wait("SE", ("p2c.ev", vp, qp, cp))
                    ins = tensor.matmul(
                        psb[sc_list[g][3]][:, :], qtile[w % 2],
                        KT[v % 2][:, 512 * ck:512 * (ck + 1)],
                        start=True, stop=True)
                    ins.then_inc("TE")
                    env.mark(("p2c.sc", v, qh, ck))
            if wi >= 1:
                w = wi - 1
                s = ST[w]
                v, qh, nbt, nkt = s["v"], s["qh"], s["nbt"], 8 * s["nbt"]
                tensor.wait("VE", ("p2c.msk", v, qh))
                if w >= 1:
                    tensor.wait("SE", ("p2c.cev", *divmod(w - 1, 4)))
                for bt in range(nbt):
                    g = pt_of[w] + bt
                    bank, slot, owner = sl_map[(v, qh, bt)]
                    if g >= 2:
                        vp, qp, bp2, _, _, op = pt_list[g - 2]
                        tensor.wait(op, ("p2c.ptev", vp, qp, bp2))
                    pbf = psb[bank][:, :].bitcast(BF16)
                    for u2 in range(8):
                        kt = 8 * bt + u2
                        ins = tensor.transpose(
                            pbf[:, 128 * u2:128 * (u2 + 1)],
                            Pn(w)[:, 128 * kt:128 * (kt + 1)], identt)
                    ins.then_inc("TE")
                    env.mark(("p2c.pt", v, qh, bt))
                    if bt >= 1:
                        _, pslot, powner = sl_map[(v, qh, bt - 1)]
                        tensor.wait(powner, ("p2c.ptev", v, qh, bt - 1))
                        for u2 in range(8):
                            kt = 8 * (bt - 1) + u2
                            ins = tensor.matmul(
                                psb[6][:, 0:128],
                                VH[v % 2].rearrange("p (kt d) -> p kt d",
                                                    d=128)[:, kt, :],
                                PTb[:, 1024 * pslot:1024 * pslot + 1024]
                                    .rearrange("p (a b) -> p a b", b=128)[:, u2, :],
                                start=(kt == 0), stop=False)
                _, fslot, fowner = sl_map[(v, qh, nbt - 1)]
                tensor.wait(fowner, ("p2c.ptev", v, qh, nbt - 1))
                for u2 in range(8):
                    kt = 8 * (nbt - 1) + u2
                    ins = tensor.matmul(
                        psb[6][:, 0:128],
                        VH[v % 2].rearrange("p (kt d) -> p kt d", d=128)[:, kt, :],
                        PTb[:, 1024 * fslot:1024 * fslot + 1024]
                            .rearrange("p (a b) -> p a b", b=128)[:, u2, :],
                        start=(nbt == 1 and u2 == 0), stop=(u2 == 7))
                ins.then_inc("TE")
                env.mark(("p2c.pv", v, qh))
                tensor.wait("VE", ("p2c.rec", v, qh))
                if w >= 1:
                    tensor.wait("VE", ("p2c.ctt", *divmod(w - 1, 4)))
                ins = tensor.transpose(psb[7][0:1, 0:128],
                                       recf[:, 2 * (w % 2):2 * (w % 2) + 1],
                                       identf)
                ins.then_inc("TE")
                env.mark(("p2c.rtp", v, qh))
                tensor.wait("VE", ("p2c.rrow", v, qh))
                ins = tensor.matmul(psb[7][:, 128:256], onesf[0:1, :],
                                    recr[0:1, 128 * (w % 2):128 * (w % 2) + 128],
                                    start=True, stop=True)
                ins.then_inc("TE")
                env.mark(("p2c.rbc", v, qh))

    @env.vector
    def _(vector):
        vector.wait("CN", "p2c.const")
        vector.wait("TE", ("p2b.mm", 11, 7))
        vector.memset(onesf[0:1, :], 1.0)
        for w in range(256):
            s = ST[w]
            v, qh, qt = s["v"], s["qh"], s["qt"]
            L = 1024 * (qt + 1)
            p = w % 2
            vector.wait("SE", ("p2c.ev", v, qh, s["nck"] - 1))
            vector.wait("CN", ("p2c.mask", qt))
            vector.memzero(sums[:, p:p + 1]).then_inc("VE")
            env.mark(("p2c.sz", v, qh))
            vector.wait("VE", ("p2c.sz", v, qh))
            vector.scalar_tensor_tensor(
                out=Pn(w)[:, 0:L], in0=Pn(w)[:, 0:L], scalar=1.0,
                in1=mask01[:, 0:L], op0=ALU.mult, op1=ALU.mult,
                accum_out=sums[:, p:p + 1]).then_inc("VE")
            env.mark(("p2c.msk", v, qh))
            vector.wait("VE", ("p2c.msk", v, qh))
            vector.reciprocal(recf[:, 2 * p:2 * p + 1],
                              sums[:, p:p + 1]).then_inc("VE")
            env.mark(("p2c.rec", v, qh))
            for bt in range(s["nbt"]):
                bank, slot, owner = sl_map[(v, qh, bt)]
                if owner != "VE":
                    continue
                vector.wait("TE", ("p2c.pt", v, qh, bt))
                vector.tensor_copy(
                    out=PTb[:, 1024 * slot:1024 * slot + 1024],
                    in_=psb[bank][:, :].bitcast(BF16)).then_inc("VE")
                env.mark(("p2c.ptev", v, qh, bt))
            vector.wait("TE", ("p2c.rtp", v, qh))
            vector.tensor_copy(out=recr[0:1, 128 * p:128 * p + 128],
                               in_=psb[7][0:1, 0:128]).then_inc("VE")
            env.mark(("p2c.rrow", v, qh))
            vector.wait("SE", ("p2c.cev", v, qh))
            vector.wait("TE", ("p2c.rbc", v, qh))
            if w >= 2:
                vector.wait("SO" + "01"[w % 2], ("p2c.cst", *divmod(w - 2, 4)))
            vector.tensor_mul(ctxb[p], ctxf[p], psb[7][:, 128:256]
                              ).then_inc("VE")
            env.mark(("p2c.ctt", v, qh))

    @env.scalar
    def _(scalar):
        for wi in range(257):
            if wi < 256:
                w = wi
                s = ST[w]
                v, qh = s["v"], s["qh"]
                for ck in range(s["nck"]):
                    g = sc_of[w] + ck
                    scalar.wait("TE", ("p2c.sc", v, qh, ck))
                    if ck == 0 and w >= 2:
                        s2 = ST[w - 2]
                        scalar.wait("TE", ("p2c.pt", s2["v"], s2["qh"],
                                           s2["nbt"] - 1))
                    scalar.activation(out=Pn(w)[:, 512 * ck:512 * (ck + 1)],
                                      in_=psb[sc_list[g][3]][:, :],
                                      func=AF.Exp, scale=ISQ,
                                      bias=0.0).then_inc("SE")
                    env.mark(("p2c.ev", v, qh, ck))
            if wi >= 1:
                w = wi - 1
                s = ST[w]
                v, qh = s["v"], s["qh"]
                p = w % 2
                for bt in range(s["nbt"]):
                    bank, slot, owner = sl_map[(v, qh, bt)]
                    if owner != "SE":
                        continue
                    scalar.wait("TE", ("p2c.pt", v, qh, bt))
                    scalar.copy(
                        out=PTb[:, 1024 * slot:1024 * slot + 1024],
                        in_=psb[bank][:, :].bitcast(BF16)).then_inc("SE")
                    env.mark(("p2c.ptev", v, qh, bt))
                scalar.wait("TE", ("p2c.pv", v, qh))
                scalar.copy(out=ctxf[p], in_=psb[6][:, 0:128]).then_inc("SE")
                env.mark(("p2c.cev", v, qh))

    @env.gpsimd
    def _(gpsimd):
        for qt in range(8):
            for hp in range(8):
                v = 8 * qt + hp
                for qh in range(4):
                    w = 4 * v + qh
                    h = 4 * hp + qh
                    gpsimd.wait("VE", ("p2c.ctt", v, qh))
                    gpsimd.dma_start(
                        out=dram["ctxT_d"][128 * h:128 * (h + 1),
                                           128 * qt:128 * (qt + 1)],
                        in_=ctxb[w % 2]).then_inc("SO" + "01"[w % 2], 16)
                    env.mark(("p2c.cst", v, qh))
        if "kvdump_0" in dram:
            for b in range(8):
                gpsimd.dma_start(out=dram["kvdump_%d" % b],
                                 in_=dram["kvb_%d" % b]).then_inc("GD", 16)
                env.mark(("p2c.kvd", b))
            gpsimd.wait("GD", ("p2c.kvd", 7))
    ctx.pop()

    ctx.push()
    ctxT_sb = ctx.alloc(32 * 1024, BF16)
    wbuf2 = [ctx.alloc(32 * 128, BF16), ctx.alloc(32 * 128, BF16)]
    wstg = ctx.alloc(512, BF16)
    tstg = [ctx.alloc(512, BF16), ctx.alloc(512, BF16)]

    @env.sync
    def _(sync):
        sync.wait("SO0", ("p2c.cst", 62, 3))
        sync.wait("SO1", ("p2c.cst", 63, 3))
        sync.dma_start(out=ctxT_sb.rearrange("p (kk t) -> p kk t", t=1024),
                       in_=dram["ctxT_d"].rearrange("(kk p) t -> p kk t", p=128)
                       ).then_inc("CN", 16)
        env.mark("p2d.ctx")

    @env.gpsimd
    def _(gpsimd):
        def load_wo(om):
            if om >= 2:
                gpsimd.wait("TE", ("p2d.mm", om - 2, 1))
            gpsimd.dma_start(
                out=wbuf2[om % 2].rearrange("p (kk n) -> p kk n", n=128),
                in_=dram["l1wo"][:, 128 * om:128 * (om + 1)]
                    .rearrange("(kk p) n -> p kk n", p=128)).then_inc(
                        "W" + "01"[om % 2], 16)
            env.mark(("p2d.wl", om))
        load_wo(0)
        load_wo(1)
        for om in range(32):
            for tb in range(2):
                u = 2 * om + tb
                gpsimd.wait("VE", ("p2d.tev", om, tb))
                gpsimd.dma_start(
                    out=dram["att_d"][512 * tb:512 * (tb + 1),
                                      128 * om:128 * (om + 1)]
                        .rearrange("(j p) o -> p j o", p=128),
                    in_=tstg[u % 2].rearrange("p (j o) -> p j o", o=128),
                ).then_inc("SO" + "01"[u % 2], 16)
                env.mark(("p2d.ast", om, tb))
            if om + 2 < 32:
                load_wo(om + 2)

    @env.tensor
    def _(tensor):
        tensor.wait("CN", "p2d.ctx")
        for om in range(32):
            tensor.wait("W" + "01"[om % 2], ("p2d.wl", om))
            for tb in range(2):
                u = 2 * om + tb
                bank = u % 2
                if u >= 2:
                    tensor.wait("VE", ("p2d.wev", *divmod(u - 2, 2)))
                for kk in range(32):
                    ins = tensor.matmul(
                        psb[bank][:, :],
                        wbuf2[om % 2].rearrange("p (kk n) -> p kk n", n=128)[:, kk, :],
                        ctxT_sb[:, 1024 * kk + 512 * tb:1024 * kk + 512 * (tb + 1)],
                        start=(kk == 0), stop=(kk == 31))
                ins.then_inc("TE")
                env.mark(("p2d.mm", om, tb))
                tensor.wait("VE", ("p2d.wev", om, tb))
                if u >= 2:
                    tensor.wait("VE", ("p2d.tev", *divmod(u - 2, 2)))
                pbf = psb[2 + u % 2][:, :].bitcast(BF16)
                for j in range(4):
                    ins = tensor.transpose(pbf[:, 128 * j:128 * (j + 1)],
                                           wstg[:, 128 * j:128 * (j + 1)], identt)
                ins.then_inc("TE")
                env.mark(("p2d.tp", om, tb))

    @env.vector
    def _(vector):
        for om in range(32):
            for tb in range(2):
                u = 2 * om + tb
                vector.wait("TE", ("p2d.mm", om, tb))
                if u >= 1:
                    vector.wait("TE", ("p2d.tp", *divmod(u - 1, 2)))
                vector.tensor_copy(out=wstg, in_=psb[u % 2][:, :]).then_inc("VE")
                env.mark(("p2d.wev", om, tb))
                vector.wait("TE", ("p2d.tp", om, tb))
                if u >= 2:
                    vector.wait("SO" + "01"[u % 2], ("p2d.ast", *divmod(u - 2, 2)))
                vector.tensor_copy(out=tstg[u % 2],
                                   in_=psb[2 + u % 2][:, 0:256].bitcast(BF16)
                                   ).then_inc("VE")
                env.mark(("p2d.tev", om, tb))
    ctx.pop()

    ctx.push()
    xr3 = [ctx.alloc(4096, F32), ctx.alloc(4096, F32)]
    at3 = [ctx.alloc(4096, BF16), ctx.alloc(4096, BF16)]
    xo3 = [ctx.alloc(4096, F32), ctx.alloc(4096, F32)]

    @env.sync
    def _(sync):
        sync.wait("SO0", ("p2d.ast", 31, 0))
        sync.wait("SO1", ("p2d.ast", 31, 1))
        for tt in range(8):
            if tt >= 2:
                sync.wait("VE", ("p2e.add", tt - 2))
            sync.dma_start(out=xr3[tt % 2],
                           in_=dram["xres2"][128 * tt:128 * (tt + 1)]
                           ).then_inc("R" + "01"[tt % 2], 16)
            sync.dma_start(out=at3[tt % 2],
                           in_=dram["att_d"][128 * tt:128 * (tt + 1)]
                           ).then_inc("R" + "01"[tt % 2], 16)
            env.mark(("p2e.load", tt))

    @env.vector
    def _(vector):
        for tt in range(8):
            vector.wait("R" + "01"[tt % 2], ("p2e.load", tt))
            if tt >= 2:
                vector.wait("SO" + "01"[tt % 2], ("p2e.store", tt - 2))
            vector.tensor_add(xo3[tt % 2], xr3[tt % 2], at3[tt % 2]).then_inc("VE")
            env.mark(("p2e.add", tt))

    @env.gpsimd
    def _(gpsimd):
        for tt in range(8):
            gpsimd.wait("VE", ("p2e.add", tt))
            gpsimd.dma_start(out=dram["xres3"][128 * tt:128 * (tt + 1)],
                             in_=xo3[tt % 2]).then_inc("SO" + "01"[tt % 2], 16)
            env.mark(("p2e.store", tt))
    ctx.pop()
    ctx.pop()


def declare_ph2(nc, dram, standalone=True, debug=False):
    if "xres2" not in dram:
        dram["xres2"] = nc.dram_tensor(
            "xres2", [C, D], F32,
            kind="ExternalInput" if standalone else "Internal").ap()
    for nm, shp in [("l1wq", [D, D]), ("l1wk", [D, 1024]), ("l1wv", [D, 1024]),
                    ("l1wo", [D, D])]:
        dram[nm] = nc.dram_tensor(nm, shp, F32, kind="ExternalInput").ap()
    dram["cosP_c"] = nc.dram_tensor("cosP_c", [128, 512], BF16,
                                    kind="ExternalInput").ap()
    dram["sinP_c"] = nc.dram_tensor("sinP_c", [128, 512], BF16,
                                    kind="ExternalInput").ap()
    dram["mask_c"] = nc.dram_tensor("mask_c", [8, 128, 8192], BF16,
                                    kind="ExternalInput").ap()
    dram["identf"] = nc.dram_tensor("identf", [128, 128], F32,
                                    kind="ExternalInput").ap()
    if "ident" not in dram:
        dram["ident"] = nc.dram_tensor("ident", [128, 128], BF16,
                                       kind="ExternalInput").ap()
    if "epsin" not in dram:
        dram["epsin"] = nc.dram_tensor("epsin", [128, 1], F32,
                                       kind="ExternalInput").ap()
    dk = "ExternalOutput" if debug else "Internal"
    if debug:
        dram["dbg1"] = nc.dram_tensor("dbg1", [128, 32, C], BF16, kind=dk).ap()
    dram["qT_d"] = nc.dram_tensor("qT_d", [D, C], BF16, kind=dk).ap()
    for b in range(8):
        dram["agin2_%d" % b] = nc.dram_tensor(
            "agin2_%d" % b, [128, 2048], BF16, kind="Internal").ap()
        dram["kvb_%d" % b] = nc.dram_tensor(
            "kvb_%d" % b, [1024, 2048], BF16, kind="Internal").ap()
        if debug:
            dram["kvdump_%d" % b] = nc.dram_tensor(
                "kvdump_%d" % b, [1024, 2048], BF16, kind="ExternalOutput").ap()
            dram["agdump_%d" % b] = nc.dram_tensor(
                "agdump_%d" % b, [128, 2048], BF16, kind="ExternalOutput").ap()
    dram["ctxT_d"] = nc.dram_tensor("ctxT_d", [D, C], BF16, kind=dk).ap()
    dram["att_d"] = nc.dram_tensor("att_d", [C, D], BF16, kind=dk).ap()
    if "xres3" not in dram:
        dram["xres3"] = nc.dram_tensor(
            "xres3", [C, D], F32,
            kind="ExternalOutput" if standalone else "Internal").ap()


def host_prep_ph2(inp):
    sh = {}
    sh["l1wq"] = np.ascontiguousarray(np.asarray(inp["L1_attn_Wq"]))
    sh["l1wk"] = np.ascontiguousarray(np.asarray(inp["L1_attn_Wk"]))
    sh["l1wv"] = np.ascontiguousarray(np.asarray(inp["L1_attn_Wv"]))
    sh["l1wo"] = np.ascontiguousarray(np.asarray(inp["L1_attn_Wo"]))
    inv = 1.0 / 500000.0 ** (np.arange(0, 128, 2, dtype=np.float32) / 128)
    tpos = np.arange(T, dtype=np.float32)
    fr = np.outer(tpos, inv)
    cosf = np.cos(fr).astype(np.float32)
    sinf = np.sin(fr).astype(np.float32)
    sh["identf"] = np.eye(128, dtype=np.float32)
    ss = np.arange(T)
    keypos = 128 * (8 * (ss // 1024) + (ss % 1024) // 128) + ss % 128
    pc = []
    for c in range(NCORES):
        m = {}
        rows = zz_rows(c)
        m["cosP_c"] = np.ascontiguousarray(
            cosf[rows].reshape(8, 128, 64).transpose(1, 0, 2).reshape(128, 512)
        ).astype(BF)
        m["sinP_c"] = np.ascontiguousarray(
            sinf[rows].reshape(8, 128, 64).transpose(1, 0, 2).reshape(128, 512)
        ).astype(BF)
        msk = np.zeros((8, 128, T), np.float32)
        for qt in range(8):
            pos = 128 * (8 * qt + c) + np.arange(128)
            msk[qt] = (keypos[None, :] <= pos[:, None]).astype(np.float32)
        m["mask_c"] = msk.astype(BF)
        pc.append(m)
    return sh, pc


def build_ph2(debug=False):
    nc = bass.Bass("TRN2", target_bir_lowering=False, debug=False)
    dram = {}
    declare_ph2(nc, dram, standalone=True, debug=debug)
    es = contextlib.ExitStack()
    with es:
        ctx = Ctx(nc, es)
        ctx.sems(*PH1_SEMS)
        epst = ctx.alloc(1, F32)
        dram["epsc"] = epst
        with nc.Block() as block:
            @block.sync
            def _(sync):
                sync.dma_start(out=epst, in_=dram["epsin"]).then_inc(ctx.sem["CN"], 16)
            ctx.cnt["CN"] += 16

            run_twopass(ctx, block, lambda env: emit_ph2(env, dram))

            @block.gpsimd
            def _(gpsimd):
                gpsimd.wait_ge(ctx.sem["SO0"], ctx.cnt["SO0"])
                gpsimd.wait_ge(ctx.sem["SO1"], ctx.cnt["SO1"])
    return nc



def emit_ph3(env, dram, fence=None):
    ctx = env.ctx
    psb = ctx.psb
    epst = dram["epsc"]

    ctx.push()
    identt = ctx.alloc(128, BF16)
    stats = ctx.alloc(64, F32)
    varsum = ctx.alloc(8, F32)
    ctx.push()
    xlnT = ctx.alloc(32 * 1024, BF16)
    ctx.push()
    xr = [ctx.alloc(4096, F32), ctx.alloc(4096, F32)]
    lnscr = ctx.alloc(4096, BF16)

    @env.sync
    def _(sync):
        if fence:
            for s_, v_ in fence.items():
                sync.wait(s_, val=v_)
        sync.dma_start(out=identt, in_=dram["ident"]).then_inc("CN", 16)
        env.mark("p3.ident")
        for tt in range(8):
            if tt >= 2:
                sync.wait("VE", ("p3.norm", tt - 2))
                sync.wait("SE", ("p3.sq1", tt - 2))
            sync.dma_start(out=xr[tt % 2],
                           in_=dram["xres3"][128 * tt:128 * (tt + 1)]
                           ).then_inc("X" + "AB"[tt % 2], 16)
            env.mark(("p3.xload", tt))

    @env.vector
    def _(vector):
        vector.wait("CN", "p3.ident")
        vector.memzero(varsum)
        for tt in range(8):
            mus = stats[:, 8 * tt + 0:8 * tt + 1]
            nmu = stats[:, 8 * tt + 1:8 * tt + 2]
            rstd = stats[:, 8 * tt + 2:8 * tt + 3]
            sqv = stats[:, 8 * tt + 3:8 * tt + 4]
            nmur = stats[:, 8 * tt + 4:8 * tt + 5]
            vector.wait("X" + "AB"[tt % 2], ("p3.xload", tt))
            vector.reduce_sum(out=mus, in_=xr[tt % 2], axis=AX).then_inc("VE")
            env.mark(("p3.mus", tt))
            vector.wait("VE", ("p3.mus", tt))
            vector.tensor_scalar_mul(nmu, mus, -1.0 / 4096.0).then_inc("VE")
            env.mark(("p3.nmu", tt))
            vector.wait("SE", ("p3.sq2", tt))
            vector.reciprocal(rstd, sqv).then_inc("VE")
            env.mark(("p3.rstd", tt))
            vector.wait("VE", ("p3.rstd", tt))
            vector.tensor_mul(nmur, nmu, rstd).then_inc("VE")
            env.mark(("p3.nmur", tt))
            vector.wait("VE", ("p3.nmur", tt))
            vector.tensor_scalar(out=lnscr, in0=xr[tt % 2], scalar1=rstd,
                                 scalar2=nmur, op0=ALU.mult, op1=ALU.add
                                 ).then_inc("VE")
            env.mark(("p3.norm", tt))
            for b in range(4):
                vector.wait("TE", ("p3.tp", tt, b))
                vector.tensor_copy(
                    out=xlnT.rearrange("p (kk t) -> p kk t", t=1024)
                        [:, 8 * b:8 * b + 8, 128 * tt:128 * (tt + 1)],
                    in_=psb[b][:, :].bitcast(BF16)
                        .rearrange("p (kk t) -> p kk t", t=128),
                ).then_inc("VE")
                env.mark(("p3.evac", tt, b))

    @env.scalar
    def _(scalar):
        for tt in range(8):
            nmu = stats[:, 8 * tt + 1:8 * tt + 2]
            sqv = stats[:, 8 * tt + 3:8 * tt + 4]
            scalar.wait("VE", ("p3.nmu", tt))
            scalar.activation(out=lnscr, in_=xr[tt % 2], func=AF.Square, bias=nmu,
                              scale=1.0, accum_out=varsum[:, tt:tt + 1]
                              ).then_inc("SE")
            env.mark(("p3.sq1", tt))
            scalar.wait("SE", ("p3.sq1", tt))
            scalar.activation(out=sqv, in_=varsum[:, tt:tt + 1], func=AF.Sqrt,
                              scale=1.0 / 4096.0, bias=epst).then_inc("SE")
            env.mark(("p3.sq2", tt))

    @env.tensor
    def _(tensor):
        for tt in range(8):
            tensor.wait("VE", ("p3.norm", tt))
            for b in range(4):
                if 4 * tt + b >= 4:
                    tpp, bp = divmod(4 * tt + b - 4, 4)
                    tensor.wait("VE", ("p3.evac", tpp, bp))
                pbf = psb[b][:, :].bitcast(BF16)
                for u in range(8):
                    kk = 8 * b + u
                    ins = tensor.transpose(pbf[:, 128 * u:128 * (u + 1)],
                                           lnscr[:, 128 * kk:128 * (kk + 1)],
                                           identt)
                ins.then_inc("TE")
                env.mark(("p3.tp", tt, b))
    ctx.pop()

    ctx.push()
    wgt = ctx.alloc(32 * 8, BF16)
    at_sb = ctx.alloc(32 * 128, BF16)
    b_sb = ctx.alloc(4096, BF16)
    i8_sb = ctx.alloc(128, BF16)
    bbase = ctx.alloc(32, F32)
    lg = ctx.alloc(8, F32)
    m1n = ctx.alloc(8, F32)
    pe_ = ctx.alloc(8, F32)
    mxm = ctx.alloc(8, F32)
    lgm = ctx.alloc(8, F32)
    wexp = ctx.alloc(8, F32)
    route = ctx.alloc(8, BF16)
    routeT = ctx.alloc(1024, BF16)
    r_sb = ctx.alloc(1024, BF16)
    hw_sb = ctx.alloc(1024, BF16)
    wbuf3 = [ctx.alloc(32 * 128, BF16), ctx.alloc(32 * 128, BF16)]
    wstg = ctx.alloc(512, BF16)
    tstg = [ctx.alloc(512, BF16), ctx.alloc(512, BF16)]

    @env.sync
    def _(sync):
        sync.wait("VE", ("p3.evac", 7, 3))
        sync.wait("SE", ("p3.sq2", 7))
        sync.dma_start(out=i8_sb[0:8], in_=dram["i8c"]).then_inc("CN", 16)
        sync.dma_start(out=bbase, in_=dram["bbase_c"]).then_inc("CN", 16)
        env.mark("p3b.res")

    @env.gpsimd
    def _(gpsimd):
        gpsimd.wait("VE", ("p3.evac", 7, 3))
        gpsimd.wait("SE", ("p3.sq2", 7))
        gpsimd.dma_start(out=at_sb.rearrange("p (kk n) -> p kk n", n=128),
                         in_=dram["l1A"].rearrange("(kk p) n -> p kk n", p=128)
                         ).then_inc("CN2", 16)
        gpsimd.dma_start(out=b_sb[0:128], in_=dram["l1B"]).then_inc("CN2", 16)
        gpsimd.dma_start(out=wgt.rearrange("p (kk n) -> p kk n", n=8),
                         in_=dram["wgateT"].rearrange("(kk p) n -> p kk n", p=128)
                         ).then_inc("CN2", 16)
        env.mark("p3b.res2")
        def load_wb(om):
            if om >= 2:
                gpsimd.wait("TE", ("p3b.mm", om - 2, 1))
            gpsimd.dma_start(
                out=wbuf3[om % 2].rearrange("p (kk n) -> p kk n", n=128),
                in_=dram["wbaseT"][:, 128 * om:128 * (om + 1)]
                    .rearrange("(kk p) n -> p kk n", p=128)).then_inc(
                        "W" + "01"[om % 2], 16)
            env.mark(("p3b.wl", om))
        load_wb(0)
        load_wb(1)
        for om in range(32):
            for tb in range(2):
                u = 2 * om + tb
                gpsimd.wait("VE", ("p3b.tev", om, tb))
                gpsimd.dma_start(
                    out=dram["att2_d"][512 * tb:512 * (tb + 1),
                                       128 * om:128 * (om + 1)]
                        .rearrange("(j p) o -> p j o", p=128),
                    in_=tstg[u % 2].rearrange("p (j o) -> p j o", o=128),
                ).then_inc("SO" + "01"[u % 2], 16)
                env.mark(("p3b.ast", om, tb))
            if om + 2 < 32:
                load_wb(om + 2)

    @env.tensor
    def _(tensor):
        tensor.wait("CN", "p3b.res")
        tensor.wait("CN2", "p3b.res2")
        tensor.wait("VE", ("p3.evac", 7, 3))
        tensor.wait("CN2", "p3b.res2")
        for tm in range(8):
            if tm >= 1:
                tensor.wait("VE", ("p3b.lgev", tm - 1))
            for kk in range(32):
                ins = tensor.matmul(
                    psb[4][:, 0:8],
                    xlnT[:, 1024 * kk + 128 * tm:1024 * kk + 128 * (tm + 1)],
                    wgt.rearrange("p (kk n) -> p kk n", n=8)[:, kk, :],
                    start=(kk == 0), stop=(kk == 31))
            ins.then_inc("TE")
            env.mark(("p3b.lg", tm))
            tensor.wait("VE", ("p3b.route", tm))
            ins = tensor.transpose(psb[5][0:8, 0:64].bitcast(BF16)[0:8, 0:128],
                                   route, identt)
            ins.then_inc("TE")
            env.mark(("p3b.rtp", tm))
        for tb in range(2):
            tensor.wait("VE", ("p3b.rtev", 7))
            if tb >= 0:
                pass
            for kk in range(32):
                ins = tensor.matmul(
                    psb[6][:, :],
                    at_sb.rearrange("p (kk n) -> p kk n", n=128)[:, kk, :],
                    xlnT[:, 1024 * kk + 512 * tb:1024 * kk + 512 * (tb + 1)],
                    start=(kk == 0), stop=(kk == 31))
            ins.then_inc("TE")
            env.mark(("p3b.rmm", tb))
            tensor.wait("VE", ("p3b.rev", tb))
            ins = tensor.matmul(psb[7][:, :], i8_sb[0:8, :],
                                routeT[0:8, 512 * tb:512 * (tb + 1)],
                                start=True, stop=True)
            ins.then_inc("TE")
            env.mark(("p3b.rwb", tb))
        for om in range(32):
            tensor.wait("W" + "01"[om % 2], ("p3b.wl", om))
            for tb in range(2):
                u = 2 * om + tb
                bank = u % 2
                if u >= 2:
                    tensor.wait("VE", ("p3b.wev", *divmod(u - 2, 2)))
                for kk in range(32):
                    tensor.matmul(
                        psb[bank][:, :],
                        wbuf3[om % 2].rearrange("p (kk n) -> p kk n", n=128)[:, kk, :],
                        xlnT[:, 1024 * kk + 512 * tb:1024 * kk + 512 * (tb + 1)],
                        start=(kk == 0), stop=False)
                tensor.wait("VE", ("p3b.hw", tb))
                ins = tensor.matmul(
                    psb[bank][:, :], b_sb[0:128, 128 * om:128 * (om + 1)],
                    hw_sb[0:128, 512 * tb:512 * (tb + 1)],
                    start=False, stop=True)
                ins.then_inc("TE")
                env.mark(("p3b.mm", om, tb))
                tensor.wait("VE", ("p3b.wev", om, tb))
                if u >= 2:
                    tensor.wait("VE", ("p3b.tev", *divmod(u - 2, 2)))
                pbf = psb[2 + u % 2][:, :].bitcast(BF16)
                for j in range(4):
                    ins = tensor.transpose(pbf[:, 128 * j:128 * (j + 1)],
                                           wstg[:, 128 * j:128 * (j + 1)], identt)
                ins.then_inc("TE")
                env.mark(("p3b.tp", om, tb))

    @env.vector
    def _(vector):
        vector.wait("CN", "p3b.res")
        for tm in range(8):
            vector.wait("TE", ("p3b.lg", tm))
            vector.tensor_copy(out=lg, in_=psb[4][:, 0:8]).then_inc("VE")
            env.mark(("p3b.lgev", tm))
            vector.wait("VE", ("p3b.lgev", tm))
            vector.reduce_max(out=m1n[:, 0:1], in_=lg, axis=AX, negate=True
                              ).then_inc("VE")
            env.mark(("p3b.m1", tm))
            vector.wait("VE", ("p3b.m1", tm))
            vector.tensor_scalar_mul(m1n[:, 1:2], m1n[:, 0:1], -1.0).then_inc("VE")
            env.mark(("p3b.m1p", tm))
            vector.wait("VE", ("p3b.m1p", tm))
            vector.tensor_single_scalar(out=mxm, in_=lg, scalar=m1n[:, 1:2],
                                        op=ALU.is_ge).then_inc("VE")
            env.mark(("p3b.mx", tm))
            vector.wait("VE", ("p3b.mx", tm))
            vector.scalar_tensor_tensor(out=lgm, in0=mxm, scalar=-1e30, in1=lg,
                                        op0=ALU.mult, op1=ALU.add).then_inc("VE")
            env.mark(("p3b.lgm", tm))
            vector.wait("VE", ("p3b.lgm", tm))
            vector.reduce_max(out=m1n[:, 2:3], in_=lgm, axis=AX, negate=True
                              ).then_inc("VE")
            env.mark(("p3b.m2", tm))
            vector.wait("VE", ("p3b.m2", tm))
            vector.tensor_scalar_mul(m1n[:, 3:4], m1n[:, 2:3], -1.0).then_inc("VE")
            env.mark(("p3b.m2p", tm))
            vector.wait("VE", ("p3b.m2p", tm))
            vector.tensor_single_scalar(out=mxm, in_=lg, scalar=m1n[:, 3:4],
                                        op=ALU.is_ge).then_inc("VE")
            env.mark(("p3b.mask2", tm))
            vector.wait("SE", ("p3b.pe", tm))
            vector.wait("VE", ("p3b.mask2", tm))
            vector.tensor_mul(wexp, pe_, mxm).then_inc("VE")
            env.mark(("p3b.w", tm))
            vector.wait("VE", ("p3b.w", tm))
            vector.reduce_sum(out=m1n[:, 4:5], in_=wexp, axis=AX).then_inc("VE")
            env.mark(("p3b.den", tm))
            vector.wait("VE", ("p3b.den", tm))
            vector.reciprocal(m1n[:, 5:6], m1n[:, 4:5]).then_inc("VE")
            env.mark(("p3b.rden", tm))
            vector.wait("VE", ("p3b.rden", tm))
            vector.tensor_single_scalar(out=route, in_=wexp, scalar=m1n[:, 5:6],
                                        op=ALU.mult).then_inc("VE")
            env.mark(("p3b.route", tm))
            vector.wait("TE", ("p3b.rtp", tm))
            vector.tensor_copy(out=routeT[0:8, 128 * tm:128 * (tm + 1)],
                               in_=psb[5][0:8, 0:64].bitcast(BF16)[0:8, 0:128]
                               ).then_inc("VE")
            env.mark(("p3b.rtev", tm))
        for tb in range(2):
            vector.wait("TE", ("p3b.rmm", tb))
            vector.tensor_copy(out=r_sb[0:128, 512 * tb:512 * (tb + 1)],
                               in_=psb[6][:, :]).then_inc("VE")
            env.mark(("p3b.rev", tb))
            vector.wait("TE", ("p3b.rwb", tb))
            vector.wait("VE", ("p3b.rev", tb))
            vector.tensor_mul(hw_sb[0:128, 512 * tb:512 * (tb + 1)],
                              r_sb[0:128, 512 * tb:512 * (tb + 1)],
                              psb[7][:, :]).then_inc("VE")
            env.mark(("p3b.hw", tb))
        for om in range(32):
            for tb in range(2):
                u = 2 * om + tb
                vector.wait("TE", ("p3b.mm", om, tb))
                if u >= 1:
                    vector.wait("TE", ("p3b.tp", *divmod(u - 1, 2)))
                vector.tensor_scalar(out=wstg, in0=psb[u % 2][:, :],
                                     scalar1=bbase[:, om:om + 1], scalar2=None,
                                     op0=ALU.add).then_inc("VE")
                env.mark(("p3b.wev", om, tb))
                vector.wait("TE", ("p3b.tp", om, tb))
                if u >= 2:
                    vector.wait("SO" + "01"[u % 2], ("p3b.ast", *divmod(u - 2, 2)))
                vector.tensor_copy(out=tstg[u % 2],
                                   in_=psb[2 + u % 2][:, 0:256].bitcast(BF16)
                                   ).then_inc("VE")
                env.mark(("p3b.tev", om, tb))

    @env.scalar
    def _(scalar):
        for tm in range(8):
            scalar.wait("VE", ("p3b.m1", tm))
            scalar.activation(out=pe_, in_=lg, func=AF.Exp, scale=1.0,
                              bias=m1n[:, 0:1]).then_inc("SE")
            env.mark(("p3b.pe", tm))
    ctx.pop()
    ctx.pop()

    ctx.push()
    xr4 = [ctx.alloc(4096, F32), ctx.alloc(4096, F32)]
    at4 = [ctx.alloc(4096, BF16), ctx.alloc(4096, BF16)]
    xo4 = [ctx.alloc(4096, F32), ctx.alloc(4096, F32)]

    @env.sync
    def _(sync):
        sync.wait("SO0", ("p3b.ast", 31, 0))
        sync.wait("SO1", ("p3b.ast", 31, 1))
        for tt in range(8):
            if tt >= 2:
                sync.wait("VE", ("p3c.add", tt - 2))
            sync.dma_start(out=xr4[tt % 2],
                           in_=dram["xres3"][128 * tt:128 * (tt + 1)]
                           ).then_inc("R" + "01"[tt % 2], 16)
            sync.dma_start(out=at4[tt % 2],
                           in_=dram["att2_d"][128 * tt:128 * (tt + 1)]
                           ).then_inc("R" + "01"[tt % 2], 16)
            env.mark(("p3c.load", tt))

    @env.vector
    def _(vector):
        for tt in range(8):
            vector.wait("R" + "01"[tt % 2], ("p3c.load", tt))
            if tt >= 2:
                vector.wait("SO" + "01"[tt % 2], ("p3c.store", tt - 2))
            vector.tensor_add(xo4[tt % 2], xr4[tt % 2], at4[tt % 2]).then_inc("VE")
            env.mark(("p3c.add", tt))

    @env.gpsimd
    def _(gpsimd):
        for tt in range(8):
            gpsimd.wait("VE", ("p3c.add", tt))
            gpsimd.dma_start(out=dram["y_rows"][128 * tt:128 * (tt + 1)],
                             in_=xo4[tt % 2]).then_inc("SO" + "01"[tt % 2], 16)
            env.mark(("p3c.store", tt))
    ctx.pop()
    ctx.pop()


def declare_ph3(nc, dram, standalone=True):
    if "xres3" not in dram:
        dram["xres3"] = nc.dram_tensor(
            "xres3", [C, D], F32,
            kind="ExternalInput" if standalone else "Internal").ap()
    dram["wbaseT"] = nc.dram_tensor("wbaseT", [D, D], F32, kind="ExternalInput").ap()
    dram["wgateT"] = nc.dram_tensor("wgateT", [D, 8], F32, kind="ExternalInput").ap()
    dram["l1A"] = nc.dram_tensor("l1A", [D, 128], F32, kind="ExternalInput").ap()
    dram["l1B"] = nc.dram_tensor("l1B", [128, D], F32, kind="ExternalInput").ap()
    dram["i8c"] = nc.dram_tensor("i8c", [8, 128], BF16, kind="ExternalInput").ap()
    dram["bbase_c"] = nc.dram_tensor("bbase_c", [128, 32], F32,
                                     kind="ExternalInput").ap()
    if "ident" not in dram:
        dram["ident"] = nc.dram_tensor("ident", [128, 128], BF16,
                                       kind="ExternalInput").ap()
    if "epsin" not in dram:
        dram["epsin"] = nc.dram_tensor("epsin", [128, 1], F32,
                                       kind="ExternalInput").ap()
    dram["att2_d"] = nc.dram_tensor("att2_d", [C, D], BF16, kind="Internal").ap()
    dram["y_rows"] = nc.dram_tensor("y_rows", [C, D], F32,
                                    kind="ExternalOutput").ap()


def host_prep_ph3(inp):
    sh = {}
    sh["wbaseT"] = np.ascontiguousarray(np.asarray(inp["L1_ffn_W_base"]).T)
    sh["wgateT"] = np.ascontiguousarray(np.asarray(inp["L1_ffn_W_gate"]).T)
    A = np.asarray(inp["L1_ffn_A"])
    sh["l1A"] = np.ascontiguousarray(A.transpose(2, 0, 1).reshape(D, 128))
    B = np.asarray(inp["L1_ffn_B"])
    sh["l1B"] = np.ascontiguousarray(B.transpose(0, 2, 1).reshape(128, D))
    i8 = np.zeros((8, 128), np.float32)
    for e in range(8):
        i8[e, 16 * e:16 * (e + 1)] = 1.0
    sh["i8c"] = i8.astype(BF)
    sh["bbase_c"] = np.ascontiguousarray(
        np.asarray(inp["L1_ffn_b_base"]).reshape(32, 128).T)
    return sh


def build_ph3():
    nc = bass.Bass("TRN2", target_bir_lowering=False, debug=False)
    dram = {}
    declare_ph3(nc, dram, standalone=True)
    es = contextlib.ExitStack()
    with es:
        ctx = Ctx(nc, es)
        ctx.sems(*PH1_SEMS)
        epst = ctx.alloc(1, F32)
        dram["epsc"] = epst
        with nc.Block() as block:
            @block.sync
            def _(sync):
                sync.dma_start(out=epst, in_=dram["epsin"]).then_inc(ctx.sem["CN"], 16)
            ctx.cnt["CN"] += 16
            run_twopass(ctx, block, lambda env: emit_ph3(env, dram))

            @block.gpsimd
            def _(gpsimd):
                gpsimd.wait_ge(ctx.sem["SO0"], ctx.cnt["SO0"])
                gpsimd.wait_ge(ctx.sem["SO1"], ctx.cnt["SO1"])
    return nc


def build_full(dbg=False, nph=4):
    nc = bass.Bass("TRN2", target_bir_lowering=False, debug=False)
    dram = {}
    if dbg:
        dram["xres1"] = nc.dram_tensor("xres1", [C, D], F32, kind="ExternalOutput").ap()
        dram["xres2"] = nc.dram_tensor("xres2", [C, D], F32, kind="ExternalOutput").ap()
        if nph >= 3:
            dram["xres3"] = nc.dram_tensor("xres3", [C, D], F32, kind="ExternalOutput").ap()
    declare_ph0(nc, dram, standalone=False)
    declare_ph1(nc, dram, standalone=(nph == 2) and not dbg)
    if nph >= 3:
        declare_ph2(nc, dram, standalone=False)
    if nph >= 4:
        declare_ph3(nc, dram, standalone=False)
    es = contextlib.ExitStack()
    with es:
        ctx = Ctx(nc, es)
        ctx.sems(*PH1_SEMS)
        epst = ctx.alloc(1, F32)
        dram["epsc"] = epst
        with nc.Block() as block:
            @block.sync
            def _(sync):
                sync.dma_start(out=epst, in_=dram["epsin"]).then_inc(ctx.sem["CN"], 16)
            ctx.cnt["CN"] += 16

            def emit(env):
                emit_ph0(env, dram)
                f1 = {"SO0": ctx.cnt["SO0"], "SO1": ctx.cnt["SO1"]}
                emit_ph1(env, dram, fence=f1)
                emit_ph1_down(env, dram)
                if nph >= 3:
                    f2 = {"SO0": ctx.cnt["SO0"], "SO1": ctx.cnt["SO1"]}
                    emit_ph2(env, dram, fence=f2)
                if nph >= 4:
                    f3 = {"SO0": ctx.cnt["SO0"], "SO1": ctx.cnt["SO1"]}
                    emit_ph3(env, dram, fence=f3)
            run_twopass(ctx, block, emit)

            @block.gpsimd
            def _(gpsimd):
                gpsimd.wait_ge(ctx.sem["SO0"], ctx.cnt["SO0"])
                gpsimd.wait_ge(ctx.sem["SO1"], ctx.cnt["SO1"])
    return nc

def kernel(**inputs):
    ident = np.eye(128, dtype=np.float32).astype(BF)
    eps = np.full((128, 1), 1e-5, np.float32)
    sh0, pc0 = host_prep(inputs)
    sh1, pc1 = host_prep_ph1(inputs)
    sh2, pc2 = host_prep_ph2(inputs)
    sh3 = host_prep_ph3(inputs)
    nc = build_full()
    in_maps = []
    for c in range(NCORES):
        m = {}
        m.update(pc0[c])
        m.update(pc1[c])
        m.update(pc2[c])
        for k in ["wq_r", "wk_r", "wv_r", "bq", "bk", "bv"]:
            m[k] = sh0[k]
        m["Gc"] = sh1["Gc"]
        m["sel"] = sh1["sel"]
        for k in ["l1wq", "l1wk", "l1wv", "l1wo", "identf"]:
            m[k] = sh2[k]
        m.update(sh3)
        m["ident"] = ident
        m["epsin"] = eps
        in_maps.append(m)
    res = run_bass_kernel_spmd(nc, in_maps, core_ids=list(range(NCORES)))
    y = np.empty((T, D), np.float32)
    for c in range(NCORES):
        y[zz_rows(c)] = np.asarray(res.results[c]["y_rows"])
    return y.reshape(1, T, D).astype(np.float32)


# version 19
# speedup vs baseline: 1.3228x; 1.0333x over previous
import contextlib
import numpy as np
import ml_dtypes
import concourse.bass as bass
import concourse.mybir as mybir
from concourse.bass_utils import run_bass_kernel_spmd

F32 = mybir.dt.float32
BF16 = mybir.dt.bfloat16
U8 = mybir.dt.uint8
AF = mybir.ActivationFunctionType
ALU = mybir.AluOpType
AX = mybir.AxisListType.X
BF = ml_dtypes.bfloat16

NCORES = 8
T = 8192
D = 4096
C = T // NCORES
ISQ = 0.08838834764831843


def zz_rows(c):
    return np.concatenate([np.arange(128) + 128 * (8 * b + c)
                           for b in range(8)])


def host_prep(inp):
    x = np.ascontiguousarray(np.asarray(inp["L0_attn_x"]).reshape(T, D))
    sh = {}
    Wq = np.asarray(inp["L0_attn_Wq"])
    sh["wq_r"] = np.ascontiguousarray(
        Wq.reshape(D, 128, 8, 4).transpose(0, 2, 3, 1).reshape(D, D))
    Wk = np.asarray(inp["L0_attn_Wk"])
    sh["wk_r"] = np.ascontiguousarray(Wk.transpose(0, 2, 1).reshape(D, 1024))
    sh["wv_r"] = np.ascontiguousarray(
        np.asarray(inp["L0_attn_Wv"]).transpose(0, 2, 1).reshape(D, 1024))
    sh["bq"] = np.ascontiguousarray(np.asarray(inp["L0_attn_bq"]))
    sh["bk"] = np.ascontiguousarray(np.asarray(inp["L0_attn_bk"]))
    sh["bv"] = np.ascontiguousarray(np.asarray(inp["L0_attn_bv"]))
    sh["ident"] = np.eye(128, dtype=np.float32).astype(BF)
    sh["epsin"] = np.full((128, 1), 1e-5, np.float32)
    percore = [{"x_rows": np.ascontiguousarray(x[zz_rows(c)])}
               for c in range(NCORES)]
    return sh, percore



def host_prep_ph1(inp):
    qwgu = np.asarray(inp["L0_ffn_qweight_gu"])
    qzgu = np.asarray(inp["L0_ffn_qzeros_gu"])
    sgu = np.asarray(inp["L0_ffn_scales_gu"])
    qwdn = np.asarray(inp["L0_ffn_qweight_dn"])
    qzdn = np.asarray(inp["L0_ffn_qzeros_dn"])
    sdn = np.asarray(inp["L0_ffn_scales_dn"])
    qwgu_B = np.ascontiguousarray(qwgu).view(np.uint8).reshape(D, 11008)
    qzgu_B = np.ascontiguousarray(qzgu).view(np.uint8).reshape(32, 11008)
    qwdn_B = np.ascontiguousarray(qwdn).view(np.uint8).reshape(11008, 2048)
    qzdn_B = np.ascontiguousarray(qzdn).view(np.uint8).reshape(86, 2048)
    HSL = 1376
    BSL = HSL // 2
    pc = []
    for c in range(NCORES):
        m = {}
        m["qwgu_b"] = np.ascontiguousarray(
            np.concatenate([qwgu_B[:, BSL * c:BSL * (c + 1)],
                            qwgu_B[:, 5504 + BSL * c:5504 + BSL * (c + 1)]], axis=1))
        m["qzgu_b"] = np.ascontiguousarray(
            np.concatenate([qzgu_B[:, BSL * c:BSL * (c + 1)],
                            qzgu_B[:, 5504 + BSL * c:5504 + BSL * (c + 1)]], axis=1))
        m["sgu_c"] = np.ascontiguousarray(
            np.concatenate([sgu[:, HSL * c:HSL * (c + 1)],
                            sgu[:, 11008 + HSL * c:11008 + HSL * (c + 1)]], axis=1))
        m["qwdn_b"] = np.ascontiguousarray(qwdn_B[HSL * c:HSL * (c + 1)])
        g_lo = (HSL * c) // 128
        g_hi = (HSL * (c + 1) - 1) // 128
        ng = g_hi - g_lo + 1
        qz = np.zeros((12, 2048), np.uint8)
        qz[:ng] = qzdn_B[g_lo:g_hi + 1]
        m["qzdn_b"] = qz
        sd = np.zeros((12, 4096), np.float32)
        sd[:ng] = sdn[g_lo:g_hi + 1]
        m["sdn_c"] = sd
        Ei = np.zeros((11 * 128, 16), np.float32)
        Eid = np.zeros((12, 11 * 128), np.float32)
        for r in range(HSL):
            gl = (HSL * c + r) // 128 - g_lo
            Ei[r, gl] = 1.0
            Eid[gl, r] = 1.0
        m["Ei"] = Ei.astype(BF)
        m["Eid"] = np.ascontiguousarray(
            Eid.reshape(12, 11, 128).transpose(1, 0, 2)).astype(BF)
        pc.append(m)
    shared = {}
    Gc = np.zeros((D, 32), np.float32)
    for k in range(D):
        Gc[k, k // 128] = 1.0
    shared["Gc"] = Gc.astype(BF)
    sel = np.zeros((32, 32 * 128), np.float32)
    for kk in range(32):
        sel[kk, 128 * kk:128 * (kk + 1)] = 1.0
    shared["sel"] = sel.astype(BF)
    return shared, pc


class Ctx:
    def __init__(self, nc, es):
        self.nc = nc
        self.es = es
        self.sem = {}
        self.cnt = {}
        arena_cols = 51200
        self.arena = es.enter_context(nc.sbuf_tensor("arena", [128, arena_cols], F32))
        self.aoff = 0
        self.amax = arena_cols * 4
        self.marks = []
        self.psb = [es.enter_context(nc.psum_tensor(f"psb{i}", [128, 512], F32))
                    for i in range(8)]

    def sems(self, *names):
        for n in names:
            self.sem[n] = self.es.enter_context(self.nc.semaphore(n))
            self.cnt[n] = 0

    def alloc(self, cols, dtype):
        sz = {F32: 4, BF16: 2, U8: 1}[dtype]
        nbytes = cols * sz
        self.aoff = (self.aoff + 63) & ~63
        assert self.aoff + nbytes <= self.amax, f"SBUF arena overflow at {self.aoff + nbytes}"
        st = self.aoff // 4
        n32 = (nbytes + 3) // 4
        ap = self.arena[:, st:st + n32]
        self.aoff += n32 * 4
        if dtype != F32:
            ap = ap.bitcast(dtype)
        return ap

    def push(self):
        self.marks.append(self.aoff)

    def pop(self):
        self.aoff = self.marks.pop()



def emit_ph0(env, dram):
    ctx = env.ctx
    cnt = ctx.cnt
    psb = ctx.psb
    CN, XA, XB, W0, W1 = "CN", "XA", "XB", "W0", "W1"
    TS0, TS1, QA, QB = "TS0", "TS1", "QA", "QB"
    R0, R1, SO0, SO1, GD, TE, VE, SE = "R0", "R1", "SO0", "SO1", "GD", "TE", "VE", "SE"
    XS = [XA, XB]
    WS = [W0, W1]
    TSs = [TS0, TS1]
    QS = [QA, QB]
    RS = [R0, R1]
    SOS = [SO0, SO1]

    ctx.push()
    identt = ctx.alloc(128, BF16)
    stats = ctx.alloc(64, F32)
    bqs = ctx.alloc(32, F32)
    bqt = ctx.alloc(8, F32)
    bks = ctx.alloc(8, F32)
    bvs = ctx.alloc(8, F32)
    varsum = ctx.alloc(8, F32)
    qT = ctx.alloc(8 * 1024, BF16)
    kT = ctx.alloc(8 * 1024, BF16)
    vT = ctx.alloc(8 * 1024, BF16)
    epst = dram["epsc"]

    ctx.push()
    xlnT = ctx.alloc(32 * 1024, BF16)
    ctx.push()
    xr = [ctx.alloc(4096, F32), ctx.alloc(4096, F32)]
    lnscr = ctx.alloc(4096, BF16)

    V0 = cnt["VE"]; S0 = cnt["SE"]; E0 = cnt["TE"]
    CN0 = cnt["CN"]; XA0 = cnt["XA"]; XB0 = cnt["XB"]
    XBASE = [XA0, XB0]
    def ve_s1(tt, k):
        return V0 + 1 + 9 * tt + k

    @env.sync
    def _(sync):
        sync.dma_start(out=identt, in_=dram["ident"]).then_inc(CN, 16)
        sync.dma_start(out=bqs, in_=dram["bq"]).then_inc(CN, 16)
        sync.dma_start(out=bks, in_=dram["bk"]).then_inc(CN, 16)
        sync.dma_start(out=bvs, in_=dram["bv"]).then_inc(CN, 16)
        for tt in range(8):
            if tt >= 2:
                sync.wait(VE, val=ve_s1(tt - 2, 5))
                sync.wait(SE, val=S0 + 2 * (tt - 2) + 1)
            sync.dma_start(out=xr[tt % 2],
                           in_=dram["x_rows"][128 * tt:128 * (tt + 1)]
                           ).then_inc(XS[tt % 2], 16)
    assert cnt["CN"] == (CN0 + 16 * 4), ("cnt CN", cnt["CN"], CN0 + 16 * 4)
    assert cnt["XA"] == (XA0 + 16 * 4), ("cnt XA", cnt["XA"], XA0 + 16 * 4)
    assert cnt["XB"] == (XB0 + 16 * 4), ("cnt XB", cnt["XB"], XB0 + 16 * 4)

    @env.vector
    def _(vector):
        vector.wait(CN, val=CN0 + 64)
        vector.memzero(varsum)
        vector.tensor_reduce(out=bqt.rearrange("p (a b) -> p a b", b=1),
                             in_=bqs.rearrange("p (a b) -> p a b", b=4),
                             axis=AX, op=ALU.add).then_inc(VE, 1)
        for tt in range(8):
            mus = stats[:, 8 * tt + 0:8 * tt + 1]
            nmu = stats[:, 8 * tt + 1:8 * tt + 2]
            rstd = stats[:, 8 * tt + 2:8 * tt + 3]
            sqv = stats[:, 8 * tt + 3:8 * tt + 4]
            nmur = stats[:, 8 * tt + 4:8 * tt + 5]
            vector.wait(XS[tt % 2], val=XBASE[tt % 2] + 16 * (tt // 2 + 1))
            vector.reduce_sum(out=mus, in_=xr[tt % 2], axis=AX).then_inc(VE, 1)
            vector.wait(VE, val=ve_s1(tt, 1))
            vector.tensor_scalar_mul(nmu, mus, -1.0 / 4096.0).then_inc(VE, 1)
            vector.wait(SE, val=S0 + 2 * tt + 2)
            vector.reciprocal(rstd, sqv).then_inc(VE, 1)
            vector.wait(VE, val=ve_s1(tt, 3))
            vector.tensor_mul(nmur, nmu, rstd).then_inc(VE, 1)
            vector.wait(VE, val=ve_s1(tt, 4))
            vector.tensor_scalar(out=lnscr, in0=xr[tt % 2], scalar1=rstd,
                                 scalar2=nmur, op0=ALU.mult, op1=ALU.add
                                 ).then_inc(VE, 1)
            for b in range(4):
                vector.wait(TE, val=E0 + 4 * tt + b + 1)
                vector.tensor_copy(
                    out=xlnT.rearrange("p (kk t) -> p kk t", t=1024)
                        [:, 8 * b:8 * b + 8, 128 * tt:128 * (tt + 1)],
                    in_=psb[b][:, :].bitcast(BF16).rearrange("p (kk t) -> p kk t", t=128),
                ).then_inc(VE, 1)
    assert cnt["VE"] == (V0 + 1 + 72), ("cnt VE", cnt["VE"], V0 + 1 + 72)

    @env.scalar
    def _(scalar):
        for tt in range(8):
            nmu = stats[:, 8 * tt + 1:8 * tt + 2]
            sqv = stats[:, 8 * tt + 3:8 * tt + 4]
            scalar.wait(VE, val=ve_s1(tt, 2))
            scalar.activation(out=lnscr, in_=xr[tt % 2], func=AF.Square, bias=nmu,
                              scale=1.0, accum_out=varsum[:, tt:tt + 1]
                              ).then_inc(SE, 1)
            scalar.wait(SE, val=S0 + 2 * tt + 1)
            scalar.activation(out=sqv, in_=varsum[:, tt:tt + 1], func=AF.Sqrt,
                              scale=1.0 / 4096.0, bias=epst).then_inc(SE, 1)
    assert cnt["SE"] == (S0 + 16), ("cnt SE", cnt["SE"], S0 + 16)

    @env.tensor
    def _(tensor):
        for tt in range(8):
            tensor.wait(VE, val=ve_s1(tt, 5))
            for b in range(4):
                if 4 * tt + b >= 4:
                    tpp, bp = divmod(4 * tt + b - 4, 4)
                    tensor.wait(VE, val=ve_s1(tpp, 6 + bp))
                pbf = psb[b][:, :].bitcast(BF16)
                for u in range(8):
                    kk = 8 * b + u
                    ins = tensor.transpose(pbf[:, 128 * u:128 * (u + 1)],
                                           lnscr[:, 128 * kk:128 * (kk + 1)], identt)
                ins.then_inc(TE, 1)
    assert cnt["TE"] == (E0 + 32), ("cnt TE", cnt["TE"], E0 + 32)

    ctx.pop()

    ctx.push()
    wbuf = [ctx.alloc(32 * 512, BF16), ctx.alloc(32 * 512, BF16)]
    E1 = cnt["TE"]; V1 = cnt["VE"]
    WB = [cnt["W0"], cnt["W1"]]

    slabs = ([("q", h) for h in range(8)] + [("k", h) for h in range(8)] +
             [("v", h) for h in range(8)])

    @env.gpsimd
    def _(gpsimd):
        gpsimd.wait(XA, val=cnt["XA"])
        gpsimd.wait(XB, val=cnt["XB"])
        gpsimd.wait(TE, val=E1)
        gpsimd.wait(VE, val=V1)
        gpsimd.wait(SE, val=cnt["SE"])
        for i, (kind, h) in enumerate(slabs):
            if i >= 2:
                gpsimd.wait(TE, val=E1 + 2 * (i - 1))
            w = {"q": dram["wq_r"], "k": dram["wk_r"], "v": dram["wv_r"]}[kind]
            ncols = 512 if kind == "q" else 128
            dst = wbuf[i % 2].rearrange("p (kk n) -> p kk n", n=512)[:, :, 0:ncols]
            gpsimd.dma_start(
                out=dst,
                in_=w[:, ncols * h:ncols * (h + 1)]
                    .rearrange("(kk p) n -> p kk n", p=128)).then_inc(WS[i % 2], 16)
    assert cnt["W0"] == (WB[0] + 16 * 12), ("cnt W0", cnt["W0"], WB[0] + 16 * 12)
    assert cnt["W1"] == (WB[1] + 16 * 12), ("cnt W1", cnt["W1"], WB[1] + 16 * 12)

    @env.tensor
    def _(tensor):
        tensor.wait(VE, val=V1)
        for i, (kind, h) in enumerate(slabs):
            tensor.wait(WS[i % 2], val=WB[i % 2] + 16 * (i // 2 + 1))
            for tb in range(2):
                u = 2 * i + tb
                if u >= 2:
                    tensor.wait(VE, val=V1 + u - 1)
                njs = 4 if kind == "q" else 1
                n = 0
                for j in range(njs):
                    for kk in range(32):
                        n += 1
                        ins = tensor.matmul(
                            psb[u % 2][:, :],
                            wbuf[i % 2].rearrange("p (kk n) -> p kk n", n=512)
                                [:, kk, 128 * j:128 * (j + 1)],
                            xlnT[:, 1024 * kk + 512 * tb:1024 * kk + 512 * (tb + 1)],
                            start=(n == 1), stop=(n == njs * 32))
                ins.then_inc(TE, 1)
    assert cnt["TE"] == (E1 + 48), ("cnt TE", cnt["TE"], E1 + 48)

    @env.vector
    def _(vector):
        for i, (kind, h) in enumerate(slabs):
            dstf = {"q": qT, "k": kT, "v": vT}[kind]
            biasf = {"q": bqt, "k": bks, "v": bvs}[kind]
            for tb in range(2):
                u = 2 * i + tb
                vector.wait(TE, val=E1 + u + 1)
                vector.tensor_scalar(
                    out=dstf[:, 1024 * h + 512 * tb:1024 * h + 512 * (tb + 1)],
                    in0=psb[u % 2][:, :], scalar1=biasf[:, h:h + 1],
                    scalar2=None, op0=ALU.add).then_inc(VE, 1)
    assert cnt["VE"] == (V1 + 48), ("cnt VE", cnt["VE"], V1 + 48)

    ctx.pop()
    ctx.pop()

    ctx.push()
    tokstage = [ctx.alloc(1024, BF16), ctx.alloc(1024, BF16)]
    E2 = cnt["TE"]; V2 = cnt["VE"]
    TB = [cnt["TS0"], cnt["TS1"]]

    @env.tensor
    def _(tensor):
        tensor.wait(VE, val=V2)
        for idx in range(16):
            src = qT if idx < 8 else kT
            tt = idx % 8
            bank = 4 + idx % 2
            if idx >= 2:
                tensor.wait(VE, val=V2 + idx - 1)
            pbf = psb[bank][:, :].bitcast(BF16)
            for pt in range(8):
                ins = tensor.transpose(
                    pbf[:, 128 * pt:128 * (pt + 1)],
                    src[:, 1024 * pt + 128 * tt:1024 * pt + 128 * (tt + 1)], identt)
            ins.then_inc(TE, 1)
    assert cnt["TE"] == (E2 + 16), ("cnt TE", cnt["TE"], E2 + 16)

    @env.vector
    def _(vector):
        for idx in range(16):
            vector.wait(TE, val=E2 + idx + 1)
            if idx >= 2:
                vector.wait(TSs[idx % 2], val=TB[idx % 2] + 16 * (idx // 2))
            vector.tensor_copy(out=tokstage[idx % 2],
                               in_=psb[4 + idx % 2][:, :].bitcast(BF16)).then_inc(VE, 1)
    assert cnt["VE"] == (V2 + 16), ("cnt VE", cnt["VE"], V2 + 16)

    @env.gpsimd
    def _(gpsimd):
        for idx in range(16):
            dstd = dram["qtok_d"] if idx < 8 else dram["ktok_d"]
            tt = idx % 8
            gpsimd.wait(VE, val=V2 + idx + 1)
            gpsimd.dma_start(out=dstd[128 * tt:128 * (tt + 1)],
                             in_=tokstage[idx % 2]).then_inc(TSs[idx % 2], 16)
    assert cnt["TS0"] == (TB[0] + 16 * 8), ("cnt TS0", cnt["TS0"], TB[0] + 16 * 8)
    assert cnt["TS1"] == (TB[1] + 16 * 8), ("cnt TS1", cnt["TS1"], TB[1] + 16 * 8)
    ctx.pop()

    ctx.push()
    bd3 = ctx.alloc(32 * 1024, BF16)
    attn = ctx.alloc(16 * 1024, BF16)
    attnT = ctx.alloc(16 * 1024, BF16)
    sums = ctx.alloc(128, F32)
    recip = ctx.alloc(128, F32)
    nmx = ctx.alloc(64, F32)

    E3 = cnt["TE"]; V3 = cnt["VE"]; S3c = cnt["SE"]; G3 = cnt["GD"]
    QB0 = [cnt["QA"], cnt["QB"]]
    TS_done = [cnt["TS0"], cnt["TS1"]]
    TE_CH = 32 + 16 + 32
    VE_CH = 146
    SE_CH = 32

    def te4(ch, k):
        return E3 + TE_CH * ch + k

    def ve4(ch, k):
        return V3 + VE_CH * ch + k

    @env.sync
    def _(sync):
        sync.wait(TS0, val=TS_done[0])
        sync.wait(TS1, val=TS_done[1])
        for ch in range(8):
            if ch >= 1:
                sync.wait(TE, val=te4(ch - 1, 32))
            for dstt, srcd in ((bd3[0:8, 0:16384], dram["qtok_d"]),
                               (bd3[0:8, 16384:32768], dram["ktok_d"])):
                sync.dma_start(
                    out=dstt.rearrange("p (t d) -> p t d", d=128),
                    in_=srcd[128 * ch:128 * (ch + 1)]
                        .rearrange("t (h d) -> h t d", d=128)
                    ).then_inc(QS[ch % 2], 16)
    assert cnt["QA"] == (QB0[0] + 16 * 8), ("cnt QA", cnt["QA"], QB0[0] + 16 * 8)
    assert cnt["QB"] == (QB0[1] + 16 * 8), ("cnt QB", cnt["QB"], QB0[1] + 16 * 8)

    @env.tensor
    def _(tensor):
        for ch in range(8):
            tensor.wait(QS[ch % 2], val=QB0[ch % 2] + 32 * (ch // 2 + 1))
            for g in range(32):
                nglob = 32 * ch + g
                if nglob >= 2:
                    tensor.wait(SE, val=S3c + nglob - 1)
                for tau in range(4):
                    t = 4 * g + tau
                    ins = tensor.matmul(
                        psb[g % 2][:, 128 * tau:128 * (tau + 1)],
                        bd3[0:8, 128 * t:128 * (t + 1)],
                        bd3[0:8, 16384 + 128 * t:16384 + 128 * (t + 1)],
                        start=True, stop=True)
                ins.then_inc(TE, 1)
            tensor.wait(VE, val=ve4(ch, 98))
            for bt in range(16):
                nglob = 16 * ch + bt
                if nglob >= 2:
                    chp, btp = divmod(nglob - 2, 16)
                    tensor.wait(VE, val=ve4(chp, 98 + btp + 1))
                pbf = psb[2 + bt % 2][:, :].bitcast(BF16)
                for u in range(8):
                    t = 8 * bt + u
                    ins = tensor.transpose(pbf[:, 128 * u:128 * (u + 1)],
                                           attn[:, 128 * t:128 * (t + 1)], identt)
                ins.then_inc(TE, 1)
            for g in range(32):
                nglob = 32 * ch + g
                if nglob >= 2:
                    chp, gp = divmod(nglob - 2, 32)
                    tensor.wait(VE, val=ve4(chp, 114 + gp + 1))
                if g == 0 and ch > 0:
                    tensor.wait(GD, val=G3 + 16 * ch)
                bt_need = (4 * g + 3) // 8 + 1
                tensor.wait(VE, val=ve4(ch, 98 + bt_need))
                for tau in range(4):
                    t = 4 * g + tau
                    glob_t = 128 * ch + t
                    v_t = vT.rearrange("p (h t) -> p h t", t=1024)[:, :, glob_t]
                    ins = tensor.matmul(
                        psb[4 + g % 2][0:8, 128 * tau:128 * (tau + 1)],
                        v_t, attnT[:, 128 * t:128 * (t + 1)],
                        start=True, stop=True)
                ins.then_inc(TE, 1)
    assert cnt["TE"] == (E3 + 8 * TE_CH), ("cnt TE", cnt["TE"], E3 + 8 * TE_CH)

    @env.vector
    def _(vector):
        for ch in range(8):
            for g in range(32):
                vector.wait(TE, val=te4(ch, g + 1))
                vector.tensor_reduce(out=nmx[:, 2 * g:2 * g + 1], in_=psb[g % 2][:, :],
                                     axis=AX, op=ALU.max, negate=True).then_inc(VE, 1)
                vector.wait(VE, val=ve4(ch, 3 * g + 1))
                vector.tensor_scalar_mul(nmx[:, 2 * g + 1:2 * g + 2],
                                         nmx[:, 2 * g:2 * g + 1], ISQ).then_inc(VE, 1)
                vector.wait(SE, val=S3c + 32 * ch + g + 1)
                vector.tensor_reduce(
                    out=sums.rearrange("p (g t o) -> p g t o", t=4, o=1)[:, g],
                    in_=attn.rearrange("p (g t e) -> p g t e", t=4, e=128)[:, g],
                    axis=AX, op=ALU.add).then_inc(VE, 1)
            vector.wait(VE, val=ve4(ch, 96))
            vector.reciprocal(recip, sums).then_inc(VE, 1)
            vector.wait(VE, val=ve4(ch, 97))
            vector.tensor_tensor(
                out=attn.rearrange("p (t e) -> p t e", e=128),
                in0=attn.rearrange("p (t e) -> p t e", e=128),
                in1=recip.rearrange("p (t o) -> p t o", o=1)
                    .broadcast_to([128, 128, 128]),
                op=ALU.mult).then_inc(VE, 1)
            for bt in range(16):
                vector.wait(TE, val=te4(ch, 32 + bt + 1))
                vector.tensor_copy(out=attnT[:, 1024 * bt:1024 * (bt + 1)],
                                   in_=psb[2 + bt % 2][:, :].bitcast(BF16)).then_inc(VE, 1)
            for g in range(32):
                if g == 0 and ch > 0:
                    vector.wait(GD, val=G3 + 16 * ch)
                vector.wait(TE, val=te4(ch, 48 + g + 1))
                vector.tensor_copy(out=bd3[64:72, 512 * g:512 * (g + 1)],
                                   in_=psb[4 + g % 2][0:8, :]).then_inc(VE, 1)
    assert cnt["VE"] == (V3 + 8 * VE_CH), ("cnt VE", cnt["VE"], V3 + 8 * VE_CH)

    @env.scalar
    def _(scalar):
        for ch in range(8):
            for g in range(32):
                scalar.wait(VE, val=ve4(ch, 3 * g + 2))
                scalar.activation(out=attn[:, 512 * g:512 * (g + 1)],
                                  in_=psb[g % 2][:, :], func=AF.Exp,
                                  scale=ISQ, bias=nmx[:, 2 * g + 1:2 * g + 2]
                                  ).then_inc(SE, 1)
    assert cnt["SE"] == (S3c + 8 * SE_CH), ("cnt SE", cnt["SE"], S3c + 8 * SE_CH)

    @env.gpsimd
    def _(gpsimd):
        for ch in range(8):
            gpsimd.wait(VE, val=ve4(ch, VE_CH))
            gpsimd.dma_start(
                out=dram["otok_d"][128 * ch:128 * (ch + 1)]
                    .rearrange("t (h d) -> h t d", d=128),
                in_=bd3[64:72, 0:16384].rearrange("p (t d) -> p t d", d=128)).then_inc(GD, 16)
    assert cnt["GD"] == (G3 + 16 * 8), ("cnt GD", cnt["GD"], G3 + 16 * 8)
    ctx.pop()

    ctx.push()
    xr2 = [ctx.alloc(4096, F32), ctx.alloc(4096, F32)]
    ot2 = [ctx.alloc(1024, BF16), ctx.alloc(1024, BF16)]
    xout = [ctx.alloc(4096, F32), ctx.alloc(4096, F32)]
    V5 = cnt["VE"]; G5 = cnt["GD"]
    RB = [cnt["R0"], cnt["R1"]]
    SOB = [cnt["SO0"], cnt["SO1"]]

    @env.sync
    def _(sync):
        sync.wait(GD, val=G5)
        for tt in range(8):
            if tt >= 2:
                sync.wait(VE, val=V5 + tt - 1)
            sync.dma_start(out=xr2[tt % 2],
                           in_=dram["x_rows"][128 * tt:128 * (tt + 1)]
                           ).then_inc(RS[tt % 2], 16)
            sync.dma_start(out=ot2[tt % 2],
                           in_=dram["otok_d"][128 * tt:128 * (tt + 1)]
                           ).then_inc(RS[tt % 2], 16)
    assert cnt["R0"] == (RB[0] + 16 * 8), ("cnt R0", cnt["R0"], RB[0] + 16 * 8)
    assert cnt["R1"] == (RB[1] + 16 * 8), ("cnt R1", cnt["R1"], RB[1] + 16 * 8)

    @env.vector
    def _(vector):
        for tt in range(8):
            vector.wait(RS[tt % 2], val=RB[tt % 2] + 32 * (tt // 2 + 1))
            if tt >= 2:
                vector.wait(SOS[tt % 2], val=SOB[tt % 2] + 16 * (tt // 2))
            vector.tensor_tensor(
                out=xout[tt % 2].rearrange("p (d h j) -> p d h j", h=8, j=4),
                in0=xr2[tt % 2].rearrange("p (d h j) -> p d h j", h=8, j=4),
                in1=ot2[tt % 2].rearrange("p (h d) -> p h d", d=128)
                    .transpose([0, 2, 1]).unsqueeze(3).broadcast_to([128, 128, 8, 4]),
                op=ALU.add).then_inc(VE, 1)
    assert cnt["VE"] == (V5 + 8), ("cnt VE", cnt["VE"], V5 + 8)

    @env.gpsimd
    def _(gpsimd):
        for tt in range(8):
            gpsimd.wait(VE, val=V5 + tt + 1)
            gpsimd.dma_start(out=dram["xres1"][128 * tt:128 * (tt + 1)],
                             in_=xout[tt % 2]).then_inc(SOS[tt % 2], 16)
    assert cnt["SO0"] == (SOB[0] + 16 * 4), ("cnt SO0", cnt["SO0"], SOB[0] + 16 * 4)
    assert cnt["SO1"] == (SOB[1] + 16 * 4), ("cnt SO1", cnt["SO1"], SOB[1] + 16 * 4)
    ctx.pop()
    ctx.pop()


def declare_ph0(nc, dram, standalone=True):
    dram["x_rows"] = nc.dram_tensor("x_rows", [C, D], F32, kind="ExternalInput").ap()
    if "xres1" in dram:
        _skip_xres1 = True
    dram["wq_r"] = nc.dram_tensor("wq_r", [D, D], F32, kind="ExternalInput").ap()
    dram["wk_r"] = nc.dram_tensor("wk_r", [D, 1024], F32, kind="ExternalInput").ap()
    dram["wv_r"] = nc.dram_tensor("wv_r", [D, 1024], F32, kind="ExternalInput").ap()
    dram["bq"] = nc.dram_tensor("bq", [128, 32], F32, kind="ExternalInput").ap()
    dram["bk"] = nc.dram_tensor("bk", [128, 8], F32, kind="ExternalInput").ap()
    dram["bv"] = nc.dram_tensor("bv", [128, 8], F32, kind="ExternalInput").ap()
    dram["ident"] = nc.dram_tensor("ident", [128, 128], BF16, kind="ExternalInput").ap()
    dram["epsin"] = nc.dram_tensor("epsin", [128, 1], F32, kind="ExternalInput").ap()
    dram["qtok_d"] = nc.dram_tensor("qtok_d", [C, 1024], BF16, kind="Internal").ap()
    dram["ktok_d"] = nc.dram_tensor("ktok_d", [C, 1024], BF16, kind="Internal").ap()
    dram["otok_d"] = nc.dram_tensor("otok_d", [C, 1024], BF16, kind="Internal").ap()
    if "xres1" not in dram:
        dram["xres1"] = nc.dram_tensor(
            "xres1", [C, D], F32,
            kind="ExternalOutput" if standalone else "Internal").ap()


def build_ph0():
    nc = bass.Bass("TRN2", target_bir_lowering=False, debug=False)
    dram = {}
    declare_ph0(nc, dram, standalone=True)
    es = contextlib.ExitStack()
    with es:
        ctx = Ctx(nc, es)
        ctx.sems(*PH1_SEMS)
        epst = ctx.alloc(1, F32)
        dram["epsc"] = epst
        with nc.Block() as block:
            @block.sync
            def _(sync):
                sync.dma_start(out=epst, in_=dram["epsin"]).then_inc(ctx.sem["CN"], 16)
            ctx.cnt["CN"] += 16
            run_twopass(ctx, block, lambda env: emit_ph0(env, dram))

            @block.gpsimd
            def _(gpsimd):
                gpsimd.wait_ge(ctx.sem["SO0"], ctx.cnt["SO0"])
                gpsimd.wait_ge(ctx.sem["SO1"], ctx.cnt["SO1"])
    return nc


class _MockInstr:
    def __init__(self, env):
        self.env = env

    def then_inc(self, semname, k=1):
        self.env.ctx.cnt[semname] += k
        return self


class _MockEngine:
    def __init__(self, env):
        self._env = env

    def wait(self, semname, key=None, val=None):
        pass

    def __getattr__(self, name):
        env = self._env
        def f(*a, **k):
            return _MockInstr(env)
        return f


class _RealInstr:
    def __init__(self, env, instr):
        self.env = env
        self.instr = instr

    def then_inc(self, semname, k=1):
        self.instr.then_inc(self.env.ctx.sem[semname], k)
        self.env.ctx.cnt[semname] += k
        return self


class _RealEngine:
    def __init__(self, env, eng):
        self._env = env
        self._eng = eng

    def wait(self, semname, key=None, val=None):
        if val is None:
            val = self._env.rec[key][semname]
        self._eng.wait_ge(self._env.ctx.sem[semname], val)

    def __getattr__(self, name):
        env = self._env
        m = getattr(self._eng, name)
        def f(*a, **k):
            return _RealInstr(env, m(*a, **k))
        return f


class Env:

    def __init__(self, ctx, block):
        self.ctx = ctx
        self.block = block
        self.rec = {}
        self.pass1 = True

    def mark(self, key):
        if self.pass1:
            assert key not in self.rec, f"dup mark {key}"
            self.rec[key] = dict(self.ctx.cnt)
        else:
            for s, v in self.rec[key].items():
                assert self.ctx.cnt[s] == v, \
                    f"pass2 divergence at {key}: {s} {self.ctx.cnt[s]} != {v}"

    def at(self, key, semname):
        return self.rec[key][semname]

    def _section(self, engname):
        env = self

        def deco(f):
            if env.pass1:
                f(_MockEngine(env))
            else:
                getattr(env.block, engname)(lambda eng: f(_RealEngine(env, eng)))
        return deco

    @property
    def sync(self):
        return self._section("sync")

    @property
    def vector(self):
        return self._section("vector")

    @property
    def scalar(self):
        return self._section("scalar")

    @property
    def tensor(self):
        return self._section("tensor")

    @property
    def gpsimd(self):
        return self._section("gpsimd")


def run_twopass(ctx, block, emit):
    env = Env(ctx, block)
    saved = dict(ctx.cnt)
    amark = ctx.aoff
    mstack = list(ctx.marks)
    emit(env)
    ctx.cnt.clear()
    ctx.cnt.update(saved)
    ctx.aoff = amark
    ctx.marks = mstack
    env.pass1 = False
    emit(env)
    return env



NSUB = 4
HSUB = 344
TBW = 512
NTB = 16
MT = [(0, 128), (128, 128), (256, 88)]


def emit_ph1(env, dram, fence=None):
    ctx = env.ctx
    psb = ctx.psb
    epst = dram["epsc"]

    ctx.push()
    identt = ctx.alloc(128, BF16)
    stats = ctx.alloc(64, F32)
    varsum = ctx.alloc(8, F32)
    xr = [ctx.alloc(4096, F32), ctx.alloc(4096, F32)]
    lnscr = ctx.alloc(4096, BF16)
    agst = [ctx.alloc(4096, BF16), ctx.alloc(4096, BF16)]

    @env.sync
    def _(sync):
        if fence:
            for s_, v_ in fence.items():
                sync.wait(s_, val=v_)
        sync.dma_start(out=identt, in_=dram["ident"]).then_inc("CN", 16)
        env.mark("p1.ident")
        for tt in range(8):
            if tt >= 2:
                sync.wait("VE", ("p1.norm", tt - 2))
                sync.wait("SE", ("p1.sq1", tt - 2))
            sync.dma_start(out=xr[tt % 2],
                           in_=dram["xres1"][128 * tt:128 * (tt + 1)]
                           ).then_inc("X" + "AB"[tt % 2], 16)
            env.mark(("p1.xload", tt))

    @env.vector
    def _(vector):
        vector.wait("CN", "p1.ident")
        vector.memzero(varsum)
        for tt in range(8):
            mus = stats[:, 8 * tt + 0:8 * tt + 1]
            nmu = stats[:, 8 * tt + 1:8 * tt + 2]
            rstd = stats[:, 8 * tt + 2:8 * tt + 3]
            sqv = stats[:, 8 * tt + 3:8 * tt + 4]
            nmur = stats[:, 8 * tt + 4:8 * tt + 5]
            vector.wait("X" + "AB"[tt % 2], ("p1.xload", tt))
            vector.reduce_sum(out=mus, in_=xr[tt % 2], axis=AX).then_inc("VE")
            env.mark(("p1.mus", tt))
            vector.wait("VE", ("p1.mus", tt))
            vector.tensor_scalar_mul(nmu, mus, -1.0 / 4096.0).then_inc("VE")
            env.mark(("p1.nmu", tt))
            vector.wait("SE", ("p1.sq2", tt))
            vector.reciprocal(rstd, sqv).then_inc("VE")
            env.mark(("p1.rstd", tt))
            vector.wait("VE", ("p1.rstd", tt))
            vector.tensor_mul(nmur, nmu, rstd).then_inc("VE")
            env.mark(("p1.nmur", tt))
            vector.wait("VE", ("p1.nmur", tt))
            vector.tensor_scalar(out=lnscr, in0=xr[tt % 2], scalar1=rstd,
                                 scalar2=nmur, op0=ALU.mult, op1=ALU.add
                                 ).then_inc("VE")
            env.mark(("p1.norm", tt))
            for b in range(4):
                vector.wait("TE", ("p1.tp", tt, b))
                if tt >= 2 and b == 0:
                    vector.wait("TS" + "01"[tt % 2], ("p1.agstore", tt - 2))
                vector.tensor_copy(
                    out=agst[tt % 2].rearrange("p (kk t) -> p kk t", t=128)
                        [:, 8 * b:8 * b + 8, :],
                    in_=psb[b][:, :].bitcast(BF16)
                        .rearrange("p (kk t) -> p kk t", t=128),
                ).then_inc("VE")
                env.mark(("p1.evac", tt, b))

    @env.scalar
    def _(scalar):
        for tt in range(8):
            nmu = stats[:, 8 * tt + 1:8 * tt + 2]
            sqv = stats[:, 8 * tt + 3:8 * tt + 4]
            scalar.wait("VE", ("p1.nmu", tt))
            scalar.activation(out=lnscr, in_=xr[tt % 2], func=AF.Square, bias=nmu,
                              scale=1.0, accum_out=varsum[:, tt:tt + 1]
                              ).then_inc("SE")
            env.mark(("p1.sq1", tt))
            scalar.wait("SE", ("p1.sq1", tt))
            scalar.activation(out=sqv, in_=varsum[:, tt:tt + 1], func=AF.Sqrt,
                              scale=1.0 / 4096.0, bias=epst).then_inc("SE")
            env.mark(("p1.sq2", tt))

    @env.tensor
    def _(tensor):
        for tt in range(8):
            tensor.wait("VE", ("p1.norm", tt))
            for b in range(4):
                if 4 * tt + b >= 4:
                    tpp, bp = divmod(4 * tt + b - 4, 4)
                    tensor.wait("VE", ("p1.evac", tpp, bp))
                pbf = psb[b][:, :].bitcast(BF16)
                for u in range(8):
                    kk = 8 * b + u
                    ins = tensor.transpose(pbf[:, 128 * u:128 * (u + 1)],
                                           lnscr[:, 128 * kk:128 * (kk + 1)],
                                           identt)
                ins.then_inc("TE")
                env.mark(("p1.tp", tt, b))

    @env.gpsimd
    def _(gpsimd):
        for tt in range(8):
            gpsimd.wait("VE", ("p1.evac", tt, 3))
            gpsimd.dma_start(
                out=dram["agin%d" % (tt // 2)]
                    .rearrange("(kk p) t -> p kk t", p=128)
                    [:, :, 128 * (tt % 2):128 * (tt % 2 + 1)],
                in_=agst[tt % 2].rearrange("p (kk t) -> p kk t", t=128),
            ).then_inc("TS" + "01"[tt % 2], 16)
            env.mark(("p1.agstore", tt))
            if tt % 2 == 1:
                gpsimd.wait("TS0", ("p1.agstore", tt - 1))
                gpsimd.wait("TS1", ("p1.agstore", tt))
                gpsimd.collective_compute(
                    "AllGather", ALU.bypass,
                    replica_groups=[list(range(NCORES))],
                    ins=[dram["agin%d" % (tt // 2)].opt()],
                    outs=[dram["xg%d" % (tt // 2)].opt()],
                ).then_inc("CC")
                env.mark(("p1.ag", tt // 2))
    ctx.pop()

    ctx.push()
    qwgu_sb = ctx.alloc(32 * 1376, U8)
    qzgu_sb = ctx.alloc(1376, U8)
    sgu_bf = ctx.alloc(2752, BF16)
    zscr = ctx.alloc(2752, U8)
    zscrb = ctx.alloc(2752, BF16)
    sel_sb = ctx.alloc(32 * 128, BF16)
    Wdeq = ctx.alloc(32 * 688, BF16)
    nib = ctx.alloc(688, U8)
    nibbf = ctx.alloc(688, BF16)
    centbf = ctx.alloc(688, BF16)
    xgT = [ctx.alloc(32 * TBW, BF16), ctx.alloc(32 * TBW, BF16)]
    hst = [ctx.alloc(3 * TBW, BF16), ctx.alloc(3 * TBW, BF16)]
    sil = [ctx.alloc(TBW, F32), ctx.alloc(TBW, F32)]

    def qwb(kk, lo, n):
        return qwgu_sb.rearrange("p (kk b) -> p kk b", b=1376)[:, kk, lo:lo + n]

    def xg_tile(u, kk):
        return xgT[u % 2].rearrange("p (kk t) -> p kk t", t=TBW)[:, kk, :]

    @env.sync
    def _(sync):
        sync.wait("TS0", ("p1.agstore", 6))
        sync.wait("TS1", ("p1.agstore", 7))
        sync.dma_start(out=qwgu_sb.rearrange("p (kk b) -> p kk b", b=1376),
                       in_=dram["qwgu_b"].rearrange("(kk p) b -> p kk b", p=128)
                       ).then_inc("CN", 16)
        sync.dma_start(out=qzgu_sb[0:32], in_=dram["qzgu_b"]).then_inc("CN", 16)
        sync.dma_start(out=sel_sb[0:32], in_=dram["sel"]).then_inc("CN", 16)
        env.mark("p1b.res")
        for s in range(NSUB):
            for tb in range(NTB):
                u = NTB * s + tb
                if u >= 2:
                    sync.wait("TE", ("p1b.mt", u - 2, 5))
                r, loc = tb % 8, tb // 8
                if s == 0 and tb % 8 == 0:
                    sync.wait("CC", ("p1.ag", 2 * loc + 1))
                for half in range(2):
                    for jj in range(2):
                        sync.dma_start(
                            out=xgT[u % 2].rearrange("p (kk t) -> p kk t", t=TBW)
                                [:, 16 * half:16 * (half + 1),
                                 256 * jj:256 * (jj + 1)],
                            in_=dram["xg%d" % (2 * loc + jj)]
                                [4096 * r + 2048 * half:
                                 4096 * r + 2048 * (half + 1), :]
                                .rearrange("(kk p) t -> p kk t", p=128),
                        ).then_inc("Q" + "AB"[u % 2], 16)
                env.mark(("p1b.xg", u))

    @env.gpsimd
    def _(gpsimd):
        gpsimd.dma_start(out=sgu_bf[0:32], in_=dram["sgu_c"]).then_inc("CN2", 16)
        env.mark("p1b.sgu")
        for s in range(NSUB):
            for tb in range(NTB):
                u = NTB * s + tb
                tpos = 2 * (tb % 8) + tb // 8
                gpsimd.wait("VE", ("p1b.h", u, 2))
                gpsimd.dma_start(
                    out=dram["h_d"][HSUB * s:HSUB * s + 256,
                                    TBW * tpos:TBW * (tpos + 1)]
                        .rearrange("(kk p) t -> p kk t", p=128),
                    in_=hst[u % 2][:, 0:2 * TBW]
                        .rearrange("p (kk t) -> p kk t", t=TBW),
                ).then_inc("SO" + "01"[u % 2], 16)
                gpsimd.dma_start(
                    out=dram["h_d"][HSUB * s + 256:HSUB * (s + 1),
                                    TBW * tpos:TBW * (tpos + 1)],
                    in_=hst[u % 2][0:88, 2 * TBW:3 * TBW],
                ).then_inc("SO" + "01"[u % 2], 16)
                env.mark(("p1b.hstore", u))

    @env.tensor
    def _(tensor):
        tensor.wait("CN", "p1b.res")
        tensor.wait("CN2", "p1b.sgu")
        tensor.wait("VE", "p1b.zs")
        for s in range(NSUB):
            for kk in range(32):
                if kk >= 1:
                    tensor.wait("VE", ("p1b.deq", s, kk - 1))
                if s >= 1 and kk == 0:
                    tensor.wait("VE", ("p1b.h", NTB * s - 1, 1))
                    tensor.wait("VE", ("p1b.h", NTB * s - 1, 2))
                tensor.matmul(psb[4][:, 0:344],
                              sel_sb[0:32, 128 * kk:128 * (kk + 1)],
                              zscrb[0:32, HSUB * s:HSUB * (s + 1)],
                              start=True, stop=True)
                tensor.matmul(psb[5][:, 0:344],
                              sel_sb[0:32, 128 * kk:128 * (kk + 1)],
                              zscrb[0:32, 1376 + HSUB * s:1376 + HSUB * (s + 1)],
                              start=True, stop=True)
                tensor.matmul(psb[6][:, 0:344],
                              sel_sb[0:32, 128 * kk:128 * (kk + 1)],
                              sgu_bf[0:32, HSUB * s:HSUB * (s + 1)],
                              start=True, stop=True)
                ins = tensor.matmul(psb[7][:, 0:344],
                                    sel_sb[0:32, 128 * kk:128 * (kk + 1)],
                                    sgu_bf[0:32,
                                           1376 + HSUB * s:1376 + HSUB * (s + 1)],
                                    start=True, stop=True)
                ins.then_inc("TE")
                env.mark(("p1b.bcast", s, kk))
            for tb in range(NTB):
                u = NTB * s + tb
                tensor.wait("Q" + "AB"[u % 2], ("p1b.xg", u))
                tensor.wait("VE", ("p1b.deq", s, 31))
                for mt in range(6):
                    gate = mt < 3
                    j = mt % 3
                    off, sz = MT[j]
                    bank = mt
                    if u >= 1:
                        if gate:
                            tensor.wait("SE", ("p1b.sil", u - 1, j))
                        else:
                            tensor.wait("VE", ("p1b.h", u - 1, j))
                    wcol = off if gate else 344 + off
                    for kk in range(32):
                        ins = tensor.matmul(
                            psb[bank][0:sz, 0:TBW],
                            Wdeq.rearrange("p (kk n) -> p kk n", n=688)
                                [:, kk, wcol:wcol + sz],
                            xg_tile(u, kk), start=(kk == 0), stop=(kk == 31))
                    ins.then_inc("TE")
                    env.mark(("p1b.mt", u, mt))
                env.mark(("p1b.zcmm", u))

    @env.vector
    def _(vector):
        vector.wait("CN", "p1b.res")
        vector.wait("CN2", "p1b.sgu")
        vector.tensor_scalar(out=zscr[0:32, 0:2752:2], in0=qzgu_sb[0:32],
                             scalar1=15, scalar2=None, op0=ALU.bitwise_and)
        vector.tensor_scalar(out=zscr[0:32, 1:2752:2], in0=qzgu_sb[0:32],
                             scalar1=4, scalar2=None,
                             op0=ALU.logical_shift_right).then_inc("VE")
        env.mark("p1b.zunpack")
        vector.wait("VE", "p1b.zunpack")
        vector.tensor_copy(out=zscrb[0:32], in_=zscr[0:32]).then_inc("VE")
        env.mark("p1b.zs")
        for s in range(NSUB):
            for kk in range(32):
                if s > 0 and kk == 0:
                    vector.wait("TE", ("p1b.zcmm", NTB * s - 1))
                vector.tensor_scalar(out=nib[:, 0:344:2],
                                     in0=qwb(kk, 172 * s, 172), scalar1=15,
                                     scalar2=None, op0=ALU.bitwise_and)
                vector.tensor_scalar(out=nib[:, 1:344:2],
                                     in0=qwb(kk, 172 * s, 172), scalar1=4,
                                     scalar2=None, op0=ALU.logical_shift_right)
                vector.tensor_scalar(out=nib[:, 344:688:2],
                                     in0=qwb(kk, 688 + 172 * s, 172), scalar1=15,
                                     scalar2=None, op0=ALU.bitwise_and)
                vector.tensor_scalar(out=nib[:, 345:688:2],
                                     in0=qwb(kk, 688 + 172 * s, 172), scalar1=4,
                                     scalar2=None, op0=ALU.logical_shift_right
                                     ).then_inc("VE")
                env.mark(("p1b.unp", s, kk))
                vector.wait("VE", ("p1b.unp", s, kk))
                vector.tensor_copy(out=nibbf, in_=nib).then_inc("VE")
                env.mark(("p1b.cast", s, kk))
                vector.wait("VE", ("p1b.cast", s, kk))
                vector.wait("TE", ("p1b.bcast", s, kk))
                vector.tensor_sub(centbf[:, 0:344], nibbf[:, 0:344],
                                  psb[4][:, 0:344])
                vector.tensor_sub(centbf[:, 344:688], nibbf[:, 344:688],
                                  psb[5][:, 0:344]).then_inc("VE")
                env.mark(("p1b.cent", s, kk))
                vector.wait("VE", ("p1b.cent", s, kk))
                vector.tensor_mul(
                    Wdeq.rearrange("p (kk n) -> p kk n", n=688)[:, kk, 0:344],
                    centbf[:, 0:344], psb[6][:, 0:344])
                vector.tensor_mul(
                    Wdeq.rearrange("p (kk n) -> p kk n", n=688)[:, kk, 344:688],
                    centbf[:, 344:688], psb[7][:, 0:344]).then_inc("VE")
                env.mark(("p1b.deq", s, kk))
            for tb in range(NTB):
                u = NTB * s + tb
                for j in range(3):
                    vector.wait("SE", ("p1b.sil", u, j))
                    vector.wait("TE", ("p1b.mt", u, 3 + j))
                    if u >= 2 and j == 0:
                        vector.wait("SO" + "01"[u % 2], ("p1b.hstore", u - 2))
                    vector.tensor_mul(
                        hst[u % 2][:, TBW * j:TBW * (j + 1)],
                        sil[j % 2][:, 0:TBW],
                        psb[3 + j][:, 0:TBW]).then_inc("VE")
                    env.mark(("p1b.h", u, j))

    @env.scalar
    def _(scalar):
        for s in range(NSUB):
            for tb in range(NTB):
                u = NTB * s + tb
                for j in range(3):
                    scalar.wait("TE", ("p1b.mt", u, j))
                    pu, pj = (u, 0) if j == 2 else (u - 1, 2 if j == 0 else 1)
                    if pu >= 0:
                        scalar.wait("VE", ("p1b.h", pu, pj))
                    scalar.activation(out=sil[j % 2][:, 0:TBW],
                                      in_=psb[j][:, 0:TBW], func=AF.Silu
                                      ).then_inc("SE")
                    env.mark(("p1b.sil", u, j))
    ctx.pop()


def emit_ph1_down(env, dram):
    ctx = env.ctx
    psb = ctx.psb

    ctx.push()
    qwdn_sb = ctx.alloc(11 * 2048, U8)
    qzdn_sb = ctx.alloc(2048, U8)
    sdn_bf = ctx.alloc(4096, BF16)
    zdnscr = ctx.alloc(4096, U8)
    zdnscrb = ctx.alloc(4096, BF16)
    eid_sb = ctx.alloc(11 * 128, BF16)
    Wdnq = ctx.alloc(11 * 2048, BF16)
    dnib = ctx.alloc(2048, U8)
    dnibbf = ctx.alloc(2048, BF16)
    dcent = ctx.alloc(2048, BF16)
    h_ts = [ctx.alloc(11 * 128, BF16), ctx.alloc(11 * 128, BF16)]
    dstage = [ctx.alloc(2048, BF16), ctx.alloc(2048, BF16)]

    def wdn_bytes(kk, hh):
        return qwdn_sb.rearrange("p (kk b) -> p kk b", b=2048)[:, kk,
                                                              1024 * hh:1024 * (hh + 1)]

    @env.sync
    def _(sync):
        sync.wait("SO0", ("p1b.hstore", 62))
        sync.wait("SO1", ("p1b.hstore", 63))
        sync.wait("TE", ("p1b.zcmm", 63))
        def st_w(sync, w):
            hh_, ts_ = divmod(w, 64)
            sync.wait("SE", ("p1c.ev", w, 3))
            sync.dma_start(
                out=dram["rsin%d" % hh_][128 * ts_:128 * (ts_ + 1), :],
                in_=dstage[w % 2]).then_inc("TS" + "01"[w % 2], 16)
            env.mark(("p1c.st", w))
        sync.dma_start(out=qwdn_sb.rearrange("p (kk b) -> p kk b", b=2048)[:, 0:10, :],
                       in_=dram["qwdn_b"][0:1280]
                           .rearrange("(kk p) b -> p kk b", p=128)).then_inc("CN", 16)
        sync.dma_start(out=qwdn_sb[0:96].rearrange("p (kk b) -> p kk b", b=2048)
                       [:, 10:11, :],
                       in_=dram["qwdn_b"][1280:1376].unsqueeze(1)
                       ).then_inc("CN", 16)
        sync.dma_start(out=qzdn_sb[0:12], in_=dram["qzdn_b"]).then_inc("CN", 16)
        sync.dma_start(out=eid_sb[0:12].rearrange("p (kk t) -> p kk t", t=128),
                       in_=dram["Eid"].rearrange("kk g t -> g kk t")
                       ).then_inc("CN", 16)
        env.mark("p1c.res")
        for hh in range(2):
            for ts in range(64):
                w = 64 * hh + ts
                if w >= 2:
                    sync.wait("TE", ("p1c.dn", w - 2, 3))
                sync.dma_start(
                    out=h_ts[w % 2].rearrange("p (kk t) -> p kk t", t=128)[:, 0:10, :],
                    in_=dram["h_d"][0:1280, 128 * ts:128 * (ts + 1)]
                        .rearrange("(kk p) t -> p kk t", p=128)
                ).then_inc("R" + "01"[w % 2], 16)
                sync.dma_start(
                    out=h_ts[w % 2][0:96, 10 * 128:11 * 128],
                    in_=dram["h_d"][1280:1376, 128 * ts:128 * (ts + 1)]
                ).then_inc("R" + "01"[w % 2], 16)
                env.mark(("p1c.h", w))
                if w >= 2:
                    st_w(sync, w - 2)
        st_w(sync, 126)
        st_w(sync, 127)

    @env.gpsimd
    def _(gpsimd):
        gpsimd.dma_start(out=sdn_bf[0:12], in_=dram["sdn_c"]).then_inc("CN2", 16)
        env.mark("p1c.sdn")
        for hh in range(2):
            gpsimd.wait("TS0", ("p1c.st", 64 * hh + 62))
            gpsimd.wait("TS1", ("p1c.st", 64 * hh + 63))
            gpsimd.collective_compute(
                "ReduceScatter", ALU.add, replica_groups=[list(range(NCORES))],
                ins=[dram["rsin%d" % hh].opt()],
                outs=[dram["rsout%d" % hh].opt()],
            ).then_inc("CC")
            env.mark(("p1c.rs", hh))

    @env.vector
    def _(vector):
        vector.wait("CN", "p1c.res")
        vector.wait("CN2", "p1c.sdn")
        vector.tensor_scalar(out=zdnscr[0:12, 0:4096:2], in0=qzdn_sb[0:12],
                             scalar1=15, scalar2=None, op0=ALU.bitwise_and)
        vector.tensor_scalar(out=zdnscr[0:12, 1:4096:2], in0=qzdn_sb[0:12],
                             scalar1=4, scalar2=None,
                             op0=ALU.logical_shift_right).then_inc("VE")
        env.mark("p1c.zunp")
        vector.wait("VE", "p1c.zunp")
        vector.tensor_copy(out=zdnscrb[0:12], in_=zdnscr[0:12]).then_inc("VE")
        env.mark("p1c.zs")
        for hh in range(2):
            for kk in range(11):
                np_ = 128 if kk < 10 else 96
                if hh > 0 and kk == 0:
                    vector.wait("TE", ("p1c.dn", 63, 3))
                vector.tensor_scalar(out=dnib[0:np_, 0:2048:2],
                                     in0=wdn_bytes(kk, hh)[0:np_], scalar1=15,
                                     scalar2=None, op0=ALU.bitwise_and)
                vector.tensor_scalar(out=dnib[0:np_, 1:2048:2],
                                     in0=wdn_bytes(kk, hh)[0:np_], scalar1=4,
                                     scalar2=None, op0=ALU.logical_shift_right
                                     ).then_inc("VE")
                env.mark(("p1c.unp", hh, kk))
                vector.wait("VE", ("p1c.unp", hh, kk))
                vector.tensor_copy(out=dnibbf[0:np_], in_=dnib[0:np_]).then_inc("VE")
                env.mark(("p1c.cast", hh, kk))
                vector.wait("VE", ("p1c.cast", hh, kk))
                for m in range(4):
                    q = 4 * kk + m
                    vector.wait("TE", ("p1c.bc", hh, kk, m))
                    vector.tensor_sub(
                        dcent[0:np_, 512 * m:512 * (m + 1)],
                        dnibbf[0:np_, 512 * m:512 * (m + 1)],
                        psb[4 + q % 2][0:np_, :]).then_inc("VE")
                    env.mark(("p1c.cent", hh, kk, m))
                    vector.wait("VE", ("p1c.cent", hh, kk, m))
                    vector.tensor_mul(
                        Wdnq.rearrange("p (kk n) -> p kk n", n=2048)
                            [0:np_, kk, 512 * m:512 * (m + 1)],
                        dcent[0:np_, 512 * m:512 * (m + 1)],
                        psb[6 + q % 2][0:np_, :]
                    ).then_inc("VE")
                    env.mark(("p1c.deq", hh, kk, m))


    @env.tensor
    def _(tensor):
        tensor.wait("CN", "p1c.res")
        tensor.wait("VE", "p1c.zs")
        for hh in range(2):
            for kk in range(11):
                np_ = 128 if kk < 10 else 96
                for m in range(4):
                    q = 4 * kk + m
                    if q >= 2:
                        tensor.wait("VE", ("p1c.deq", hh, (q - 2) // 4, (q - 2) % 4))
                    if hh > 0 and kk == 0 and m < 2:
                        tensor.wait("VE", ("p1c.deq", 0, 10, 2 + m))
                    tensor.matmul(
                        psb[4 + q % 2][0:np_, :],
                        eid_sb[0:12, 128 * kk:128 * kk + np_],
                        zdnscrb[0:12, 2048 * hh + 512 * m:2048 * hh + 512 * (m + 1)],
                        start=True, stop=True)
                    ins = tensor.matmul(
                        psb[6 + q % 2][0:np_, :],
                        eid_sb[0:12, 128 * kk:128 * kk + np_],
                        sdn_bf[0:12, 2048 * hh + 512 * m:2048 * hh + 512 * (m + 1)],
                        start=True, stop=True)
                    ins.then_inc("TE")
                    env.mark(("p1c.bc", hh, kk, m))
            tensor.wait("VE", ("p1c.deq", hh, 10, 3))
            for ts in range(64):
                w = 64 * hh + ts
                tensor.wait("R" + "01"[w % 2], ("p1c.h", w))
                for nt in range(4):
                    if w >= 1:
                        tensor.wait("SE", ("p1c.ev", w - 1, nt))
                    for kk in range(11):
                        np_ = 128 if kk < 10 else 96
                        ins = tensor.matmul(
                            psb[nt][:, :],
                            h_ts[w % 2].rearrange("p (kk t) -> p kk t", t=128)
                                [0:np_, kk, :],
                            Wdnq.rearrange("p (kk n) -> p kk n", n=2048)
                                [0:np_, kk, 512 * nt:512 * (nt + 1)],
                            start=(kk == 0), stop=(kk == 10))
                    ins.then_inc("TE")
                    env.mark(("p1c.dn", w, nt))

    @env.scalar
    def _(scalar):
        for hh in range(2):
            for ts in range(64):
                w = 64 * hh + ts
                for nt in range(4):
                    scalar.wait("TE", ("p1c.dn", w, nt))
                    if w >= 2 and nt == 0:
                        scalar.wait("TS" + "01"[w % 2], ("p1c.st", w - 2))
                    scalar.copy(out=dstage[w % 2][:, 512 * nt:512 * (nt + 1)],
                                in_=psb[nt][:, :]).then_inc("SE")
                    env.mark(("p1c.ev", w, nt))
    ctx.pop()

    ctx.push()
    xr2 = [ctx.alloc(4096, F32), ctx.alloc(4096, F32)]
    rs2 = [ctx.alloc(4096, BF16), ctx.alloc(4096, BF16)]
    xout = [ctx.alloc(4096, F32), ctx.alloc(4096, F32)]
    rsf = [ctx.alloc(4096, F32), ctx.alloc(4096, F32)]

    @env.sync
    def _(sync):
        sync.wait("CC", ("p1c.rs", 1))
        for tt in range(8):
            if tt >= 2:
                sync.wait("VE", ("p1d.add", tt - 2))
            sync.dma_start(out=xr2[tt % 2],
                           in_=dram["xres1"][128 * tt:128 * (tt + 1)]
                           ).then_inc("R" + "01"[tt % 2], 16)
            for hh in range(2):
                sync.dma_start(out=rs2[tt % 2][:, 2048 * hh:2048 * (hh + 1)],
                               in_=dram["rsout%d" % hh]
                                   [128 * tt:128 * (tt + 1)]
                               ).then_inc("R" + "01"[tt % 2], 16)
            env.mark(("p1d.load", tt))

    @env.vector
    def _(vector):
        for tt in range(8):
            vector.wait("R" + "01"[tt % 2], ("p1d.load", tt))
            if tt >= 2:
                vector.wait("SO" + "01"[tt % 2], ("p1d.store", tt - 2))
            vector.tensor_copy(out=rsf[tt % 2], in_=rs2[tt % 2]).then_inc("VE")
            env.mark(("p1d.cast", tt))
            vector.wait("VE", ("p1d.cast", tt))
            vector.tensor_add(xout[tt % 2], xr2[tt % 2], rsf[tt % 2]).then_inc("VE")
            env.mark(("p1d.add", tt))

    @env.gpsimd
    def _(gpsimd):
        for tt in range(8):
            gpsimd.wait("VE", ("p1d.add", tt))
            gpsimd.dma_start(out=dram["xres2"][128 * tt:128 * (tt + 1)],
                             in_=xout[tt % 2]).then_inc("SO" + "01"[tt % 2], 16)
            env.mark(("p1d.store", tt))
    ctx.pop()


def declare_ph1(nc, dram, standalone=True):
    if "xres1" not in dram:
        dram["xres1"] = nc.dram_tensor(
            "xres1", [C, D], F32,
            kind="ExternalInput" if standalone else "Internal").ap()
    if "xres2" in dram:
        pass
    for nm, shp, dt in [
        ("qwgu_b", [D, 1376], U8), ("qzgu_b", [32, 1376], U8),
        ("sgu_c", [32, 2752], F32), ("qwdn_b", [1376, 2048], U8),
        ("qzdn_b", [12, 2048], U8), ("sdn_c", [12, 4096], F32),
        ("Ei", [11 * 128, 16], BF16), ("Eid", [11, 12, 128], BF16),
        ("Gc", [D, 32], BF16), ("sel", [32, 32 * 128], BF16),
    ]:
        dram[nm] = nc.dram_tensor(nm, shp, dt, kind="ExternalInput").ap()
    if "ident" not in dram:
        dram["ident"] = nc.dram_tensor("ident", [128, 128], BF16,
                                       kind="ExternalInput").ap()
    if "epsin" not in dram:
        dram["epsin"] = nc.dram_tensor("epsin", [128, 1], F32,
                                       kind="ExternalInput").ap()
    for j in range(4):
        dram["agin%d" % j] = nc.dram_tensor("agin%d" % j, [D, C // 4], BF16,
                                            kind="Internal").ap()
        dram["xg%d" % j] = nc.dram_tensor("xg%d" % j, [NCORES * D, C // 4],
                                          BF16, kind="Internal").ap()
    dram["xs_d"] = nc.dram_tensor("xs_d", [32, T], BF16, kind="Internal").ap()
    dram["h_d"] = nc.dram_tensor("h_d", [1376, T], BF16, kind="Internal").ap()
    for hh in range(2):
        dram["rsin%d" % hh] = nc.dram_tensor("rsin%d" % hh, [T, D // 2], BF16,
                                             kind="Internal").ap()
        dram["rsout%d" % hh] = nc.dram_tensor("rsout%d" % hh, [C, D // 2], BF16,
                                              kind="Internal").ap()
    if "xres2" not in dram:
        dram["xres2"] = nc.dram_tensor(
            "xres2", [C, D], F32,
            kind="ExternalOutput" if standalone else "Internal").ap()


PH1_SEMS = ("CN", "CN2", "XA", "XB", "W0", "W1", "TS0", "TS1", "QA", "QB",
            "R0", "R1", "SO0", "SO1", "GD", "TE", "VE", "SE", "XSA", "XSB",
            "CC", "LD", "ST")


def build_ph1():
    nc = bass.Bass("TRN2", target_bir_lowering=False, debug=False)
    dram = {}
    declare_ph1(nc, dram, standalone=True)
    es = contextlib.ExitStack()
    with es:
        ctx = Ctx(nc, es)
        ctx.sems(*PH1_SEMS)
        epst = ctx.alloc(1, F32)
        dram["epsc"] = epst
        with nc.Block() as block:
            @block.sync
            def _(sync):
                sync.dma_start(out=epst, in_=dram["epsin"]).then_inc(ctx.sem["CN"], 16)
            ctx.cnt["CN"] += 16

            def emit(env):
                emit_ph1(env, dram)
                emit_ph1_down(env, dram)
            run_twopass(ctx, block, emit)

            @block.gpsimd
            def _(gpsimd):
                gpsimd.wait_ge(ctx.sem["SO0"], ctx.cnt["SO0"])
                gpsimd.wait_ge(ctx.sem["SO1"], ctx.cnt["SO1"])
    return nc



def emit_ph2(env, dram, fence=None):
    ctx = env.ctx
    psb = ctx.psb
    epst = dram["epsc"]

    ctx.push()
    identt = ctx.alloc(128, BF16)
    identf = ctx.alloc(128, F32)
    stats = ctx.alloc(64, F32)
    varsum = ctx.alloc(8, F32)
    ctx.push()
    xlnT = ctx.alloc(32 * 1024, BF16)
    ctx.push()
    xr = [ctx.alloc(4096, F32), ctx.alloc(4096, F32)]
    lnscr = ctx.alloc(4096, BF16)

    @env.sync
    def _(sync):
        if fence:
            for s_, v_ in fence.items():
                sync.wait(s_, val=v_)
        sync.dma_start(out=identt, in_=dram["ident"]).then_inc("CN", 16)
        sync.dma_start(out=identf, in_=dram["identf"]).then_inc("CN", 16)
        env.mark("p2.ident")
        for tt in range(8):
            if tt >= 2:
                sync.wait("VE", ("p2.norm", tt - 2))
                sync.wait("SE", ("p2.sq1", tt - 2))
            sync.dma_start(out=xr[tt % 2],
                           in_=dram["xres2"][128 * tt:128 * (tt + 1)]
                           ).then_inc("X" + "AB"[tt % 2], 16)
            env.mark(("p2.xload", tt))

    @env.vector
    def _(vector):
        vector.wait("CN", "p2.ident")
        vector.memzero(varsum)
        for tt in range(8):
            mus = stats[:, 8 * tt + 0:8 * tt + 1]
            nmu = stats[:, 8 * tt + 1:8 * tt + 2]
            rstd = stats[:, 8 * tt + 2:8 * tt + 3]
            sqv = stats[:, 8 * tt + 3:8 * tt + 4]
            nmur = stats[:, 8 * tt + 4:8 * tt + 5]
            vector.wait("X" + "AB"[tt % 2], ("p2.xload", tt))
            vector.reduce_sum(out=mus, in_=xr[tt % 2], axis=AX).then_inc("VE")
            env.mark(("p2.mus", tt))
            vector.wait("VE", ("p2.mus", tt))
            vector.tensor_scalar_mul(nmu, mus, -1.0 / 4096.0).then_inc("VE")
            env.mark(("p2.nmu", tt))
            vector.wait("SE", ("p2.sq2", tt))
            vector.reciprocal(rstd, sqv).then_inc("VE")
            env.mark(("p2.rstd", tt))
            vector.wait("VE", ("p2.rstd", tt))
            vector.tensor_mul(nmur, nmu, rstd).then_inc("VE")
            env.mark(("p2.nmur", tt))
            vector.wait("VE", ("p2.nmur", tt))
            vector.tensor_scalar(out=lnscr, in0=xr[tt % 2], scalar1=rstd,
                                 scalar2=nmur, op0=ALU.mult, op1=ALU.add
                                 ).then_inc("VE")
            env.mark(("p2.norm", tt))
            for b in range(4):
                vector.wait("TE", ("p2.tp", tt, b))
                vector.tensor_copy(
                    out=xlnT.rearrange("p (kk t) -> p kk t", t=1024)
                        [:, 8 * b:8 * b + 8, 128 * tt:128 * (tt + 1)],
                    in_=psb[b][:, :].bitcast(BF16)
                        .rearrange("p (kk t) -> p kk t", t=128),
                ).then_inc("VE")
                env.mark(("p2.evac", tt, b))

    @env.scalar
    def _(scalar):
        for tt in range(8):
            nmu = stats[:, 8 * tt + 1:8 * tt + 2]
            sqv = stats[:, 8 * tt + 3:8 * tt + 4]
            scalar.wait("VE", ("p2.nmu", tt))
            scalar.activation(out=lnscr, in_=xr[tt % 2], func=AF.Square, bias=nmu,
                              scale=1.0, accum_out=varsum[:, tt:tt + 1]
                              ).then_inc("SE")
            env.mark(("p2.sq1", tt))
            scalar.wait("SE", ("p2.sq1", tt))
            scalar.activation(out=sqv, in_=varsum[:, tt:tt + 1], func=AF.Sqrt,
                              scale=1.0 / 4096.0, bias=epst).then_inc("SE")
            env.mark(("p2.sq2", tt))

    @env.tensor
    def _(tensor):
        for tt in range(8):
            tensor.wait("VE", ("p2.norm", tt))
            for b in range(4):
                if 4 * tt + b >= 4:
                    tpp, bp = divmod(4 * tt + b - 4, 4)
                    tensor.wait("VE", ("p2.evac", tpp, bp))
                pbf = psb[b][:, :].bitcast(BF16)
                for u in range(8):
                    kk = 8 * b + u
                    ins = tensor.transpose(pbf[:, 128 * u:128 * (u + 1)],
                                           lnscr[:, 128 * kk:128 * (kk + 1)],
                                           identt)
                ins.then_inc("TE")
                env.mark(("p2.tp", tt, b))
    @env.gpsimd
    def _(gpsimd):
        if "dbg1" in dram:
            gpsimd.wait("VE", ("p2.evac", 7, 3))
            gpsimd.dma_start(out=dram["dbg1"],
                             in_=xlnT.rearrange("p (kk t) -> p kk t", t=1024)
                             ).then_inc("GD", 16)
            env.mark("p2.dbg1")

    ctx.pop()

    ctx.push()
    wbuf = [ctx.alloc(32 * 512, BF16), ctx.alloc(32 * 512, BF16)]
    ptile = ctx.alloc(512, BF16)
    rptile = ctx.alloc(512, BF16)
    rs1 = ctx.alloc(512, BF16)
    rs2 = ctx.alloc(512, BF16)
    cosP = ctx.alloc(8 * 64, BF16)
    sinP = ctx.alloc(8 * 64, BF16)
    qstg = [ctx.alloc(512, BF16), ctx.alloc(512, BF16)]
    kstg = [ctx.alloc(512, BF16), ctx.alloc(512, BF16)]

    SG = ([("q", g) for g in range(8)] + [("k", g) for g in range(2)] +
          [("v", g) for g in range(2)])

    def h4(tile):
        return tile.rearrange("p (h d) -> p h d", d=128)

    @env.sync
    def _(sync):
        sync.wait("VE", ("p2.evac", 7, 3))
        sync.wait("SE", ("p2.sq2", 7))
        sync.wait("TE", ("p2.tp", 7, 3))
        sync.dma_start(out=cosP, in_=dram["cosP_c"]).then_inc("CN", 16)
        sync.dma_start(out=sinP, in_=dram["sinP_c"]).then_inc("CN", 16)
        env.mark("p2b.cs")

    @env.gpsimd
    def _(gpsimd):
        def load_slab(i):
            kind, g = SG[i]
            if i >= 2:
                gpsimd.wait("TE", ("p2b.mm", i - 2, 7))
            w = {"q": dram["l1wq"], "k": dram["l1wk"], "v": dram["l1wv"]}[kind]
            gpsimd.dma_start(
                out=wbuf[i % 2].rearrange("p (kk n) -> p kk n", n=512),
                in_=w[:, 512 * g:512 * (g + 1)]
                    .rearrange("(kk p) n -> p kk n", p=128)).then_inc(
                        "W" + "01"[i % 2], 16)
            env.mark(("p2b.wl", i))
        gpsimd.wait("VE", ("p2.evac", 7, 3))
        gpsimd.wait("SE", ("p2.sq2", 7))
        gpsimd.wait("TE", ("p2.tp", 7, 3))
        load_slab(0)
        load_slab(1)
        for i, (kind, g) in enumerate(SG):
            for tm in range(8):
                u = 8 * i + tm
                if kind == "q":
                    gpsimd.wait("VE", ("p2b.qevac", i, tm))
                    gpsimd.dma_start(
                        out=dram["qT_d"]
                            .rearrange("(h d) t -> d h t", d=128)
                            [:, 4 * g:4 * (g + 1), 128 * tm:128 * (tm + 1)],
                        in_=qstg[u % 2].rearrange("p (h t) -> p h t", t=128),
                    ).then_inc("SO" + "01"[u % 2], 16)
                    env.mark(("p2b.qst", i, tm))
                else:
                    gpsimd.wait("VE", ("p2b.kevac", i, tm))
                    col = 512 * g if kind == "k" else 1024 + 512 * g
                    gpsimd.dma_start(
                        out=dram["agin2_%d" % tm][0:128, col:col + 512],
                        in_=kstg[u % 2],
                    ).then_inc("SO" + "01"[u % 2], 16)
                    env.mark(("p2b.kst", i, tm))
            if i + 2 < len(SG):
                load_slab(i + 2)
        if "agdump_0" in dram:
            gpsimd.wait("SO0", ("p2b.kst", 11, 6))
            gpsimd.wait("SO1", ("p2b.kst", 11, 7))
            for b in range(8):
                gpsimd.dma_start(out=dram["agdump_%d" % b],
                                 in_=dram["agin2_%d" % b]).then_inc("GD", 16)
                env.mark(("p2b.agd", b))
            gpsimd.wait("GD", ("p2b.agd", 7))
        for b in range(8):
            gpsimd.wait("SO0", ("p2b.kst", 11, b))
            gpsimd.wait("SO1", ("p2b.kst", 11, b))
            gpsimd.collective_compute(
                "AllGather", ALU.bypass, replica_groups=[list(range(NCORES))],
                ins=[dram["agin2_%d" % b].opt()],
                outs=[dram["kvb_%d" % b].opt()],
            ).then_inc("CC")
            env.mark(("p2b.ag", b))

    @env.tensor
    def _(tensor):
        tensor.wait("VE", ("p2.evac", 7, 3))
        tensor.wait("CN", "p2b.cs")
        for i, (kind, g) in enumerate(SG):
            tensor.wait("W" + "01"[i % 2], ("p2b.wl", i))
            for tm in range(8):
                u = 8 * i + tm
                bank = u % 2
                if u >= 2:
                    iu, tu = divmod(u - 2, 8)
                    tensor.wait("VE", ("p2b.pevac", iu, tu))
                for kk in range(32):
                    ins = tensor.matmul(
                        psb[bank][:, :],
                        xlnT[:, 1024 * kk + 128 * tm:1024 * kk + 128 * (tm + 1)],
                        wbuf[i % 2].rearrange("p (kk n) -> p kk n", n=512)[:, kk, :],
                        start=(kk == 0), stop=(kk == 31))
                ins.then_inc("TE")
                env.mark(("p2b.mm", i, tm))
                if kind == "q":
                    tensor.wait("VE", ("p2b.rope", i, tm))
                    if u >= 2:
                        iu, tu = divmod(u - 2, 8)
                        tensor.wait("VE", ("p2b.qevac", iu, tu))
                    pbf = psb[4 + u % 2][:, :].bitcast(BF16)
                    for j in range(4):
                        ins = tensor.transpose(
                            pbf[:, 128 * j:128 * (j + 1)],
                            h4(rptile)[:, j, :], identt)
                    ins.then_inc("TE")
                    env.mark(("p2b.qtp", i, tm))

    @env.vector
    def _(vector):
        vector.wait("CN", "p2b.cs")
        for i, (kind, g) in enumerate(SG):
            for tm in range(8):
                u = 8 * i + tm
                bank = u % 2
                vector.wait("TE", ("p2b.mm", i, tm))
                if u >= 1:
                    vector.wait("VE", ("p2b.ropedone", *divmod(u - 1, 8)))
                vector.tensor_copy(out=ptile, in_=psb[bank][:, :]).then_inc("VE")
                env.mark(("p2b.pevac", i, tm))
                ct = cosP.rearrange("p (tm d) -> p tm d", d=64)[:, tm, :]
                st = sinP.rearrange("p (tm d) -> p tm d", d=64)[:, tm, :]
                ctb = ct.unsqueeze(1).broadcast_to([128, 4, 64])
                stb = st.unsqueeze(1).broadcast_to([128, 4, 64])
                x1 = h4(ptile)[:, :, 0:64]
                x2 = h4(ptile)[:, :, 64:128]
                if kind != "v":
                    vector.wait("VE", ("p2b.pevac", i, tm))
                    if kind == "q" and u >= 1:
                        vector.wait("TE", ("p2b.qtp", *divmod(u - 1, 8)))
                    vector.tensor_mul(h4(rs1)[:, :, 0:64], x1, ctb)
                    vector.tensor_mul(h4(rs1)[:, :, 64:128], x2, stb)
                    vector.tensor_mul(h4(rs2)[:, :, 0:64], x1, stb)
                    vector.tensor_mul(h4(rs2)[:, :, 64:128], x2, ctb).then_inc("VE")
                    env.mark(("p2b.r1", i, tm))
                    vector.wait("VE", ("p2b.r1", i, tm))
                    vector.tensor_sub(h4(rptile)[:, :, 0:64],
                                      h4(rs1)[:, :, 0:64], h4(rs1)[:, :, 64:128])
                    vector.tensor_add(h4(rptile)[:, :, 64:128],
                                      h4(rs2)[:, :, 0:64], h4(rs2)[:, :, 64:128]
                                      ).then_inc("VE")
                    env.mark(("p2b.rope", i, tm))
                if kind == "q":
                    vector.wait("TE", ("p2b.qtp", i, tm))
                    if u >= 2:
                        vector.wait("SO" + "01"[u % 2], ("p2b.qst", *divmod(u - 2, 8)))
                    vector.tensor_copy(out=qstg[u % 2],
                                       in_=psb[4 + u % 2][:, 0:256].bitcast(BF16)
                                       ).then_inc("VE")
                    env.mark(("p2b.qevac", i, tm))
                else:
                    srct = rptile if kind == "k" else ptile
                    vector.wait("VE", ("p2b.rope", i, tm) if kind == "k"
                                else ("p2b.pevac", i, tm))
                    iu, tu = divmod(u - 2, 8)
                    if u >= 2 and SG[iu][0] != "q":
                        vector.wait("SO" + "01"[u % 2], ("p2b.kst", iu, tu))
                    vector.tensor_copy(out=kstg[u % 2], in_=srct).then_inc("VE")
                    env.mark(("p2b.kevac", i, tm))
                env.mark(("p2b.ropedone", i, tm))

    ctx.pop()
    ctx.pop()

    ctx.push()
    mask01 = ctx.alloc(8192, BF16)
    KT = [ctx.alloc(8192, BF16), ctx.alloc(8192, BF16)]
    VH = [ctx.alloc(64 * 128, BF16), ctx.alloc(64 * 128, BF16)]
    P = [ctx.alloc(8192, BF16), ctx.alloc(8192, BF16)]
    PTb = ctx.alloc(4 * 1024, BF16)
    qtile = [ctx.alloc(128, BF16), ctx.alloc(128, BF16)]
    ctxf = [ctx.alloc(128, F32), ctx.alloc(128, F32)]
    ctxb = [ctx.alloc(128, BF16), ctx.alloc(128, BF16)]
    sums = ctx.alloc(4, F32)
    recf = ctx.alloc(4, F32)
    recr = ctx.alloc(256, F32)
    onesf = ctx.alloc(128, F32)

    def Pn(w):
        return P[w % 2]

    ST = []
    sc_list = []
    pt_list = []
    sl_map = {}
    for w in range(256):
        v, qh = divmod(w, 4)
        qt, hp = divmod(v, 8)
        nck, nbt = 2 * (qt + 1), qt + 1
        ST.append(dict(v=v, qh=qh, qt=qt, hp=hp, nck=nck, nbt=nbt))
        for ck in range(nck):
            sc_list.append((v, qh, ck, len(sc_list) % 4))
        for bt in range(nbt):
            g = len(pt_list)
            owner = "SE" if bt % 4 == 3 else "VE"
            pt_list.append((v, qh, bt, 4 + g % 2, g % 4, owner))
            sl_map[(v, qh, bt)] = (4 + g % 2, g % 4, owner)
    sc_of, pt_of = {}, {}
    si = pi = 0
    for w in range(256):
        s = ST[w]
        sc_of[w] = si
        pt_of[w] = pi
        si += s["nck"]
        pi += s["nbt"]

    @env.sync
    def _(sync):
        sync.wait("SO0", ("p2b.qst", 7, 6))
        sync.wait("SO1", ("p2b.qst", 7, 7))
        sync.wait("TE", ("p2b.mm", 11, 7))
        env.mark("p2c.const")
        for qt in range(8):
            L = 1024 * (qt + 1)
            if qt >= 1:
                sync.wait("VE", ("p2c.msk", 8 * qt - 1, 3))
            sync.dma_start(out=mask01[:, 0:L],
                           in_=dram["mask_c"][qt][:, 0:L]).then_inc("CN", 16)
            env.mark(("p2c.mask", qt))
            sync.wait("CC", ("p2b.ag", qt))
            for hp in range(8):
                v = 8 * qt + hp
                if v >= 2:
                    sync.wait("TE", ("p2c.pv", v - 2, 3))
                for bp in range(qt + 1):
                    sync.dma_start(
                        out=KT[v % 2][:, 1024 * bp:1024 * (bp + 1)]
                            .rearrange("p (a b) -> p a b", b=1024),
                        in_=dram["kvb_%d" % bp][:, 128 * hp:128 * (hp + 1)],
                        transpose=True).then_inc("Q" + "AB"[v % 2], 16)
                    sync.dma_start(
                        out=VH[v % 2].rearrange("p (kt d) -> p kt d", d=128)
                            [:, 8 * bp:8 * (bp + 1), :],
                        in_=dram["kvb_%d" % bp]
                            [:, 1024 + 128 * hp:1024 + 128 * (hp + 1)]
                            .rearrange("(kt p) d -> p kt d", p=128),
                    ).then_inc("Q" + "AB"[v % 2], 16)
                env.mark(("p2c.kv", v))
                for qh in range(4):
                    w = 4 * v + qh
                    if w >= 2:
                        s2 = ST[w - 2]
                        sync.wait("TE", ("p2c.sc", s2["v"], s2["qh"],
                                         s2["nck"] - 1))
                    sync.dma_start(
                        out=qtile[w % 2],
                        in_=dram["qT_d"][128 * (4 * hp + qh):128 * (4 * hp + qh) + 128,
                                         128 * qt:128 * (qt + 1)],
                    ).then_inc("R" + "01"[w % 2], 16)
                    env.mark(("p2c.q", v, qh))

    @env.tensor
    def _(tensor):
        tensor.wait("CN", "p2c.const")
        for wi in range(257):
            if wi < 256:
                w = wi
                s = ST[w]
                v, qh = s["v"], s["qh"]
                if qh == 0:
                    tensor.wait("Q" + "AB"[v % 2], ("p2c.kv", v))
                tensor.wait("R" + "01"[w % 2], ("p2c.q", v, qh))
                for ck in range(s["nck"]):
                    g = sc_of[w] + ck
                    if g >= 4:
                        vp, qp, cp, _ = sc_list[g - 4]
                        tensor.wait("SE", ("p2c.ev", vp, qp, cp))
                    ins = tensor.matmul(
                        psb[sc_list[g][3]][:, :], qtile[w % 2],
                        KT[v % 2][:, 512 * ck:512 * (ck + 1)],
                        start=True, stop=True)
                    ins.then_inc("TE")
                    env.mark(("p2c.sc", v, qh, ck))
            if wi >= 1:
                w = wi - 1
                s = ST[w]
                v, qh, nbt, nkt = s["v"], s["qh"], s["nbt"], 8 * s["nbt"]
                tensor.wait("VE", ("p2c.msk", v, qh))
                if w >= 1:
                    tensor.wait("SE", ("p2c.cev", *divmod(w - 1, 4)))
                for bt in range(nbt):
                    g = pt_of[w] + bt
                    bank, slot, owner = sl_map[(v, qh, bt)]
                    if g >= 2:
                        vp, qp, bp2, _, _, op = pt_list[g - 2]
                        tensor.wait(op, ("p2c.ptev", vp, qp, bp2))
                    pbf = psb[bank][:, :].bitcast(BF16)
                    for u2 in range(8):
                        kt = 8 * bt + u2
                        ins = tensor.transpose(
                            pbf[:, 128 * u2:128 * (u2 + 1)],
                            Pn(w)[:, 128 * kt:128 * (kt + 1)], identt)
                    ins.then_inc("TE")
                    env.mark(("p2c.pt", v, qh, bt))
                    if bt >= 1:
                        _, pslot, powner = sl_map[(v, qh, bt - 1)]
                        tensor.wait(powner, ("p2c.ptev", v, qh, bt - 1))
                        for u2 in range(8):
                            kt = 8 * (bt - 1) + u2
                            ins = tensor.matmul(
                                psb[6][:, 0:128],
                                VH[v % 2].rearrange("p (kt d) -> p kt d",
                                                    d=128)[:, kt, :],
                                PTb[:, 1024 * pslot:1024 * pslot + 1024]
                                    .rearrange("p (a b) -> p a b", b=128)[:, u2, :],
                                start=(kt == 0), stop=False)
                _, fslot, fowner = sl_map[(v, qh, nbt - 1)]
                tensor.wait(fowner, ("p2c.ptev", v, qh, nbt - 1))
                for u2 in range(8):
                    kt = 8 * (nbt - 1) + u2
                    ins = tensor.matmul(
                        psb[6][:, 0:128],
                        VH[v % 2].rearrange("p (kt d) -> p kt d", d=128)[:, kt, :],
                        PTb[:, 1024 * fslot:1024 * fslot + 1024]
                            .rearrange("p (a b) -> p a b", b=128)[:, u2, :],
                        start=(nbt == 1 and u2 == 0), stop=(u2 == 7))
                ins.then_inc("TE")
                env.mark(("p2c.pv", v, qh))
                tensor.wait("VE", ("p2c.rec", v, qh))
                if w >= 1:
                    tensor.wait("VE", ("p2c.ctt", *divmod(w - 1, 4)))
                ins = tensor.transpose(psb[7][0:1, 0:128],
                                       recf[:, 2 * (w % 2):2 * (w % 2) + 1],
                                       identf)
                ins.then_inc("TE")
                env.mark(("p2c.rtp", v, qh))
                tensor.wait("VE", ("p2c.rrow", v, qh))
                ins = tensor.matmul(psb[7][:, 128:256], onesf[0:1, :],
                                    recr[0:1, 128 * (w % 2):128 * (w % 2) + 128],
                                    start=True, stop=True)
                ins.then_inc("TE")
                env.mark(("p2c.rbc", v, qh))

    @env.vector
    def _(vector):
        vector.wait("CN", "p2c.const")
        vector.wait("TE", ("p2b.mm", 11, 7))
        vector.memset(onesf[0:1, :], 1.0)
        for w in range(256):
            s = ST[w]
            v, qh, qt = s["v"], s["qh"], s["qt"]
            L = 1024 * (qt + 1)
            p = w % 2
            vector.wait("SE", ("p2c.ev", v, qh, s["nck"] - 1))
            vector.wait("CN", ("p2c.mask", qt))
            vector.memzero(sums[:, p:p + 1]).then_inc("VE")
            env.mark(("p2c.sz", v, qh))
            vector.wait("VE", ("p2c.sz", v, qh))
            vector.scalar_tensor_tensor(
                out=Pn(w)[:, 0:L], in0=Pn(w)[:, 0:L], scalar=1.0,
                in1=mask01[:, 0:L], op0=ALU.mult, op1=ALU.mult,
                accum_out=sums[:, p:p + 1]).then_inc("VE")
            env.mark(("p2c.msk", v, qh))
            vector.wait("VE", ("p2c.msk", v, qh))
            vector.reciprocal(recf[:, 2 * p:2 * p + 1],
                              sums[:, p:p + 1]).then_inc("VE")
            env.mark(("p2c.rec", v, qh))
            for bt in range(s["nbt"]):
                bank, slot, owner = sl_map[(v, qh, bt)]
                if owner != "VE":
                    continue
                vector.wait("TE", ("p2c.pt", v, qh, bt))
                vector.tensor_copy(
                    out=PTb[:, 1024 * slot:1024 * slot + 1024],
                    in_=psb[bank][:, :].bitcast(BF16)).then_inc("VE")
                env.mark(("p2c.ptev", v, qh, bt))
            vector.wait("TE", ("p2c.rtp", v, qh))
            vector.tensor_copy(out=recr[0:1, 128 * p:128 * p + 128],
                               in_=psb[7][0:1, 0:128]).then_inc("VE")
            env.mark(("p2c.rrow", v, qh))
            vector.wait("SE", ("p2c.cev", v, qh))
            vector.wait("TE", ("p2c.rbc", v, qh))
            if w >= 2:
                vector.wait("SO" + "01"[w % 2], ("p2c.cst", *divmod(w - 2, 4)))
            vector.tensor_mul(ctxb[p], ctxf[p], psb[7][:, 128:256]
                              ).then_inc("VE")
            env.mark(("p2c.ctt", v, qh))

    @env.scalar
    def _(scalar):
        for wi in range(257):
            if wi < 256:
                w = wi
                s = ST[w]
                v, qh = s["v"], s["qh"]
                for ck in range(s["nck"]):
                    g = sc_of[w] + ck
                    scalar.wait("TE", ("p2c.sc", v, qh, ck))
                    if ck == 0 and w >= 2:
                        s2 = ST[w - 2]
                        scalar.wait("TE", ("p2c.pt", s2["v"], s2["qh"],
                                           s2["nbt"] - 1))
                    scalar.activation(out=Pn(w)[:, 512 * ck:512 * (ck + 1)],
                                      in_=psb[sc_list[g][3]][:, :],
                                      func=AF.Exp, scale=ISQ,
                                      bias=0.0).then_inc("SE")
                    env.mark(("p2c.ev", v, qh, ck))
            if wi >= 1:
                w = wi - 1
                s = ST[w]
                v, qh = s["v"], s["qh"]
                p = w % 2
                for bt in range(s["nbt"]):
                    bank, slot, owner = sl_map[(v, qh, bt)]
                    if owner != "SE":
                        continue
                    scalar.wait("TE", ("p2c.pt", v, qh, bt))
                    scalar.copy(
                        out=PTb[:, 1024 * slot:1024 * slot + 1024],
                        in_=psb[bank][:, :].bitcast(BF16)).then_inc("SE")
                    env.mark(("p2c.ptev", v, qh, bt))
                scalar.wait("TE", ("p2c.pv", v, qh))
                scalar.copy(out=ctxf[p], in_=psb[6][:, 0:128]).then_inc("SE")
                env.mark(("p2c.cev", v, qh))

    @env.gpsimd
    def _(gpsimd):
        for qt in range(8):
            for hp in range(8):
                v = 8 * qt + hp
                for qh in range(4):
                    w = 4 * v + qh
                    h = 4 * hp + qh
                    gpsimd.wait("VE", ("p2c.ctt", v, qh))
                    gpsimd.dma_start(
                        out=dram["ctxT_d"][128 * h:128 * (h + 1),
                                           128 * qt:128 * (qt + 1)],
                        in_=ctxb[w % 2]).then_inc("SO" + "01"[w % 2], 16)
                    env.mark(("p2c.cst", v, qh))
        if "kvdump_0" in dram:
            for b in range(8):
                gpsimd.dma_start(out=dram["kvdump_%d" % b],
                                 in_=dram["kvb_%d" % b]).then_inc("GD", 16)
                env.mark(("p2c.kvd", b))
            gpsimd.wait("GD", ("p2c.kvd", 7))
    ctx.pop()

    ctx.push()
    ctxT_sb = ctx.alloc(32 * 1024, BF16)
    wbuf2 = [ctx.alloc(32 * 128, BF16), ctx.alloc(32 * 128, BF16)]
    wstg = ctx.alloc(512, BF16)
    tstg = [ctx.alloc(512, BF16), ctx.alloc(512, BF16)]

    @env.sync
    def _(sync):
        sync.wait("SO0", ("p2c.cst", 62, 3))
        sync.wait("SO1", ("p2c.cst", 63, 3))
        sync.dma_start(out=ctxT_sb.rearrange("p (kk t) -> p kk t", t=1024),
                       in_=dram["ctxT_d"].rearrange("(kk p) t -> p kk t", p=128)
                       ).then_inc("CN", 16)
        env.mark("p2d.ctx")

    @env.gpsimd
    def _(gpsimd):
        def load_wo(om):
            if om >= 2:
                gpsimd.wait("TE", ("p2d.mm", om - 2, 1))
            gpsimd.dma_start(
                out=wbuf2[om % 2].rearrange("p (kk n) -> p kk n", n=128),
                in_=dram["l1wo"][:, 128 * om:128 * (om + 1)]
                    .rearrange("(kk p) n -> p kk n", p=128)).then_inc(
                        "W" + "01"[om % 2], 16)
            env.mark(("p2d.wl", om))
        load_wo(0)
        load_wo(1)
        for om in range(32):
            for tb in range(2):
                u = 2 * om + tb
                gpsimd.wait("VE", ("p2d.tev", om, tb))
                gpsimd.dma_start(
                    out=dram["att_d"][512 * tb:512 * (tb + 1),
                                      128 * om:128 * (om + 1)]
                        .rearrange("(j p) o -> p j o", p=128),
                    in_=tstg[u % 2].rearrange("p (j o) -> p j o", o=128),
                ).then_inc("SO" + "01"[u % 2], 16)
                env.mark(("p2d.ast", om, tb))
            if om + 2 < 32:
                load_wo(om + 2)

    @env.tensor
    def _(tensor):
        tensor.wait("CN", "p2d.ctx")
        for om in range(32):
            tensor.wait("W" + "01"[om % 2], ("p2d.wl", om))
            for tb in range(2):
                u = 2 * om + tb
                bank = u % 2
                if u >= 2:
                    tensor.wait("VE", ("p2d.wev", *divmod(u - 2, 2)))
                for kk in range(32):
                    ins = tensor.matmul(
                        psb[bank][:, :],
                        wbuf2[om % 2].rearrange("p (kk n) -> p kk n", n=128)[:, kk, :],
                        ctxT_sb[:, 1024 * kk + 512 * tb:1024 * kk + 512 * (tb + 1)],
                        start=(kk == 0), stop=(kk == 31))
                ins.then_inc("TE")
                env.mark(("p2d.mm", om, tb))
                tensor.wait("VE", ("p2d.wev", om, tb))
                if u >= 2:
                    tensor.wait("VE", ("p2d.tev", *divmod(u - 2, 2)))
                pbf = psb[2 + u % 2][:, :].bitcast(BF16)
                for j in range(4):
                    ins = tensor.transpose(pbf[:, 128 * j:128 * (j + 1)],
                                           wstg[:, 128 * j:128 * (j + 1)], identt)
                ins.then_inc("TE")
                env.mark(("p2d.tp", om, tb))

    @env.vector
    def _(vector):
        for om in range(32):
            for tb in range(2):
                u = 2 * om + tb
                vector.wait("TE", ("p2d.mm", om, tb))
                if u >= 1:
                    vector.wait("TE", ("p2d.tp", *divmod(u - 1, 2)))
                vector.tensor_copy(out=wstg, in_=psb[u % 2][:, :]).then_inc("VE")
                env.mark(("p2d.wev", om, tb))
                vector.wait("TE", ("p2d.tp", om, tb))
                if u >= 2:
                    vector.wait("SO" + "01"[u % 2], ("p2d.ast", *divmod(u - 2, 2)))
                vector.tensor_copy(out=tstg[u % 2],
                                   in_=psb[2 + u % 2][:, 0:256].bitcast(BF16)
                                   ).then_inc("VE")
                env.mark(("p2d.tev", om, tb))
    ctx.pop()

    ctx.push()
    xr3 = [ctx.alloc(4096, F32), ctx.alloc(4096, F32)]
    at3 = [ctx.alloc(4096, BF16), ctx.alloc(4096, BF16)]
    xo3 = [ctx.alloc(4096, F32), ctx.alloc(4096, F32)]

    @env.sync
    def _(sync):
        sync.wait("SO0", ("p2d.ast", 31, 0))
        sync.wait("SO1", ("p2d.ast", 31, 1))
        for tt in range(8):
            if tt >= 2:
                sync.wait("VE", ("p2e.add", tt - 2))
            sync.dma_start(out=xr3[tt % 2],
                           in_=dram["xres2"][128 * tt:128 * (tt + 1)]
                           ).then_inc("R" + "01"[tt % 2], 16)
            sync.dma_start(out=at3[tt % 2],
                           in_=dram["att_d"][128 * tt:128 * (tt + 1)]
                           ).then_inc("R" + "01"[tt % 2], 16)
            env.mark(("p2e.load", tt))

    @env.vector
    def _(vector):
        for tt in range(8):
            vector.wait("R" + "01"[tt % 2], ("p2e.load", tt))
            if tt >= 2:
                vector.wait("SO" + "01"[tt % 2], ("p2e.store", tt - 2))
            vector.tensor_add(xo3[tt % 2], xr3[tt % 2], at3[tt % 2]).then_inc("VE")
            env.mark(("p2e.add", tt))

    @env.gpsimd
    def _(gpsimd):
        for tt in range(8):
            gpsimd.wait("VE", ("p2e.add", tt))
            gpsimd.dma_start(out=dram["xres3"][128 * tt:128 * (tt + 1)],
                             in_=xo3[tt % 2]).then_inc("SO" + "01"[tt % 2], 16)
            env.mark(("p2e.store", tt))
    ctx.pop()
    ctx.pop()


def declare_ph2(nc, dram, standalone=True, debug=False):
    if "xres2" not in dram:
        dram["xres2"] = nc.dram_tensor(
            "xres2", [C, D], F32,
            kind="ExternalInput" if standalone else "Internal").ap()
    for nm, shp in [("l1wq", [D, D]), ("l1wk", [D, 1024]), ("l1wv", [D, 1024]),
                    ("l1wo", [D, D])]:
        dram[nm] = nc.dram_tensor(nm, shp, F32, kind="ExternalInput").ap()
    dram["cosP_c"] = nc.dram_tensor("cosP_c", [128, 512], BF16,
                                    kind="ExternalInput").ap()
    dram["sinP_c"] = nc.dram_tensor("sinP_c", [128, 512], BF16,
                                    kind="ExternalInput").ap()
    dram["mask_c"] = nc.dram_tensor("mask_c", [8, 128, 8192], BF16,
                                    kind="ExternalInput").ap()
    dram["identf"] = nc.dram_tensor("identf", [128, 128], F32,
                                    kind="ExternalInput").ap()
    if "ident" not in dram:
        dram["ident"] = nc.dram_tensor("ident", [128, 128], BF16,
                                       kind="ExternalInput").ap()
    if "epsin" not in dram:
        dram["epsin"] = nc.dram_tensor("epsin", [128, 1], F32,
                                       kind="ExternalInput").ap()
    dk = "ExternalOutput" if debug else "Internal"
    if debug:
        dram["dbg1"] = nc.dram_tensor("dbg1", [128, 32, C], BF16, kind=dk).ap()
    dram["qT_d"] = nc.dram_tensor("qT_d", [D, C], BF16, kind=dk).ap()
    for b in range(8):
        dram["agin2_%d" % b] = nc.dram_tensor(
            "agin2_%d" % b, [128, 2048], BF16, kind="Internal").ap()
        dram["kvb_%d" % b] = nc.dram_tensor(
            "kvb_%d" % b, [1024, 2048], BF16, kind="Internal").ap()
        if debug:
            dram["kvdump_%d" % b] = nc.dram_tensor(
                "kvdump_%d" % b, [1024, 2048], BF16, kind="ExternalOutput").ap()
            dram["agdump_%d" % b] = nc.dram_tensor(
                "agdump_%d" % b, [128, 2048], BF16, kind="ExternalOutput").ap()
    dram["ctxT_d"] = nc.dram_tensor("ctxT_d", [D, C], BF16, kind=dk).ap()
    dram["att_d"] = nc.dram_tensor("att_d", [C, D], BF16, kind=dk).ap()
    if "xres3" not in dram:
        dram["xres3"] = nc.dram_tensor(
            "xres3", [C, D], F32,
            kind="ExternalOutput" if standalone else "Internal").ap()


def host_prep_ph2(inp):
    sh = {}
    sh["l1wq"] = np.ascontiguousarray(np.asarray(inp["L1_attn_Wq"]))
    sh["l1wk"] = np.ascontiguousarray(np.asarray(inp["L1_attn_Wk"]))
    sh["l1wv"] = np.ascontiguousarray(np.asarray(inp["L1_attn_Wv"]))
    sh["l1wo"] = np.ascontiguousarray(np.asarray(inp["L1_attn_Wo"]))
    inv = 1.0 / 500000.0 ** (np.arange(0, 128, 2, dtype=np.float32) / 128)
    tpos = np.arange(T, dtype=np.float32)
    fr = np.outer(tpos, inv)
    cosf = np.cos(fr).astype(np.float32)
    sinf = np.sin(fr).astype(np.float32)
    sh["identf"] = np.eye(128, dtype=np.float32)
    ss = np.arange(T)
    keypos = 128 * (8 * (ss // 1024) + (ss % 1024) // 128) + ss % 128
    pc = []
    for c in range(NCORES):
        m = {}
        rows = zz_rows(c)
        m["cosP_c"] = np.ascontiguousarray(
            cosf[rows].reshape(8, 128, 64).transpose(1, 0, 2).reshape(128, 512)
        ).astype(BF)
        m["sinP_c"] = np.ascontiguousarray(
            sinf[rows].reshape(8, 128, 64).transpose(1, 0, 2).reshape(128, 512)
        ).astype(BF)
        msk = np.zeros((8, 128, T), np.float32)
        for qt in range(8):
            pos = 128 * (8 * qt + c) + np.arange(128)
            msk[qt] = (keypos[None, :] <= pos[:, None]).astype(np.float32)
        m["mask_c"] = msk.astype(BF)
        pc.append(m)
    return sh, pc


def build_ph2(debug=False):
    nc = bass.Bass("TRN2", target_bir_lowering=False, debug=False)
    dram = {}
    declare_ph2(nc, dram, standalone=True, debug=debug)
    es = contextlib.ExitStack()
    with es:
        ctx = Ctx(nc, es)
        ctx.sems(*PH1_SEMS)
        epst = ctx.alloc(1, F32)
        dram["epsc"] = epst
        with nc.Block() as block:
            @block.sync
            def _(sync):
                sync.dma_start(out=epst, in_=dram["epsin"]).then_inc(ctx.sem["CN"], 16)
            ctx.cnt["CN"] += 16

            run_twopass(ctx, block, lambda env: emit_ph2(env, dram))

            @block.gpsimd
            def _(gpsimd):
                gpsimd.wait_ge(ctx.sem["SO0"], ctx.cnt["SO0"])
                gpsimd.wait_ge(ctx.sem["SO1"], ctx.cnt["SO1"])
    return nc



def emit_ph3(env, dram, fence=None):
    ctx = env.ctx
    psb = ctx.psb
    epst = dram["epsc"]

    ctx.push()
    identt = ctx.alloc(128, BF16)
    stats = ctx.alloc(64, F32)
    varsum = ctx.alloc(8, F32)
    ctx.push()
    xlnT = ctx.alloc(32 * 1024, BF16)
    ctx.push()
    xr = [ctx.alloc(4096, F32), ctx.alloc(4096, F32)]
    lnscr = ctx.alloc(4096, BF16)

    @env.sync
    def _(sync):
        if fence:
            for s_, v_ in fence.items():
                sync.wait(s_, val=v_)
        sync.dma_start(out=identt, in_=dram["ident"]).then_inc("CN", 16)
        env.mark("p3.ident")
        for tt in range(8):
            if tt >= 2:
                sync.wait("VE", ("p3.norm", tt - 2))
                sync.wait("SE", ("p3.sq1", tt - 2))
            sync.dma_start(out=xr[tt % 2],
                           in_=dram["xres3"][128 * tt:128 * (tt + 1)]
                           ).then_inc("X" + "AB"[tt % 2], 16)
            env.mark(("p3.xload", tt))

    @env.vector
    def _(vector):
        vector.wait("CN", "p3.ident")
        vector.memzero(varsum)
        for tt in range(8):
            mus = stats[:, 8 * tt + 0:8 * tt + 1]
            nmu = stats[:, 8 * tt + 1:8 * tt + 2]
            rstd = stats[:, 8 * tt + 2:8 * tt + 3]
            sqv = stats[:, 8 * tt + 3:8 * tt + 4]
            nmur = stats[:, 8 * tt + 4:8 * tt + 5]
            vector.wait("X" + "AB"[tt % 2], ("p3.xload", tt))
            vector.reduce_sum(out=mus, in_=xr[tt % 2], axis=AX).then_inc("VE")
            env.mark(("p3.mus", tt))
            vector.wait("VE", ("p3.mus", tt))
            vector.tensor_scalar_mul(nmu, mus, -1.0 / 4096.0).then_inc("VE")
            env.mark(("p3.nmu", tt))
            vector.wait("SE", ("p3.sq2", tt))
            vector.reciprocal(rstd, sqv).then_inc("VE")
            env.mark(("p3.rstd", tt))
            vector.wait("VE", ("p3.rstd", tt))
            vector.tensor_mul(nmur, nmu, rstd).then_inc("VE")
            env.mark(("p3.nmur", tt))
            vector.wait("VE", ("p3.nmur", tt))
            vector.tensor_scalar(out=lnscr, in0=xr[tt % 2], scalar1=rstd,
                                 scalar2=nmur, op0=ALU.mult, op1=ALU.add
                                 ).then_inc("VE")
            env.mark(("p3.norm", tt))
            for b in range(4):
                vector.wait("TE", ("p3.tp", tt, b))
                vector.tensor_copy(
                    out=xlnT.rearrange("p (kk t) -> p kk t", t=1024)
                        [:, 8 * b:8 * b + 8, 128 * tt:128 * (tt + 1)],
                    in_=psb[b][:, :].bitcast(BF16)
                        .rearrange("p (kk t) -> p kk t", t=128),
                ).then_inc("VE")
                env.mark(("p3.evac", tt, b))

    @env.scalar
    def _(scalar):
        for tt in range(8):
            nmu = stats[:, 8 * tt + 1:8 * tt + 2]
            sqv = stats[:, 8 * tt + 3:8 * tt + 4]
            scalar.wait("VE", ("p3.nmu", tt))
            scalar.activation(out=lnscr, in_=xr[tt % 2], func=AF.Square, bias=nmu,
                              scale=1.0, accum_out=varsum[:, tt:tt + 1]
                              ).then_inc("SE")
            env.mark(("p3.sq1", tt))
            scalar.wait("SE", ("p3.sq1", tt))
            scalar.activation(out=sqv, in_=varsum[:, tt:tt + 1], func=AF.Sqrt,
                              scale=1.0 / 4096.0, bias=epst).then_inc("SE")
            env.mark(("p3.sq2", tt))

    @env.tensor
    def _(tensor):
        for tt in range(8):
            tensor.wait("VE", ("p3.norm", tt))
            for b in range(4):
                if 4 * tt + b >= 4:
                    tpp, bp = divmod(4 * tt + b - 4, 4)
                    tensor.wait("VE", ("p3.evac", tpp, bp))
                pbf = psb[b][:, :].bitcast(BF16)
                for u in range(8):
                    kk = 8 * b + u
                    ins = tensor.transpose(pbf[:, 128 * u:128 * (u + 1)],
                                           lnscr[:, 128 * kk:128 * (kk + 1)],
                                           identt)
                ins.then_inc("TE")
                env.mark(("p3.tp", tt, b))
    ctx.pop()

    ctx.push()
    wgt = ctx.alloc(32 * 8, BF16)
    at_sb = ctx.alloc(32 * 128, BF16)
    b_sb = ctx.alloc(4096, BF16)
    i8_sb = ctx.alloc(128, BF16)
    bbase = ctx.alloc(32, F32)
    lg = ctx.alloc(8, F32)
    m1n = ctx.alloc(8, F32)
    pe_ = ctx.alloc(8, F32)
    mxm = ctx.alloc(8, F32)
    lgm = ctx.alloc(8, F32)
    wexp = ctx.alloc(8, F32)
    route = ctx.alloc(8, BF16)
    routeT = ctx.alloc(1024, BF16)
    r_sb = ctx.alloc(1024, BF16)
    hw_sb = ctx.alloc(1024, BF16)
    wbuf3 = [ctx.alloc(32 * 128, BF16), ctx.alloc(32 * 128, BF16)]
    wstg = ctx.alloc(512, BF16)
    tstg = [ctx.alloc(512, BF16), ctx.alloc(512, BF16)]

    @env.sync
    def _(sync):
        sync.wait("VE", ("p3.evac", 7, 3))
        sync.wait("SE", ("p3.sq2", 7))
        sync.dma_start(out=i8_sb[0:8], in_=dram["i8c"]).then_inc("CN", 16)
        sync.dma_start(out=bbase, in_=dram["bbase_c"]).then_inc("CN", 16)
        env.mark("p3b.res")

    @env.gpsimd
    def _(gpsimd):
        gpsimd.wait("VE", ("p3.evac", 7, 3))
        gpsimd.wait("SE", ("p3.sq2", 7))
        gpsimd.dma_start(out=at_sb.rearrange("p (kk n) -> p kk n", n=128),
                         in_=dram["l1A"].rearrange("(kk p) n -> p kk n", p=128)
                         ).then_inc("CN2", 16)
        gpsimd.dma_start(out=b_sb[0:128], in_=dram["l1B"]).then_inc("CN2", 16)
        gpsimd.dma_start(out=wgt.rearrange("p (kk n) -> p kk n", n=8),
                         in_=dram["wgateT"].rearrange("(kk p) n -> p kk n", p=128)
                         ).then_inc("CN2", 16)
        env.mark("p3b.res2")
        def load_wb(om):
            if om >= 2:
                gpsimd.wait("TE", ("p3b.mm", om - 2, 1))
            gpsimd.dma_start(
                out=wbuf3[om % 2].rearrange("p (kk n) -> p kk n", n=128),
                in_=dram["wbaseT"][:, 128 * om:128 * (om + 1)]
                    .rearrange("(kk p) n -> p kk n", p=128)).then_inc(
                        "W" + "01"[om % 2], 16)
            env.mark(("p3b.wl", om))
        load_wb(0)
        load_wb(1)
        for om in range(32):
            for tb in range(2):
                u = 2 * om + tb
                gpsimd.wait("VE", ("p3b.tev", om, tb))
                gpsimd.dma_start(
                    out=dram["att2_d"][512 * tb:512 * (tb + 1),
                                       128 * om:128 * (om + 1)]
                        .rearrange("(j p) o -> p j o", p=128),
                    in_=tstg[u % 2].rearrange("p (j o) -> p j o", o=128),
                ).then_inc("SO" + "01"[u % 2], 16)
                env.mark(("p3b.ast", om, tb))
            if om + 2 < 32:
                load_wb(om + 2)

    @env.tensor
    def _(tensor):
        tensor.wait("CN", "p3b.res")
        tensor.wait("CN2", "p3b.res2")
        tensor.wait("VE", ("p3.evac", 7, 3))
        tensor.wait("CN2", "p3b.res2")
        for tm in range(8):
            if tm >= 1:
                tensor.wait("VE", ("p3b.lgev", tm - 1))
            for kk in range(32):
                ins = tensor.matmul(
                    psb[4][:, 0:8],
                    xlnT[:, 1024 * kk + 128 * tm:1024 * kk + 128 * (tm + 1)],
                    wgt.rearrange("p (kk n) -> p kk n", n=8)[:, kk, :],
                    start=(kk == 0), stop=(kk == 31))
            ins.then_inc("TE")
            env.mark(("p3b.lg", tm))
            tensor.wait("VE", ("p3b.route", tm))
            ins = tensor.transpose(psb[5][0:8, 0:64].bitcast(BF16)[0:8, 0:128],
                                   route, identt)
            ins.then_inc("TE")
            env.mark(("p3b.rtp", tm))
        for tb in range(2):
            tensor.wait("VE", ("p3b.rtev", 7))
            if tb >= 0:
                pass
            for kk in range(32):
                ins = tensor.matmul(
                    psb[6][:, :],
                    at_sb.rearrange("p (kk n) -> p kk n", n=128)[:, kk, :],
                    xlnT[:, 1024 * kk + 512 * tb:1024 * kk + 512 * (tb + 1)],
                    start=(kk == 0), stop=(kk == 31))
            ins.then_inc("TE")
            env.mark(("p3b.rmm", tb))
            tensor.wait("VE", ("p3b.rev", tb))
            ins = tensor.matmul(psb[7][:, :], i8_sb[0:8, :],
                                routeT[0:8, 512 * tb:512 * (tb + 1)],
                                start=True, stop=True)
            ins.then_inc("TE")
            env.mark(("p3b.rwb", tb))
        for om in range(32):
            tensor.wait("W" + "01"[om % 2], ("p3b.wl", om))
            for tb in range(2):
                u = 2 * om + tb
                bank = u % 2
                if u >= 2:
                    tensor.wait("VE", ("p3b.wev", *divmod(u - 2, 2)))
                for kk in range(32):
                    tensor.matmul(
                        psb[bank][:, :],
                        wbuf3[om % 2].rearrange("p (kk n) -> p kk n", n=128)[:, kk, :],
                        xlnT[:, 1024 * kk + 512 * tb:1024 * kk + 512 * (tb + 1)],
                        start=(kk == 0), stop=False)
                tensor.wait("VE", ("p3b.hw", tb))
                ins = tensor.matmul(
                    psb[bank][:, :], b_sb[0:128, 128 * om:128 * (om + 1)],
                    hw_sb[0:128, 512 * tb:512 * (tb + 1)],
                    start=False, stop=True)
                ins.then_inc("TE")
                env.mark(("p3b.mm", om, tb))
                tensor.wait("VE", ("p3b.wev", om, tb))
                if u >= 2:
                    tensor.wait("VE", ("p3b.tev", *divmod(u - 2, 2)))
                pbf = psb[2 + u % 2][:, :].bitcast(BF16)
                for j in range(4):
                    ins = tensor.transpose(pbf[:, 128 * j:128 * (j + 1)],
                                           wstg[:, 128 * j:128 * (j + 1)], identt)
                ins.then_inc("TE")
                env.mark(("p3b.tp", om, tb))

    @env.vector
    def _(vector):
        vector.wait("CN", "p3b.res")
        for tm in range(8):
            vector.wait("TE", ("p3b.lg", tm))
            vector.tensor_copy(out=lg, in_=psb[4][:, 0:8]).then_inc("VE")
            env.mark(("p3b.lgev", tm))
            vector.wait("VE", ("p3b.lgev", tm))
            vector.reduce_max(out=m1n[:, 0:1], in_=lg, axis=AX, negate=True
                              ).then_inc("VE")
            env.mark(("p3b.m1", tm))
            vector.wait("VE", ("p3b.m1", tm))
            vector.tensor_scalar_mul(m1n[:, 1:2], m1n[:, 0:1], -1.0).then_inc("VE")
            env.mark(("p3b.m1p", tm))
            vector.wait("VE", ("p3b.m1p", tm))
            vector.tensor_single_scalar(out=mxm, in_=lg, scalar=m1n[:, 1:2],
                                        op=ALU.is_ge).then_inc("VE")
            env.mark(("p3b.mx", tm))
            vector.wait("VE", ("p3b.mx", tm))
            vector.scalar_tensor_tensor(out=lgm, in0=mxm, scalar=-1e30, in1=lg,
                                        op0=ALU.mult, op1=ALU.add).then_inc("VE")
            env.mark(("p3b.lgm", tm))
            vector.wait("VE", ("p3b.lgm", tm))
            vector.reduce_max(out=m1n[:, 2:3], in_=lgm, axis=AX, negate=True
                              ).then_inc("VE")
            env.mark(("p3b.m2", tm))
            vector.wait("VE", ("p3b.m2", tm))
            vector.tensor_scalar_mul(m1n[:, 3:4], m1n[:, 2:3], -1.0).then_inc("VE")
            env.mark(("p3b.m2p", tm))
            vector.wait("VE", ("p3b.m2p", tm))
            vector.tensor_single_scalar(out=mxm, in_=lg, scalar=m1n[:, 3:4],
                                        op=ALU.is_ge).then_inc("VE")
            env.mark(("p3b.mask2", tm))
            vector.wait("SE", ("p3b.pe", tm))
            vector.wait("VE", ("p3b.mask2", tm))
            vector.tensor_mul(wexp, pe_, mxm).then_inc("VE")
            env.mark(("p3b.w", tm))
            vector.wait("VE", ("p3b.w", tm))
            vector.reduce_sum(out=m1n[:, 4:5], in_=wexp, axis=AX).then_inc("VE")
            env.mark(("p3b.den", tm))
            vector.wait("VE", ("p3b.den", tm))
            vector.reciprocal(m1n[:, 5:6], m1n[:, 4:5]).then_inc("VE")
            env.mark(("p3b.rden", tm))
            vector.wait("VE", ("p3b.rden", tm))
            vector.tensor_single_scalar(out=route, in_=wexp, scalar=m1n[:, 5:6],
                                        op=ALU.mult).then_inc("VE")
            env.mark(("p3b.route", tm))
            vector.wait("TE", ("p3b.rtp", tm))
            vector.tensor_copy(out=routeT[0:8, 128 * tm:128 * (tm + 1)],
                               in_=psb[5][0:8, 0:64].bitcast(BF16)[0:8, 0:128]
                               ).then_inc("VE")
            env.mark(("p3b.rtev", tm))
        for tb in range(2):
            vector.wait("TE", ("p3b.rmm", tb))
            vector.tensor_copy(out=r_sb[0:128, 512 * tb:512 * (tb + 1)],
                               in_=psb[6][:, :]).then_inc("VE")
            env.mark(("p3b.rev", tb))
            vector.wait("TE", ("p3b.rwb", tb))
            vector.wait("VE", ("p3b.rev", tb))
            vector.tensor_mul(hw_sb[0:128, 512 * tb:512 * (tb + 1)],
                              r_sb[0:128, 512 * tb:512 * (tb + 1)],
                              psb[7][:, :]).then_inc("VE")
            env.mark(("p3b.hw", tb))
        for om in range(32):
            for tb in range(2):
                u = 2 * om + tb
                vector.wait("TE", ("p3b.mm", om, tb))
                if u >= 1:
                    vector.wait("TE", ("p3b.tp", *divmod(u - 1, 2)))
                vector.tensor_scalar(out=wstg, in0=psb[u % 2][:, :],
                                     scalar1=bbase[:, om:om + 1], scalar2=None,
                                     op0=ALU.add).then_inc("VE")
                env.mark(("p3b.wev", om, tb))
                vector.wait("TE", ("p3b.tp", om, tb))
                if u >= 2:
                    vector.wait("SO" + "01"[u % 2], ("p3b.ast", *divmod(u - 2, 2)))
                vector.tensor_copy(out=tstg[u % 2],
                                   in_=psb[2 + u % 2][:, 0:256].bitcast(BF16)
                                   ).then_inc("VE")
                env.mark(("p3b.tev", om, tb))

    @env.scalar
    def _(scalar):
        for tm in range(8):
            scalar.wait("VE", ("p3b.m1", tm))
            scalar.activation(out=pe_, in_=lg, func=AF.Exp, scale=1.0,
                              bias=m1n[:, 0:1]).then_inc("SE")
            env.mark(("p3b.pe", tm))
    ctx.pop()
    ctx.pop()

    ctx.push()
    xr4 = [ctx.alloc(4096, F32), ctx.alloc(4096, F32)]
    at4 = [ctx.alloc(4096, BF16), ctx.alloc(4096, BF16)]
    xo4 = [ctx.alloc(4096, F32), ctx.alloc(4096, F32)]

    @env.sync
    def _(sync):
        sync.wait("SO0", ("p3b.ast", 31, 0))
        sync.wait("SO1", ("p3b.ast", 31, 1))
        for tt in range(8):
            if tt >= 2:
                sync.wait("VE", ("p3c.add", tt - 2))
            sync.dma_start(out=xr4[tt % 2],
                           in_=dram["xres3"][128 * tt:128 * (tt + 1)]
                           ).then_inc("R" + "01"[tt % 2], 16)
            sync.dma_start(out=at4[tt % 2],
                           in_=dram["att2_d"][128 * tt:128 * (tt + 1)]
                           ).then_inc("R" + "01"[tt % 2], 16)
            env.mark(("p3c.load", tt))

    @env.vector
    def _(vector):
        for tt in range(8):
            vector.wait("R" + "01"[tt % 2], ("p3c.load", tt))
            if tt >= 2:
                vector.wait("SO" + "01"[tt % 2], ("p3c.store", tt - 2))
            vector.tensor_add(xo4[tt % 2], xr4[tt % 2], at4[tt % 2]).then_inc("VE")
            env.mark(("p3c.add", tt))

    @env.gpsimd
    def _(gpsimd):
        for tt in range(8):
            gpsimd.wait("VE", ("p3c.add", tt))
            gpsimd.dma_start(out=dram["y_rows"][128 * tt:128 * (tt + 1)],
                             in_=xo4[tt % 2]).then_inc("SO" + "01"[tt % 2], 16)
            env.mark(("p3c.store", tt))
    ctx.pop()
    ctx.pop()


def declare_ph3(nc, dram, standalone=True):
    if "xres3" not in dram:
        dram["xres3"] = nc.dram_tensor(
            "xres3", [C, D], F32,
            kind="ExternalInput" if standalone else "Internal").ap()
    dram["wbaseT"] = nc.dram_tensor("wbaseT", [D, D], F32, kind="ExternalInput").ap()
    dram["wgateT"] = nc.dram_tensor("wgateT", [D, 8], F32, kind="ExternalInput").ap()
    dram["l1A"] = nc.dram_tensor("l1A", [D, 128], F32, kind="ExternalInput").ap()
    dram["l1B"] = nc.dram_tensor("l1B", [128, D], F32, kind="ExternalInput").ap()
    dram["i8c"] = nc.dram_tensor("i8c", [8, 128], BF16, kind="ExternalInput").ap()
    dram["bbase_c"] = nc.dram_tensor("bbase_c", [128, 32], F32,
                                     kind="ExternalInput").ap()
    if "ident" not in dram:
        dram["ident"] = nc.dram_tensor("ident", [128, 128], BF16,
                                       kind="ExternalInput").ap()
    if "epsin" not in dram:
        dram["epsin"] = nc.dram_tensor("epsin", [128, 1], F32,
                                       kind="ExternalInput").ap()
    dram["att2_d"] = nc.dram_tensor("att2_d", [C, D], BF16, kind="Internal").ap()
    dram["y_rows"] = nc.dram_tensor("y_rows", [C, D], F32,
                                    kind="ExternalOutput").ap()


def host_prep_ph3(inp):
    sh = {}
    sh["wbaseT"] = np.ascontiguousarray(np.asarray(inp["L1_ffn_W_base"]).T)
    sh["wgateT"] = np.ascontiguousarray(np.asarray(inp["L1_ffn_W_gate"]).T)
    A = np.asarray(inp["L1_ffn_A"])
    sh["l1A"] = np.ascontiguousarray(A.transpose(2, 0, 1).reshape(D, 128))
    B = np.asarray(inp["L1_ffn_B"])
    sh["l1B"] = np.ascontiguousarray(B.transpose(0, 2, 1).reshape(128, D))
    i8 = np.zeros((8, 128), np.float32)
    for e in range(8):
        i8[e, 16 * e:16 * (e + 1)] = 1.0
    sh["i8c"] = i8.astype(BF)
    sh["bbase_c"] = np.ascontiguousarray(
        np.asarray(inp["L1_ffn_b_base"]).reshape(32, 128).T)
    return sh


def build_ph3():
    nc = bass.Bass("TRN2", target_bir_lowering=False, debug=False)
    dram = {}
    declare_ph3(nc, dram, standalone=True)
    es = contextlib.ExitStack()
    with es:
        ctx = Ctx(nc, es)
        ctx.sems(*PH1_SEMS)
        epst = ctx.alloc(1, F32)
        dram["epsc"] = epst
        with nc.Block() as block:
            @block.sync
            def _(sync):
                sync.dma_start(out=epst, in_=dram["epsin"]).then_inc(ctx.sem["CN"], 16)
            ctx.cnt["CN"] += 16
            run_twopass(ctx, block, lambda env: emit_ph3(env, dram))

            @block.gpsimd
            def _(gpsimd):
                gpsimd.wait_ge(ctx.sem["SO0"], ctx.cnt["SO0"])
                gpsimd.wait_ge(ctx.sem["SO1"], ctx.cnt["SO1"])
    return nc


def build_full(dbg=False, nph=4):
    nc = bass.Bass("TRN2", target_bir_lowering=False, debug=False)
    dram = {}
    if dbg:
        dram["xres1"] = nc.dram_tensor("xres1", [C, D], F32, kind="ExternalOutput").ap()
        dram["xres2"] = nc.dram_tensor("xres2", [C, D], F32, kind="ExternalOutput").ap()
        if nph >= 3:
            dram["xres3"] = nc.dram_tensor("xres3", [C, D], F32, kind="ExternalOutput").ap()
    declare_ph0(nc, dram, standalone=False)
    declare_ph1(nc, dram, standalone=(nph == 2) and not dbg)
    if nph >= 3:
        declare_ph2(nc, dram, standalone=False)
    if nph >= 4:
        declare_ph3(nc, dram, standalone=False)
    es = contextlib.ExitStack()
    with es:
        ctx = Ctx(nc, es)
        ctx.sems(*PH1_SEMS)
        epst = ctx.alloc(1, F32)
        dram["epsc"] = epst
        with nc.Block() as block:
            @block.sync
            def _(sync):
                sync.dma_start(out=epst, in_=dram["epsin"]).then_inc(ctx.sem["CN"], 16)
            ctx.cnt["CN"] += 16

            def emit(env):
                emit_ph0(env, dram)
                f1 = {"SO0": ctx.cnt["SO0"], "SO1": ctx.cnt["SO1"]}
                emit_ph1(env, dram, fence=f1)
                emit_ph1_down(env, dram)
                if nph >= 3:
                    f2 = {"SO0": ctx.cnt["SO0"], "SO1": ctx.cnt["SO1"]}
                    emit_ph2(env, dram, fence=f2)
                if nph >= 4:
                    f3 = {"SO0": ctx.cnt["SO0"], "SO1": ctx.cnt["SO1"]}
                    emit_ph3(env, dram, fence=f3)
            run_twopass(ctx, block, emit)

            @block.gpsimd
            def _(gpsimd):
                gpsimd.wait_ge(ctx.sem["SO0"], ctx.cnt["SO0"])
                gpsimd.wait_ge(ctx.sem["SO1"], ctx.cnt["SO1"])
    return nc

def kernel(**inputs):
    ident = np.eye(128, dtype=np.float32).astype(BF)
    eps = np.full((128, 1), 1e-5, np.float32)
    sh0, pc0 = host_prep(inputs)
    sh1, pc1 = host_prep_ph1(inputs)
    sh2, pc2 = host_prep_ph2(inputs)
    sh3 = host_prep_ph3(inputs)
    nc = build_full()
    in_maps = []
    for c in range(NCORES):
        m = {}
        m.update(pc0[c])
        m.update(pc1[c])
        m.update(pc2[c])
        for k in ["wq_r", "wk_r", "wv_r", "bq", "bk", "bv"]:
            m[k] = sh0[k]
        m["Gc"] = sh1["Gc"]
        m["sel"] = sh1["sel"]
        for k in ["l1wq", "l1wk", "l1wv", "l1wo", "identf"]:
            m[k] = sh2[k]
        m.update(sh3)
        m["ident"] = ident
        m["epsin"] = eps
        in_maps.append(m)
    res = run_bass_kernel_spmd(nc, in_maps, core_ids=list(range(NCORES)))
    y = np.empty((T, D), np.float32)
    for c in range(NCORES):
        y[zz_rows(c)] = np.asarray(res.results[c]["y_rows"])
    return y.reshape(1, T, D).astype(np.float32)
